# version 12
# baseline (speedup 1.0000x reference)
import contextlib
import numpy as np
import ml_dtypes

import concourse.bass as bass
import concourse.mybir as mybir
from concourse.bass_utils import run_bass_kernel_spmd

F32 = mybir.dt.float32
BF16 = mybir.dt.bfloat16
AF = mybir.ActivationFunctionType
OP = mybir.AluOpType
AX = mybir.AxisListType

NCORES = 8
T = 2048
E = 1024
EC = 8
D = 64
I = 3072
IC = 24
NB = 4
TC = 16

MAGIC = 12582912.0
QA = 32767.0
QW = 7.0
SIG = 65536.0
SIGMAX = 65535.0
C_ATTN = float(np.sqrt(3.0 / 64.0))
LNEPS = 1e-5
SEPS = 1e-12


class Ctx:

    def __init__(self):
        self.pass2 = False
        self.cnt = {}
        self.cks = {}
        self.sems = {}

    def reset(self, pass2):
        self.pass2 = pass2
        self.cnt = {k: 0 for k in ("dS", "dA", "dG", "CC", "tE", "vE", "sE", "gE")}

    AUTO = ("vE", "sE", "gE")

    def inc(self, instr, sem, n):
        if sem in self.AUTO:
            return
        self.cnt[sem] += n
        if self.pass2:
            instr.then_inc(self.sems[sem], n)

    def ck(self, label, sem):
        v = (sem, self.cnt[sem])
        if self.pass2:
            assert self.cks[label] == v, (label, self.cks[label], v)
        else:
            assert label not in self.cks, f"dup ck {label}"
            self.cks[label] = v

    def wait(self, eng, label):
        if self.pass2:
            sem, v = self.cks[label]
            eng.wait_ge(self.sems[sem], v)

    def wait_val(self, eng, sem, v):
        if self.pass2:
            eng.wait_ge(self.sems[sem], v)


class Mock:
    def __getattr__(self, name):
        return lambda *a, **k: None


_CHAIN_OPS = {
    "tensor_copy", "tensor_tensor", "tensor_scalar", "tensor_single_scalar",
    "tensor_reduce", "reduce_max", "reduce_sum", "scalar_tensor_tensor",
    "reciprocal", "memset", "memzero", "activation", "copy", "add", "mul",
    "square", "sqrt", "sign", "tensor_add", "tensor_sub", "tensor_mul",
    "tensor_max", "tensor_scalar_add", "tensor_scalar_sub", "tensor_scalar_mul",
    "tensor_scalar_max", "tensor_scalar_min", "transpose", "select",
}


class Chain:

    def __init__(self, eng, C, sem):
        self._eng = eng
        self._C = C
        self._sem = sem

    def __getattr__(self, name):
        eng, C, sem = self._eng, self._C, self._sem
        if name in _CHAIN_OPS:
            def op(*a, **k):
                C.wait_val(eng, sem, C.cnt[sem])
                instr = getattr(eng, name)(*a, **k)
                C.cnt[sem] += 1
                if C.pass2:
                    instr.then_inc(C.sems[sem], 1)
                return instr
            return op
        if name == "dma_start":
            def dma(*a, **k):
                C.wait_val(eng, sem, C.cnt[sem])
                return eng.dma_start(*a, **k)
            return dma
        return getattr(eng, name)


def build_program(debug=False):
    nc = bass.Bass("TRN2", target_bir_lowering=False)
    C = Ctx()

    xT = nc.dram_tensor("xT", [E, T], F32, kind="ExternalInput").ap()
    wqt = nc.dram_tensor("wqt", [E, D], F32, kind="ExternalInput").ap()
    wkt = nc.dram_tensor("wkt", [E, D], F32, kind="ExternalInput").ap()
    wvt = nc.dram_tensor("wvt", [E, D], F32, kind="ExternalInput").ap()
    wot = nc.dram_tensor("wot", [D, E], F32, kind="ExternalInput").ap()
    wg_nat = nc.dram_tensor("wg_nat", [I, 128], F32, kind="ExternalInput").ap()
    wu_nat = nc.dram_tensor("wu_nat", [I, 128], F32, kind="ExternalInput").ap()
    wd_nat = nc.dram_tensor("wd_nat", [E, 384], F32, kind="ExternalInput").ap()
    onesf = nc.dram_tensor("onesf", [128, 128], F32, kind="ExternalInput").ap()
    onesb = nc.dram_tensor("onesb", [128, 1], BF16, kind="ExternalInput").ap()
    identb = nc.dram_tensor("identb", [128, 128], BF16, kind="ExternalInput").ap()
    identf = nc.dram_tensor("identf", [128, 128], F32, kind="ExternalInput").ap()
    outT = nc.dram_tensor("outT", [E, T], F32, kind="ExternalOutput").ap()
    dbg = {}
    if debug:
        for n, sh, dt in [("dbg_cols", [128, 32], F32), ("dbg_misc", [1, 512], F32),
                          ("dbg_esum", [64, 64], F32), ("dbg_o0", [128, T], BF16),
                          ("dbg_up0", [128, T], BF16), ("dbg_attn0", [128, T], BF16),
                          ("dbg_kb", [128, TC * D], BF16), ("dbg_vb", [128, TC * D], BF16),
                          ("dbg_qt", [64, T], BF16), ("dbg_ctx", [64, T], BF16),
                          ("dbg_xq0", [128, T], BF16), ("dbg_dqT", [128, 1024], BF16),
                          ("dbg_sx", [1, 64], F32), ("dbg_xn0", [128, T], BF16),
                          ("dbg_trrow", [8, 128], F32), ("dbg_colsx", [128, 32], F32),
                          ("dbg_arout", [1, 256], F32), ("dbg_arbrow", [1, 128], F32),
                          ("dbg_bc", [128, 128], BF16),
                          ("dbg_arin1", [1, 128], F32), ("dbg_arout1", [1, 128], F32)]:
            dbg[n] = nc.dram_tensor(n, sh, dt, kind="ExternalOutput").ap()

    ag_in = {
        "g": nc.dram_tensor("ag_wg_in", [128, I], BF16),
        "u": nc.dram_tensor("ag_wu_in", [128, I], BF16),
        "d": nc.dram_tensor("ag_wd_in", [384, E], BF16),
    }
    ag_out = {
        "g": nc.dram_tensor("ag_wg_out", [E, I], BF16),
        "u": nc.dram_tensor("ag_wu_out", [E, I], BF16),
        "d": nc.dram_tensor("ag_wd_out", [I, E], BF16),
    }
    ag_e_in = nc.dram_tensor("ag_e_in", [D, D], F32)
    ag_e_out = nc.dram_tensor("ag_e_out", [D, D], F32)
    ar_in = [nc.dram_tensor(f"ar{j}_in", [1, 256], F32) for j in range(4)]
    ar_out = [nc.dram_tensor(f"ar{j}_out", [1, 256], F32) for j in range(4)]
    up_dram = nc.dram_tensor("up_dram", [I, T], BF16)
    attn_dram = nc.dram_tensor("attn_dram", [E, T], BF16)
    flat_dram = nc.dram_tensor("flat_dram", [4, 128], F32)

    st = contextlib.ExitStack()
    sb = lambda name, shape, dt=F32: st.enter_context(nc.sbuf_tensor(name, shape, dt)).ap()

    AR = sb("AR", [128, 49152], BF16)
    ARf = AR.bitcast(F32)
    x_ch = [ARf[:, 2048 * k:2048 * (k + 1)] for k in range(EC)]
    gt = [AR[:, 2048 * i:2048 * (i + 1)] for i in range(IC)]
    dq_nat = ARf[:, 16384:19456]
    dq_T = AR[:, 41984:45056]

    xn = sb("xn", [128, T * EC], BF16)
    xn_ch = [xn[:, 2048 * k:2048 * (k + 1)] for k in range(EC)]
    xq_ch = xn_ch
    xpref = xn.bitcast(F32)
    xpref_sl = [xpref[:, 2048 * j:2048 * (j + 1)] for j in range(2)]
    xsc_sl = [xpref[:, 4096:6144], xpref[:, 6144:8192]]

    SC = sb("SC", [128, 6144], F32)
    xnf = None
    SCt = SC[:, 0:2048]
    SCu = SC[:, 2048:4096]
    SCv = SC[:, 4096:6144]
    Rb = SCt
    Qb = SCu
    xtmp = SCv
    outstage = [SC[:, 0:2048], SC[:, 2048:4096]]
    x2b_sl = [SC.bitcast(BF16)[:, 8192:10240], SC.bitcast(BF16)[:, 10240:12288]]

    upst = sb("upst", [128, 4096], BF16)
    upst_sl = [upst[:, 2048 * j:2048 * (j + 1)] for j in range(2)]
    ctxT = sb("ctxT", [64, T], BF16)
    QTb = sb("QTb", [64, T], BF16)
    wstr = sb("wstr", [128, 2048], BF16)
    wstr_sl = [wstr[:, 1024 * j:1024 * (j + 1)] for j in range(2)]
    wstrD = sb("wstrD", [128, 6144], BF16)
    wstrD_sl = [wstrD[:, 3072 * j:3072 * (j + 1)] for j in range(2)]
    astage = sb("astage", [128, 4096], BF16)
    astage_sl = [astage[:, 2048 * j:2048 * (j + 1)] for j in range(2)]
    astageF = astage.bitcast(F32)
    Kb = sb("Kb", [128, TC * D], BF16)
    Vb = sb("Vb", [128, TC * D], BF16)

    onesf_sb = sb("onesf_sb", [128, 128], F32)
    onesb_sb = sb("onesb_sb", [128, 1], BF16)
    identb_sb = sb("identb_sb", [128, 128], BF16)
    identf_sb = sb("identf_sb", [128, 128], F32)
    wq_f = ARf[:, 22528:23040]
    wk_f = ARf[:, 23040:23552]
    wv_f = ARf[:, 23552:24064]
    wo_f = ARf[0:64, 19456:20480]
    wq_b = sb("wq_b", [128, 512], BF16)
    wk_b = sb("wk_b", [128, 512], BF16)
    wv_b = sb("wv_b", [128, 512], BF16)
    wo_b = sb("wo_b", [64, 1024], BF16)

    dqs = sb("dqs", [128, 288], F32)
    dq_bmax = dqs[:, 0:96]
    dq_s = dqs[:, 96:192]
    dq_rs = dqs[:, 192:288]

    cols = sb("cols", [128, 48], F32)
    maxcol = cols[:, 0:1]
    mctmp = cols[:, 1:2]
    gmaxcol = cols[:, 2:3]
    umaxcol = cols[:, 3:4]
    mmaxcol = cols[:, 4:5]
    rrmaxcol = cols[:, 5:6]
    Mcol = cols[:, 6:7]
    gtmp = cols[:, 7:8]
    negMcol = cols[:, 47:48]
    sx_col = cols[:, 8:9]
    rsx_col = cols[:, 9:10]
    sg_col = cols[:, 10:11]
    rsg_col = cols[:, 11:12]
    msg_col = cols[:, 12:13]
    rsu_col = cols[:, 13:14]
    cg_col = cols[:, 14:15]
    cr_col = cols[:, 15:16]
    so_col = cols[:, 16:17]
    rawc = cols[:, 17:18]
    raw2c = cols[:, 18:19]
    sg65_col = cols[:, 19:20]
    sg2_col = cols[:, 20:21]
    sup_col = cols[:, 21:22]
    tcol = cols[:, 22:23]
    pcol = cols[:, 23:24]
    rec8 = cols[:, 24:32]
    rec8b = cols[:, 32:40]
    rec8c = cols[:, 40:48]

    rows = SCv[0:1, :]
    sum_r = rows[0:1, 0:512]
    ssq_r = rows[0:1, 512:1024]
    rowa = rows[0:1, 1024:1536]
    rowb = rows[0:1, 1536:2048]
    misc = sb("misc", [1, 512], F32)
    srow = misc[0:1, 0:16]
    arrow = misc[0:1, 16:32]
    arb = misc[0:1, 32:48]
    arbrow = misc[0:1, 64:320]
    eps_sb = misc[0:1, 320:321]
    sclr = misc[0:1, 336:352]
    trrow = sb("trrow", [8, 256], F32)
    trsrc = sb("trsrc", [128, 8], F32)
    trsrc2 = sb("trsrc2", [128, 8], F32)
    e_loc = sb("e_loc", [64, 64], F32)
    e_sum = sb("e_sum", [64, 64], F32)
    e_bf = sb("e_bf", [64, 64], BF16)

    ps = [st.enter_context(nc.psum_tensor(f"psb{j}", [128, 512], F32)).ap()
          for j in range(8)]

    for name in ("dS", "dA", "dG", "CC", "tE", "vE", "sE", "gE"):
        C.sems[name] = st.enter_context(nc.semaphore(name))

    RG = [list(range(NCORES))]

    def recip_col(eng, dst_col, src_col):
        eng.tensor_copy(dq_bmax, src_col.broadcast_to([128, 96]))
        eng.reciprocal(dq_s, dq_bmax)
        return eng.tensor_copy(dst_col, dq_s[:, 0:1])


    def emit_sync(eng):
        for k in range(EC):
            C.inc(eng.dma_start(out=x_ch[k], in_=xT[128 * k:128 * (k + 1), :]), "dS", 16)
            C.ck(f"x{k}", "dS")
        for dst, src in ((wq_f, wqt), (wk_f, wkt), (wv_f, wvt)):
            C.inc(eng.dma_start(
                out=dst.rearrange("p (k d) -> p k d", d=D),
                in_=src.rearrange("(k p) d -> p k d", p=128)), "dS", 16)
        C.inc(eng.dma_start(out=wo_f[:, :], in_=wot), "dS", 16)
        C.inc(eng.dma_start(out=onesf_sb[:, :], in_=onesf), "dS", 16)
        C.inc(eng.dma_start(out=onesb_sb[:, :], in_=onesb), "dS", 16)
        C.inc(eng.dma_start(out=identb_sb[:, :], in_=identb), "dS", 16)
        C.inc(eng.dma_start(out=identf_sb[:, :], in_=identf), "dS", 16)
        C.ck("consts", "dS")
        wg_r = ag_out["g"].ap().rearrange("(k p) c -> p k c", p=128)
        for i in range(IC):
            C.wait(eng, "AGwg")
            if i >= 2:
                C.wait(eng, f"Amm{i - 2}")
            C.inc(eng.dma_start(
                out=wstr_sl[i % 2].rearrange("p (k c) -> p k c", c=128),
                in_=wg_r[:, :, 128 * i:128 * (i + 1)]), "dS", 16)
            C.ck(f"wA{i}", "dS")
        wu_r = ag_out["u"].ap().rearrange("(k p) c -> p k c", p=128)
        for i in range(IC):
            C.wait(eng, "AGwu")
            C.wait(eng, f"Bmm{i - 2}" if i >= 2 else "Amm23")
            C.inc(eng.dma_start(
                out=wstr_sl[i % 2].rearrange("p (k c) -> p k c", c=128),
                in_=wu_r[:, :, 128 * i:128 * (i + 1)]), "dS", 16)
            C.ck(f"wB{i}", "dS")
        wd_r = ag_out["d"].ap().rearrange("(i p) c -> p i c", p=128)
        for e in range(EC):
            C.wait(eng, "AGwd")
            if e >= 2:
                C.wait(eng, f"Dmm{e - 2}")
            C.inc(eng.dma_start(
                out=wstrD_sl[e % 2].rearrange("p (i c) -> p i c", c=128),
                in_=wd_r[:, :, 128 * e:128 * (e + 1)]), "dS", 16)
            C.ck(f"wD{e}", "dS")
            if e >= 2:
                C.wait(eng, f"oev{e - 2}")
                C.inc(eng.dma_start(out=outT[128 * (e - 2):128 * (e - 1), :],
                                    in_=outstage[e % 2]), "dS", 16)
                C.ck(f"out{e - 2}", "dS")
        for e in (6, 7):
            C.wait(eng, f"oev{e}")
            C.inc(eng.dma_start(out=outT[128 * e:128 * (e + 1), :],
                                in_=outstage[e % 2]), "dS", 16)
            C.ck(f"out{e}", "dS")
        C.wait(eng, "out7")

    def emit_gpsimd(eng):
        for k in range(EC):
            C.wait(eng, f"x{k}")
            C.inc(eng.dma_start(out=xn_ch[k], in_=x_ch[k]), "dG", 16)
            C.ck(f"xb{k}", "dG")
        C.wait(eng, "consts")
        C.inc(eng.dma_start(out=wq_b[:, :], in_=wq_f[:, :]), "dG", 16)
        C.inc(eng.dma_start(out=wk_b[:, :], in_=wk_f[:, :]), "dG", 16)
        C.inc(eng.dma_start(out=wv_b[:, :], in_=wv_f[:, :]), "dG", 16)
        C.inc(eng.dma_start(out=wo_b[:, :], in_=wo_f[:, :]), "dG", 16)
        C.ck("wcast", "dG")
        prev = None
        for wname in ("g", "u", "d"):
            if prev is not None:
                C.wait(eng, f"dqstage{prev}")
            if wname == "d":
                nat_in = wd_nat.rearrange("(t p) c -> p t c", p=128)
                nat_out = dq_nat.rearrange("p (t c) -> p t c", c=384)
            else:
                src = wg_nat if wname == "g" else wu_nat
                nat_in = src.rearrange("(t p) c -> p t c", p=128)
                nat_out = dq_nat.rearrange("p (t c) -> p t c", c=128)
            C.inc(eng.dma_start(out=nat_out, in_=nat_in), "dG", 16)
            C.ck(f"dqnat{wname}", "dG")
            C.wait(eng, f"dqT{wname}")
            if wname == "d":
                dTr = dq_T.rearrange("p (t j c) -> p j t c", j=3, c=128)
                for j in range(3):
                    C.inc(eng.dma_start(
                        out=ag_in["d"].ap()[128 * j:128 * (j + 1), :]
                            .rearrange("p (t c) -> p t c", c=128),
                        in_=dTr[:, j]), "dG", 16)
            else:
                C.inc(eng.dma_start(out=ag_in[wname][:, :], in_=dq_T), "dG", 16)
            C.ck(f"dqstage{wname}", "dG")
            if wname == "g":
                C.wait_val(eng, "dG", C.cnt["dG"])
                C.inc(eng.collective_compute(
                    "AllGather", OP.bypass, replica_groups=RG,
                    ins=[ag_in["g"].ap().opt()], outs=[ag_out["g"].ap().opt()]), "CC", 1)
                C.ck("AGwg", "CC")
            prev = wname

        C.wait(eng, "sxtrr")
        C.inc(eng.dma_start(out=ar_in[0][0:1, 0:128], in_=trrow[0:1, 0:128]), "dG", 16)
        C.wait_val(eng, "dG", C.cnt["dG"])
        C.inc(eng.collective_compute(
            "AllReduce", OP.max, replica_groups=RG,
            ins=[ar_in[0].ap().opt()], outs=[ar_out[0].ap().opt()]), "CC", 1)
        C.wait_val(eng, "CC", C.cnt["CC"])
        C.inc(eng.dma_start(out=arbrow[0:1, 0:128], in_=ar_out[0][0:1, 0:128]), "dG", 16)
        C.ck("sxback", "dG")
        if debug:
            C.wait(eng, "sxcols")
            C.inc(eng.dma_start(out=dbg["dbg_arout"], in_=ar_out[0][:, :]), "dG", 16)
            C.inc(eng.dma_start(out=dbg["dbg_bc"], in_=astage_sl[1][:, 0:128]), "dG", 16)
            C.inc(eng.dma_start(out=dbg["dbg_arbrow"], in_=arbrow[0:1, 0:128]), "dG", 16)
            C.inc(eng.dma_start(out=dbg["dbg_sx"], in_=misc[0:1, 0:64]), "dG", 16)
            C.inc(eng.dma_start(out=dbg["dbg_xn0"], in_=xn_ch[0]), "dG", 16)
            C.inc(eng.dma_start(out=dbg["dbg_trrow"], in_=trrow[:, 0:128]), "dG", 16)
            C.inc(eng.dma_start(out=dbg["dbg_colsx"], in_=cols[:, 0:32]), "dG", 16)
            C.ck("dbgsx", "dG")

        C.wait(eng, "eloc")
        C.inc(eng.dma_start(out=ag_e_in[:, :], in_=e_loc), "dG", 16)
        C.wait_val(eng, "dG", C.cnt["dG"])
        C.inc(eng.collective_compute(
            "AllReduce", OP.add, replica_groups=RG,
            ins=[ag_e_in.ap().opt()], outs=[ag_e_out.ap().opt()]), "CC", 1)
        C.wait_val(eng, "CC", C.cnt["CC"])
        C.inc(eng.dma_start(out=e_sum[:, :], in_=ag_e_out[:, :]), "dG", 16)
        C.ck("eback", "dG")
        C.wait_val(eng, "dG", C.cnt["dG"])
        C.inc(eng.collective_compute(
            "AllGather", OP.bypass, replica_groups=RG,
            ins=[ag_in["u"].ap().opt()], outs=[ag_out["u"].ap().opt()]), "CC", 1)
        C.ck("AGwu", "CC")
        C.inc(eng.collective_compute(
            "AllGather", OP.bypass, replica_groups=RG,
            ins=[ag_in["d"].ap().opt()], outs=[ag_out["d"].ap().opt()]), "CC", 1)
        C.ck("AGwd", "CC")

        C.wait(eng, "sgtrr")
        C.inc(eng.dma_start(out=ar_in[1][0:1, 0:128], in_=trrow[0:1, 0:128]), "dG", 16)
        C.wait_val(eng, "dG", C.cnt["dG"])
        if debug:
            C.inc(eng.dma_start(out=dbg["dbg_arin1"], in_=ar_in[1][0:1, 0:128]), "dG", 16)
            C.wait_val(eng, "dG", C.cnt["dG"])
        C.inc(eng.collective_compute(
            "AllReduce", OP.max, replica_groups=RG,
            ins=[ar_in[1].ap().opt()], outs=[ar_out[1].ap().opt()]), "CC", 1)
        C.wait_val(eng, "CC", C.cnt["CC"])
        C.inc(eng.dma_start(out=arbrow[0:1, 0:128], in_=ar_out[1][0:1, 0:128]), "dG", 16)
        if debug:
            C.inc(eng.dma_start(out=dbg["dbg_arout1"], in_=ar_out[1][0:1, 0:128]), "dG", 16)
        C.ck("sgback", "dG")

        C.wait(eng, "sutrr")
        C.inc(eng.dma_start(out=ar_in[2][0:1, 0:256], in_=trrow[0:1, 0:256]), "dG", 16)
        C.wait_val(eng, "dG", C.cnt["dG"])
        C.inc(eng.collective_compute(
            "AllReduce", OP.max, replica_groups=RG,
            ins=[ar_in[2].ap().opt()], outs=[ar_out[2].ap().opt()]), "CC", 1)
        C.wait_val(eng, "CC", C.cnt["CC"])
        C.inc(eng.dma_start(out=arbrow[0:1, 0:256], in_=ar_out[2][0:1, 0:256]), "dG", 16)
        C.ck("suback", "dG")

        C.wait(eng, "sucols")
        C.wait(eng, "upsp23")
        C.inc(eng.dma_start(out=upst_sl[1], in_=up_dram[128:256, :]), "dG", 16)
        C.ck("upldG1", "dG")
        for j in range(3, IC, 2):
            C.wait(eng, f"C2t6_{j - 2}")
            C.inc(eng.dma_start(out=upst_sl[1],
                                in_=up_dram[128 * j:128 * (j + 1), :]), "dG", 16)
            C.ck(f"upldG{j}", "dG")

        C.wait(eng, "sotrr")
        C.inc(eng.dma_start(out=ar_in[3][0:1, 0:128], in_=trrow[0:1, 0:128]), "dG", 16)
        C.wait_val(eng, "dG", C.cnt["dG"])
        C.inc(eng.collective_compute(
            "AllReduce", OP.max, replica_groups=RG,
            ins=[ar_in[3].ap().opt()], outs=[ar_out[3].ap().opt()]), "CC", 1)
        C.wait_val(eng, "CC", C.cnt["CC"])
        C.inc(eng.dma_start(out=arbrow[0:1, 0:128], in_=ar_out[3][0:1, 0:128]), "dG", 16)
        C.ck("soback", "dG")

        for e in range(EC):
            C.wait(eng, f"Dev1_{e}")
            C.wait(eng, f"apref{e}")
            for nb in range(NB):
                ti = eng.tensor_tensor(
                    outstage[e % 2][:, 512 * nb:512 * (nb + 1)],
                    outstage[e % 2][:, 512 * nb:512 * (nb + 1)],
                    astage_sl[e % 2][:, 512 * nb:512 * (nb + 1)], OP.add)
            C.inc(ti, "gE", 1)
            C.ck(f"oev{e}", "gE")
        C.wait(eng, "out7")
        if debug:
            C.inc(eng.dma_start(out=dbg["dbg_cols"], in_=cols[:, 0:32]), "dG", 16)
            C.inc(eng.dma_start(out=dbg["dbg_misc"], in_=misc), "dG", 16)
            C.inc(eng.dma_start(out=dbg["dbg_esum"], in_=e_sum[:, :]), "dG", 16)
            C.inc(eng.dma_start(out=dbg["dbg_o0"], in_=gt[0]), "dG", 16)
            C.inc(eng.dma_start(out=dbg["dbg_up0"], in_=up_dram[0:128, :]), "dG", 16)
            C.inc(eng.dma_start(out=dbg["dbg_attn0"], in_=attn_dram[0:128, :]), "dG", 16)
            C.inc(eng.dma_start(out=dbg["dbg_kb"], in_=Kb), "dG", 16)
            C.inc(eng.dma_start(out=dbg["dbg_vb"], in_=Vb), "dG", 16)
            C.inc(eng.dma_start(out=dbg["dbg_qt"], in_=QTb), "dG", 16)
            C.inc(eng.dma_start(out=dbg["dbg_ctx"], in_=ctxT), "dG", 16)
            C.inc(eng.dma_start(out=dbg["dbg_xq0"], in_=xn_ch[0]), "dG", 16)
            C.inc(eng.dma_start(out=dbg["dbg_dqT"], in_=ag_out["d"].ap()[0:128, :]), "dG", 16)
            C.wait_val(eng, "dG", C.cnt["dG"])

    def emit_tensor(eng):
        C.wait(eng, "consts")
        for wi, wname in enumerate(("g", "u", "d")):
            C.wait(eng, f"dq{wname}bf")
            for b in range(6):
                gi = wi * 6 + b
                if gi >= 2:
                    C.wait(eng, f"dqev{gi - 2}")
                for j in range(4):
                    t = b * 4 + j
                    ti = eng.transpose(ps[gi % 2][:, 128 * j:128 * (j + 1)],
                                       dq_nat[:, 128 * t:128 * (t + 1)],
                                       identf_sb[:, :])
                C.inc(ti, "tE", 1)
                C.ck(f"dqtr{gi}", "tE")
        C.wait(eng, "dqev17")
        for k in range(EC):
            C.wait(eng, f"xb{k}")
            C.wait(eng, f"x2_{k}")
            for nb in range(NB):
                eng.matmul(ps[nb][0:1, :], onesb_sb[:, 0:1],
                           xn_ch[k][:, 512 * nb:512 * (nb + 1)],
                           start=(k == 0), stop=(k == EC - 1))
            for nb in range(NB):
                ti = eng.matmul(ps[4 + nb][0:1, :], onesb_sb[:, 0:1],
                                x2b_sl[k % 2][:, 512 * nb:512 * (nb + 1)],
                                start=(k == 0), stop=(k == EC - 1))
            C.inc(ti, "tE", 1)
            C.ck(f"sums{k}", "tE")
        for nb in range(NB):
            C.wait(eng, f"rqrow{nb}")
            eng.matmul(ps[nb][:, :], onesf_sb[0:1, :], rowa, start=True, stop=True)
            ti = eng.matmul(ps[4 + nb][:, :], onesf_sb[0:1, :], rowb,
                            start=True, stop=True)
            C.inc(ti, "tE", 1)
            C.ck(f"rqmm{nb}", "tE")
        C.wait(eng, "xn7")
        C.wait(eng, "rqev")
        C.wait(eng, "wcast")
        for nb in range(NB):
            for k in range(EC):
                ti = eng.matmul(ps[nb][0:64, :], wq_b[:, 64 * k:64 * (k + 1)],
                                xn_ch[k][:, 512 * nb:512 * (nb + 1)],
                                start=(k == 0), stop=(k == EC - 1))
            C.inc(ti, "tE", 1)
            C.ck(f"qmm{nb}", "tE")
        for c in range(TC):
            if c >= 2:
                C.wait(eng, f"kvev{c - 2}")
            bk, bv = 4 + (c % 2) * 2, 5 + (c % 2) * 2
            for k in range(EC):
                eng.matmul(ps[bk][:, 0:64], xn_ch[k][:, 128 * c:128 * (c + 1)],
                           wk_b[:, 64 * k:64 * (k + 1)],
                           start=(k == 0), stop=(k == EC - 1))
                ti = eng.matmul(ps[bv][:, 0:64], xn_ch[k][:, 128 * c:128 * (c + 1)],
                                wv_b[:, 64 * k:64 * (k + 1)],
                                start=(k == 0), stop=(k == EC - 1))
            C.inc(ti, "tE", 1)
            C.ck(f"kvmm{c}", "tE")
        C.wait(eng, "qev")
        C.wait(eng, "sxcol")
        ti = eng.transpose(ps[0][0:8, 0:128], trsrc[:, :], identf_sb[:, :])
        C.inc(ti, "tE", 1)
        C.ck("sxtr", "tE")
        C.wait(eng, "sxback")
        ti = eng.matmul(ps[0][:, 0:128], onesf_sb[0:1, :], arbrow[0:1, 0:128],
                        start=True, stop=True)
        C.inc(ti, "tE", 1)
        C.ck("sxbc", "tE")
        C.wait(eng, "kvev15")
        for c in range(TC):
            ti = eng.matmul(ps[1][0:64, 0:64], Kb[:, 64 * c:64 * (c + 1)],
                            Vb[:, 64 * c:64 * (c + 1)],
                            start=(c == 0), stop=(c == TC - 1))
        C.inc(ti, "tE", 1)
        C.ck("emm", "tE")
        for i in range(IC):
            grp = (i % 2) * 4
            C.wait(eng, f"wA{i}")
            if i >= 2:
                C.wait(eng, f"actA{i - 2}")
            for k in range(EC):
                if i < 3:
                    C.wait(eng, f"xq{k}")
                for nb in range(NB):
                    ti = eng.matmul(ps[grp + nb][:, :],
                                    wstr_sl[i % 2][:, 128 * k:128 * (k + 1)],
                                    xq_ch[k][:, 512 * nb:512 * (nb + 1)],
                                    start=(k == 0), stop=(k == EC - 1))
            C.inc(ti, "tE", 1)
            C.ck(f"Amm{i}", "tE")
        C.wait(eng, "actA22")
        C.wait(eng, "gmaxcol")
        ti = eng.transpose(ps[0][0:8, 0:128], trsrc[:, :], identf_sb[:, :])
        C.inc(ti, "tE", 1)
        C.ck("sgtr", "tE")
        for i in range(IC):
            grp = (i % 2) * 4
            if i == 3:
                C.wait(eng, "sgback")
                C.wait(eng, "actB2")
                ti = eng.matmul(ps[0][:, 0:128], onesf_sb[0:1, :],
                                arbrow[0:1, 0:128], start=True, stop=True)
                C.inc(ti, "tE", 1)
                C.ck("sgbc", "tE")
            C.wait(eng, f"wB{i}")
            if i == 4:
                C.wait(eng, "sgcols")
            if i >= 2:
                C.wait(eng, f"actB{i - 2}")
            elif i == 0:
                C.wait(eng, "sgtrr")
            else:
                C.wait(eng, "actA23")
            for k in range(EC):
                for nb in range(NB):
                    ti = eng.matmul(ps[grp + nb][:, :],
                                    wstr_sl[i % 2][:, 128 * k:128 * (k + 1)],
                                    xq_ch[k][:, 512 * nb:512 * (nb + 1)],
                                    start=(k == 0), stop=(k == EC - 1))
            C.inc(ti, "tE", 1)
            C.ck(f"Bmm{i}", "tE")
        C.wait(eng, "umaxcol")
        C.wait(eng, "mmaxcol")
        C.wait(eng, "actB22")
        eng.transpose(ps[0][0:8, 0:128], trsrc[:, :], identf_sb[:, :])
        ti = eng.transpose(ps[1][0:8, 0:128], trsrc2[:, :], identf_sb[:, :])
        C.inc(ti, "tE", 1)
        C.ck("sutr", "tE")
        C.wait(eng, "ebf")
        C.wait(eng, "sxcols")
        C.wait(eng, "eev")
        C.wait(eng, "actB23")
        for nb in range(NB):
            ti = eng.matmul(ps[2 + nb][0:64, :], e_bf[:, :],
                            QTb[:, 512 * nb:512 * (nb + 1)], start=True, stop=True)
        C.inc(ti, "tE", 1)
        C.ck("cmmall", "tE")
        for e in range(EC):
            grp = 4 if e % 2 else 0
            C.wait(eng, f"aev{e - 2}" if e >= 2 else "cev")
            if e == 0:
                C.wait(eng, "sutrr")
            for nb in range(NB):
                ti = eng.matmul(ps[grp + nb][:, :], wo_b[:, 128 * e:128 * (e + 1)],
                                ctxT[:, 512 * nb:512 * (nb + 1)], start=True, stop=True)
            C.inc(ti, "tE", 1)
            C.ck(f"amm{e}", "tE")
        C.wait(eng, "suback")
        C.wait(eng, "aev6")
        C.wait(eng, "aev7")
        eng.matmul(ps[2][:, 0:128], onesf_sb[0:1, :], arbrow[0:1, 0:128],
                   start=True, stop=True)
        ti = eng.matmul(ps[3][:, 0:128], onesf_sb[0:1, :], arbrow[0:1, 128:256],
                        start=True, stop=True)
        C.inc(ti, "tE", 1)
        C.ck("subc", "tE")
        C.wait(eng, "sucols")
        C.wait(eng, "rrmaxcol")
        ti = eng.transpose(ps[0][0:8, 0:128], trsrc[:, :], identf_sb[:, :])
        C.inc(ti, "tE", 1)
        C.ck("sotr", "tE")
        C.wait(eng, "soback")
        ti = eng.matmul(ps[2][:, 0:128], onesf_sb[0:1, :], arbrow[0:1, 0:128],
                        start=True, stop=True)
        C.inc(ti, "tE", 1)
        C.ck("sobc", "tE")
        for e in range(EC):
            grp = (e % 2) * 4
            C.wait(eng, f"wD{e}")
            if e >= 2:
                C.wait(eng, f"Dev1_{e - 2}")
            else:
                C.wait(eng, f"actB{22 + e}")
            for i in range(IC):
                if e == 0:
                    C.wait(eng, f"o{i}")
                for nb in range(NB):
                    ti = eng.matmul(ps[grp + nb][:, :],
                                    wstrD_sl[e % 2][:, 128 * i:128 * (i + 1)],
                                    gt[i][:, 512 * nb:512 * (nb + 1)],
                                    start=(i == 0), stop=(i == IC - 1))
            C.inc(ti, "tE", 1)
            C.ck(f"Dmm{e}", "tE")

    def emit_vector(eng):
        eng.memset(eps_sb, LNEPS)
        eng.memset(Mcol, MAGIC)
        ti = eng.memset(negMcol, -MAGIC)
        C.inc(ti, "vE", 1)
        C.ck("memset", "vE")
        prev = None
        for wname in ("g", "u", "d"):
            C.wait(eng, f"dqnat{wname}")
            if prev is not None:
                C.wait(eng, f"dqstage{prev}")
            nat3 = dq_nat.rearrange("p (b c) -> p b c", c=32)
            eng.tensor_reduce(dq_bmax, nat3, AX.X, OP.max, apply_absolute_value=True)
            eng.tensor_scalar(dq_s, dq_bmax, 1.0 / QW, SEPS, OP.mult, OP.max)
            eng.reciprocal(dq_rs, dq_s)
            eng.tensor_tensor(nat3, nat3,
                              dq_rs[:, :, None].broadcast_to([128, 96, 32]), OP.mult)
            eng.tensor_scalar(dq_nat, dq_nat, MAGIC, MAGIC, OP.add, OP.subtract)
            ti = eng.tensor_tensor(nat3, nat3,
                                   dq_s[:, :, None].broadcast_to([128, 96, 32]), OP.mult)
            C.inc(ti, "vE", 1)
            C.ck(f"dq{wname}bf", "vE")
            prev = wname

        C.wait(eng, f"sums{EC - 1}")
        for nb in range(NB):
            if nb >= 1:
                C.wait(eng, f"rqmm{nb - 1}")
            eng.tensor_copy(sum_r, ps[nb][0:1, :])
            eng.tensor_copy(ssq_r, ps[4 + nb][0:1, :])
            eng.tensor_scalar(rowa, sum_r, 1.0 / E, None, OP.mult, OP.bypass)
            eng.tensor_tensor(rowb, rowa, rowa, OP.mult)
            eng.tensor_scalar(rowa, ssq_r, 1.0 / E, None, OP.mult, OP.bypass)
            ti = eng.tensor_tensor(rowb, rowa, rowb, OP.subtract)
            C.inc(ti, "vE", 1)
            C.ck(f"var{nb}", "vE")
            C.wait(eng, f"sqrt{nb}")
            eng.reciprocal(rowa, rowb)
            ti = eng.scalar_tensor_tensor(rowb, sum_r, 1.0 / E, rowa,
                                          OP.mult, OP.mult)
            C.inc(ti, "vE", 1)
            C.ck(f"rqrow{nb}", "vE")
        for nb in range(NB):
            C.wait(eng, f"rqmm{nb}")
            eng.tensor_copy(Rb[:, 512 * nb:512 * (nb + 1)], ps[nb][:, :])
            ti = eng.tensor_copy(Qb[:, 512 * nb:512 * (nb + 1)], ps[4 + nb][:, :])
        C.inc(ti, "vE", 1)
        C.ck("rqev", "vE")
        for k in range(EC):
            eng.tensor_tensor(xtmp, x_ch[k], Rb, OP.mult)
            ti = eng.tensor_tensor(xn_ch[k], xtmp, Qb, OP.subtract)
            C.inc(ti, "vE", 1)
            C.ck(f"xn{k}", "vE")
            eng.tensor_reduce(mctmp, xn_ch[k], AX.X, OP.max, apply_absolute_value=True)
            ti = (eng.tensor_tensor(maxcol, maxcol, mctmp, OP.max) if k else
                  eng.tensor_copy(maxcol, mctmp))
            C.inc(ti, "vE", 1)
        ti = eng.tensor_copy(trsrc[:, 0:1], maxcol)
        C.inc(ti, "vE", 1)
        C.ck("sxcol", "vE")
        C.wait(eng, "sxtr")
        ti = eng.tensor_copy(trrow[:, 0:128], ps[0][0:8, 0:128])
        C.inc(ti, "vE", 1)
        C.ck("sxtrr", "vE")
        C.wait(eng, "sxbc")
        eng.tensor_reduce(rawc, ps[0][:, 0:128], AX.X, OP.max)
        if debug:
            eng.tensor_copy(astage_sl[1][:, 0:128], ps[0][:, 0:128])
        eng.tensor_scalar(sx_col, rawc, 1.0 / QA, SEPS, OP.mult, OP.max)
        ti = recip_col(eng, rsx_col, sx_col)
        C.inc(ti, "vE", 1)
        C.ck("sxcols", "vE")
        C.wait(eng, "kvmm15")
        if debug:
            C.wait(eng, "dbgsx")
        for k in range(EC):
            eng.tensor_scalar(SCt, xn_ch[k], rsx_col, MAGIC, OP.mult, OP.add)
            ti = eng.tensor_scalar(xq_ch[k], SCt, MAGIC, sx_col, OP.subtract, OP.mult)
            C.inc(ti, "vE", 1)
            C.ck(f"xq{k}", "vE")
        C.wait(eng, "eback")
        ti = eng.tensor_scalar(e_bf[:, :], e_sum[:, :], C_ATTN, None, OP.mult, OP.bypass)
        C.inc(ti, "vE", 1)
        C.ck("ebf", "vE")
        for i in range(IC):
            C.wait(eng, f"actA{i}")
            eng.tensor_reduce(mctmp, gt[i], AX.X, OP.max, apply_absolute_value=True)
            ti = (eng.tensor_tensor(gmaxcol, gmaxcol, mctmp, OP.max) if i else
                  eng.tensor_copy(gmaxcol, mctmp))
            C.inc(ti, "vE", 1)
        ti = eng.tensor_copy(trsrc[:, 0:1], gmaxcol)
        C.inc(ti, "vE", 1)
        C.ck("gmaxcol", "vE")
        C.wait(eng, "sgtr")
        ti = eng.tensor_copy(trrow[:, 0:128], ps[0][0:8, 0:128])
        C.inc(ti, "vE", 1)
        C.ck("sgtrr", "vE")
        def vec_c1(j):
            C.wait(eng, f"sig{j}")
            tb = SCt if j % 2 == 0 else SCv
            ub = SCu if j % 2 == 0 else astageF
            eng.tensor_scalar(ub, ub, SIG, MAGIC, OP.mult, OP.add)
            eng.tensor_scalar(ub, ub, MAGIC + SIGMAX, MAGIC, OP.min, OP.subtract)
            ti = eng.scalar_tensor_tensor(gt[j], tb, -MAGIC, ub, OP.add, OP.mult)
            C.inc(ti, "vE", 1)
            C.ck(f"C1m{j}", "vE")
            eng.tensor_reduce(mctmp, gt[j], AX.X, OP.max, apply_absolute_value=True)
            ti = (eng.tensor_tensor(mmaxcol, mmaxcol, mctmp, OP.max) if j else
                  eng.tensor_copy(mmaxcol, mctmp))
            C.inc(ti, "vE", 1)

        for i in range(IC):
            if i == 2:
                C.wait(eng, "sgbc")
                eng.tensor_reduce(rawc, ps[0][:, 0:128], AX.X, OP.max)
                eng.tensor_scalar(sg_col, rawc, 1.0 / QA, SEPS, OP.mult, OP.max)
                recip_col(eng, rsg_col, sg_col)
                eng.tensor_scalar(sg65_col, sg_col, 1.0 / SIG, None,
                                  OP.mult, OP.bypass)
                ti = eng.tensor_scalar(msg_col, sg_col, -MAGIC, None,
                                       OP.mult, OP.bypass)
                C.inc(ti, "vE", 1)
                C.ck("sgcols", "vE")
            C.wait(eng, f"actB{i}")
            eng.tensor_reduce(mctmp, upst_sl[i % 2], AX.X, OP.max,
                              apply_absolute_value=True)
            ti = (eng.tensor_tensor(umaxcol, umaxcol, mctmp, OP.max) if i else
                  eng.tensor_copy(umaxcol, mctmp))
            C.inc(ti, "vE", 1)
            C.ck(f"umax{i}", "vE")
            if i >= 2:
                vec_c1(i - 2)
        for j in range(IC - 2, IC):
            vec_c1(j)
        eng.tensor_copy(trsrc[:, 0:1], umaxcol)
        ti = eng.tensor_copy(trsrc2[:, 0:1], mmaxcol)
        C.inc(ti, "vE", 1)
        C.ck("umaxcol", "vE")
        C.ck("mmaxcol", "vE")
        C.wait(eng, "sutr")
        eng.tensor_copy(trrow[:, 0:128], ps[0][0:8, 0:128])
        ti = eng.tensor_copy(trrow[:, 128:256], ps[1][0:8, 0:128])
        C.inc(ti, "vE", 1)
        C.ck("sutrr", "vE")
        for e in range(1, EC, 2):
            C.wait(eng, f"amm{e}")
            if e >= 3:
                C.wait(eng, f"asp{e - 2}")
            for nb in range(NB):
                ti = eng.tensor_copy(astage_sl[1][:, 512 * nb:512 * (nb + 1)],
                                     ps[4 + nb][:, :])
            C.inc(ti, "vE", 1)
            C.ck(f"aev{e}", "vE")
        C.wait(eng, "subc")
        eng.tensor_reduce(rawc, ps[2][:, 0:128], AX.X, OP.max)
        eng.tensor_reduce(raw2c, ps[3][:, 0:128], AX.X, OP.max)
        eng.tensor_scalar(sup_col, rawc, 1.0 / QA, SEPS, OP.mult, OP.max)
        recip_col(eng, rsu_col, sup_col)
        eng.tensor_tensor(tcol, raw2c, sg65_col, OP.mult)
        eng.tensor_scalar(sg2_col, tcol, 1.0 / QA, SEPS, OP.mult, OP.max)
        recip_col(eng, tcol, sg2_col)
        ti = eng.tensor_tensor(cg_col, sg65_col, tcol, OP.mult)
        C.inc(ti, "vE", 1)
        C.ck("sucols", "vE")
        for i in range(IC):
            C.wait(eng, f"C2t6_{i}")
            buf = SCu if i % 2 == 0 else SCv
            t5b = SCt if i % 2 == 0 else xsc_sl[0]
            eng.tensor_scalar(buf, buf, MAGIC, None, OP.subtract, OP.bypass)
            C.wait(eng, f"C2t5_{i}")
            ti = eng.scalar_tensor_tensor(gt[i], t5b, -MAGIC, buf, OP.add, OP.mult)
            C.inc(ti, "vE", 1)
            C.ck(f"C2rr{i}", "vE")
            eng.tensor_reduce(mctmp, gt[i], AX.X, OP.max, apply_absolute_value=True)
            ti = (eng.tensor_tensor(rrmaxcol, rrmaxcol, mctmp, OP.max) if i else
                  eng.tensor_copy(rrmaxcol, mctmp))
            C.inc(ti, "vE", 1)
        ti = eng.tensor_copy(trsrc[:, 0:1], rrmaxcol)
        C.inc(ti, "vE", 1)
        C.ck("rrmaxcol", "vE")
        C.wait(eng, "sotr")
        ti = eng.tensor_copy(trrow[:, 0:128], ps[0][0:8, 0:128])
        C.inc(ti, "vE", 1)
        C.ck("sotrr", "vE")
        C.wait(eng, "sobc")
        eng.tensor_reduce(rawc, ps[2][:, 0:128], AX.X, OP.max)
        eng.tensor_tensor(pcol, sg2_col, sup_col, OP.mult)
        eng.tensor_tensor(tcol, rawc, pcol, OP.mult)
        eng.tensor_scalar(so_col, tcol, 1.0 / QA, SEPS, OP.mult, OP.max)
        recip_col(eng, tcol, so_col)
        ti = eng.tensor_tensor(cr_col, pcol, tcol, OP.mult)
        C.inc(ti, "vE", 1)
        C.ck("socols", "vE")
        C.ck("somax", "vE")
        for i in range(IC):
            if i % 2 == 0:
                C.wait(eng, f"ot7_{i}")
                t7b = xsc_sl[(i % 4) // 2]
                ti = eng.tensor_scalar(gt[i], t7b, MAGIC, so_col,
                                       OP.subtract, OP.mult)
            else:
                eng.tensor_scalar(SCt, gt[i], cr_col, MAGIC, OP.mult, OP.add)
                ti = eng.tensor_scalar(gt[i], SCt, MAGIC, so_col,
                                       OP.subtract, OP.mult)
            C.inc(ti, "vE", 1)
            C.ck(f"o{i}", "vE")
        for e in range(EC):
            C.wait(eng, f"Dmm{e}")
            C.wait(eng, f"xpref{e}")
            if e >= 2:
                C.wait(eng, f"out{e - 2}")
            grp = (e % 2) * 4
            for nb in range(NB):
                ti = eng.tensor_tensor(outstage[e % 2][:, 512 * nb:512 * (nb + 1)],
                                       ps[grp + nb][:, :],
                                       xpref_sl[e % 2][:, 512 * nb:512 * (nb + 1)],
                                       OP.add)
            C.inc(ti, "vE", 1)
            C.ck(f"Dev1_{e}", "vE")

    def emit_scalar(eng):
        for wi, wname in enumerate(("g", "u", "d")):
            for b in range(6):
                gi = wi * 6 + b
                C.wait(eng, f"dqtr{gi}")
                ti = eng.copy(dq_T[:, 512 * b:512 * (b + 1)], ps[gi % 2][:, :])
                C.inc(ti, "sE", 1)
                C.ck(f"dqev{gi}", "sE")
            C.ck(f"dqT{wname}", "sE")
        for k in range(EC):
            C.wait(eng, f"x{k}")
            if k >= 2:
                C.wait(eng, f"sums{k - 2}")
            ti = eng.activation(x2b_sl[k % 2], x_ch[k], AF.Square)
            C.inc(ti, "sE", 1)
            C.ck(f"x2_{k}", "sE")
        C.wait(eng, "memset")
        for nb in range(NB):
            C.wait(eng, f"var{nb}")
            ti = eng.activation(rowb, rowb, AF.Sqrt, bias=eps_sb, scale=1.0)
            C.inc(ti, "sE", 1)
            C.ck(f"sqrt{nb}", "sE")
        for nb in range(NB):
            C.wait(eng, f"qmm{nb}")
            ti = eng.copy(QTb[:, 512 * nb:512 * (nb + 1)], ps[nb][0:64, :])
        C.inc(ti, "sE", 1)
        C.ck("qev", "sE")
        for c in range(TC):
            C.wait(eng, f"kvmm{c}")
            bk, bv = 4 + (c % 2) * 2, 5 + (c % 2) * 2
            eng.copy(Kb[:, 64 * c:64 * (c + 1)], ps[bk][:, 0:64])
            ti = eng.copy(Vb[:, 64 * c:64 * (c + 1)], ps[bv][:, 0:64])
            C.inc(ti, "sE", 1)
            C.ck(f"kvev{c}", "sE")
        C.wait(eng, "emm")
        ti = eng.copy(e_loc[:, :], ps[1][0:64, 0:64])
        C.inc(ti, "sE", 1)
        C.ck("eloc", "sE")
        C.ck("eev", "sE")
        for i in range(IC):
            C.wait(eng, f"Amm{i}")
            grp = (i % 2) * 4
            for nb in range(NB):
                ti = eng.copy(gt[i][:, 512 * nb:512 * (nb + 1)], ps[grp + nb][:, :])
            C.inc(ti, "sE", 1)
            C.ck(f"actA{i}", "sE")
        spill0 = C.cnt["dA"]

        def act_c1(j):
            if j == 0:
                C.wait(eng, "sgcols")
            if j >= 2:
                C.wait(eng, f"C1m{j - 2}")
            tb = SCt if j % 2 == 0 else SCv
            ub = SCu if j % 2 == 0 else astageF
            ti = eng.activation(tb, gt[j], AF.Identity, bias=Mcol, scale=rsg_col)
            C.inc(ti, "sE", 1)
            C.ck(f"C1t{j}", "sE")
            ti = eng.activation(ub, tb, AF.Sigmoid, bias=msg_col, scale=sg_col)
            C.inc(ti, "sE", 1)
            C.ck(f"sig{j}", "sE")

        for i in range(IC):
            C.wait(eng, f"Bmm{i}")
            if i >= 2:
                C.wait_val(eng, "dA", spill0 + 16 * (i - 1))
                C.wait(eng, f"umax{i - 2}")
            grp = (i % 2) * 4
            for nb in range(NB):
                ti = eng.copy(upst_sl[i % 2][:, 512 * nb:512 * (nb + 1)],
                              ps[grp + nb][:, :])
            C.inc(ti, "sE", 1)
            C.ck(f"actB{i}", "sE")
            C.inc(eng.dma_start(out=up_dram[128 * i:128 * (i + 1), :],
                                in_=upst_sl[i % 2]), "dA", 16)
            C.ck(f"upsp{i}", "dA")
            if i >= 2:
                act_c1(i - 2)
        for j in range(IC - 2, IC):
            act_c1(j)
        C.wait(eng, "cmmall")
        for nb in range(NB):
            ti = eng.copy(ctxT[:, 512 * nb:512 * (nb + 1)], ps[2 + nb][0:64, :])
        C.inc(ti, "sE", 1)
        C.ck("cev", "sE")
        C.wait(eng, "C1m23")
        for e in range(EC):
            if e % 2 == 0:
                C.wait(eng, f"amm{e}")
                if e >= 2:
                    C.wait(eng, f"asp{e - 2}")
                for nb in range(NB):
                    ti = eng.copy(astage_sl[0][:, 512 * nb:512 * (nb + 1)],
                                  ps[nb][:, :])
                C.inc(ti, "sE", 1)
                C.ck(f"aev{e}", "sE")
            else:
                C.wait(eng, f"aev{e}")
            C.inc(eng.dma_start(out=attn_dram[128 * e:128 * (e + 1), :],
                                in_=astage_sl[e % 2]), "dA", 16)
            C.ck(f"asp{e}", "dA")
        C.wait(eng, "sucols")
        C.inc(eng.dma_start(out=upst_sl[0], in_=up_dram[0:128, :]), "dA", 16)
        C.ck("upldA0", "dA")
        for i in range(IC):
            if i >= 2:
                C.wait(eng, f"C2rr{i - 2}")
            t5b = SCt if i % 2 == 0 else xsc_sl[0]
            ti = eng.activation(t5b, gt[i], AF.Identity, bias=Mcol, scale=cg_col)
            C.inc(ti, "sE", 1)
            C.ck(f"C2t5_{i}", "sE")
            C.wait(eng, f"upldA{i}" if i % 2 == 0 else f"upldG{i}")
            buf = SCu if i % 2 == 0 else SCv
            ti = eng.activation(buf, upst_sl[i % 2], AF.Identity,
                                bias=Mcol, scale=rsu_col)
            C.inc(ti, "sE", 1)
            C.ck(f"C2t6_{i}", "sE")
            if i + 2 < IC and i % 2 == 0:
                C.inc(eng.dma_start(out=upst_sl[0],
                                    in_=up_dram[128 * (i + 2):128 * (i + 3), :]),
                      "dA", 16)
                C.ck(f"upldA{i + 2}", "dA")
        for i in range(0, IC, 2):
            if i == 0:
                C.wait(eng, "socols")
            if i >= 4:
                C.wait(eng, f"o{i - 4}")
            t7b = xsc_sl[(i % 4) // 2]
            ti = eng.activation(t7b, gt[i], AF.Identity, bias=Mcol, scale=cr_col)
            C.inc(ti, "sE", 1)
            C.ck(f"ot7_{i}", "sE")
        pf0 = C.cnt["dA"]
        for e in range(EC):
            if e == 0:
                C.wait(eng, "somax")
            if e >= 2:
                C.wait(eng, f"Dev1_{e - 2}")
                C.wait(eng, f"oev{e - 2}")
            C.inc(eng.dma_start(out=xpref_sl[e % 2],
                                in_=xT[128 * e:128 * (e + 1), :]), "dA", 16)
            C.inc(eng.dma_start(out=astage_sl[e % 2],
                                in_=attn_dram[128 * e:128 * (e + 1), :]), "dA", 16)
            C.wait_val(eng, "dA", pf0 + 32 * (e + 1))
            C.ck(f"xpref{e}", "dA")
            C.ck(f"apref{e}", "dA")

    C.reset(False)
    emit_sync(Mock())
    emit_vector(Chain(Mock(), C, "vE"))
    emit_scalar(Chain(Mock(), C, "sE"))
    emit_tensor(Mock())
    emit_gpsimd(Chain(Mock(), C, "gE"))

    C.reset(True)
    with nc.Block() as block:
        @block.sync
        def _(eng):
            emit_sync(eng)

        @block.vector
        def _(eng):
            emit_vector(Chain(eng, C, "vE"))

        @block.scalar
        def _(eng):
            emit_scalar(Chain(eng, C, "sE"))

        @block.tensor
        def _(eng):
            emit_tensor(eng)

        @block.gpsimd
        def _(eng):
            emit_gpsimd(Chain(eng, C, "gE"))

    st.close()
    return nc


_NC_CACHE = []


def kernel(L0_attn_x, L0_attn_wq, L0_attn_wk, L0_attn_wv, L0_attn_wo,
           L0_ffn_w_gate, L0_ffn_w_up, L0_ffn_w_down):
    x = np.ascontiguousarray(np.asarray(L0_attn_x, dtype=np.float32).reshape(16384, 1024))
    wq = np.asarray(L0_attn_wq, dtype=np.float32)
    wk = np.asarray(L0_attn_wk, dtype=np.float32)
    wv = np.asarray(L0_attn_wv, dtype=np.float32)
    wo = np.asarray(L0_attn_wo, dtype=np.float32)
    wg = np.asarray(L0_ffn_w_gate, dtype=np.float32)
    wu = np.asarray(L0_ffn_w_up, dtype=np.float32)
    wd = np.asarray(L0_ffn_w_down, dtype=np.float32)

    xT_full = np.ascontiguousarray(x.T)
    wqt = np.ascontiguousarray(wq.T)
    wkt = np.ascontiguousarray(wk.T)
    wvt = np.ascontiguousarray(wv.T)
    wot = np.ascontiguousarray(wo.T)
    onesf_h = np.ones((128, 128), dtype=np.float32)
    onesb_h = np.ones((128, 1), dtype=ml_dtypes.bfloat16)
    identb_h = np.eye(128, dtype=ml_dtypes.bfloat16)

    in_maps = []
    for r in range(NCORES):
        in_maps.append({
            "xT": np.ascontiguousarray(xT_full[:, T * r:T * (r + 1)]),
            "wqt": wqt, "wkt": wkt, "wvt": wvt, "wot": wot,
            "wg_nat": np.ascontiguousarray(wg[:, 128 * r:128 * (r + 1)]),
            "wu_nat": np.ascontiguousarray(wu[:, 128 * r:128 * (r + 1)]),
            "wd_nat": np.ascontiguousarray(wd[:, 384 * r:384 * (r + 1)]),
            "onesf": onesf_h, "onesb": onesb_h, "identb": identb_h,
            "identf": np.eye(128, dtype=np.float32),
        })

    if not _NC_CACHE:
        _NC_CACHE.append(build_program())
    nc = _NC_CACHE[0]
    res = run_bass_kernel_spmd(nc, in_maps, core_ids=list(range(NCORES)))
    outT_full = np.concatenate(
        [np.asarray(res.results[r]["outT"]) for r in range(NCORES)], axis=1)
    return np.ascontiguousarray(outT_full.T).reshape(1, 16384, 1024).astype(np.float32)


# version 13
# speedup vs baseline: 1.0113x; 1.0113x over previous
import contextlib
import numpy as np
import ml_dtypes

import concourse.bass as bass
import concourse.mybir as mybir
from concourse.bass_utils import run_bass_kernel_spmd

F32 = mybir.dt.float32
BF16 = mybir.dt.bfloat16
AF = mybir.ActivationFunctionType
OP = mybir.AluOpType
AX = mybir.AxisListType

NCORES = 8
T = 2048
E = 1024
EC = 8
D = 64
I = 3072
IC = 24
NB = 4
TC = 16

MAGIC = 12582912.0
QA = 32767.0
QW = 7.0
SIG = 65536.0
SIGMAX = 65535.0
C_ATTN = float(np.sqrt(3.0 / 64.0))
LNEPS = 1e-5
SEPS = 1e-12


class Ctx:

    def __init__(self):
        self.pass2 = False
        self.cnt = {}
        self.cks = {}
        self.sems = {}

    def reset(self, pass2):
        self.pass2 = pass2
        self.cnt = {k: 0 for k in ("dS", "dA", "dG", "CC", "tE", "vE", "sE", "gE")}

    AUTO = ("vE", "sE", "gE")

    def inc(self, instr, sem, n):
        if sem in self.AUTO:
            return
        self.cnt[sem] += n
        if self.pass2:
            instr.then_inc(self.sems[sem], n)

    def ck(self, label, sem):
        v = (sem, self.cnt[sem])
        if self.pass2:
            assert self.cks[label] == v, (label, self.cks[label], v)
        else:
            assert label not in self.cks, f"dup ck {label}"
            self.cks[label] = v

    def wait(self, eng, label):
        if self.pass2:
            sem, v = self.cks[label]
            eng.wait_ge(self.sems[sem], v)

    def wait_val(self, eng, sem, v):
        if self.pass2:
            eng.wait_ge(self.sems[sem], v)


class Mock:
    def __getattr__(self, name):
        return lambda *a, **k: None


_CHAIN_OPS = {
    "tensor_copy", "tensor_tensor", "tensor_scalar", "tensor_single_scalar",
    "tensor_reduce", "reduce_max", "reduce_sum", "scalar_tensor_tensor",
    "reciprocal", "memset", "memzero", "activation", "copy", "add", "mul",
    "square", "sqrt", "sign", "tensor_add", "tensor_sub", "tensor_mul",
    "tensor_max", "tensor_scalar_add", "tensor_scalar_sub", "tensor_scalar_mul",
    "tensor_scalar_max", "tensor_scalar_min", "transpose", "select",
}


class Chain:

    def __init__(self, eng, C, sem):
        self._eng = eng
        self._C = C
        self._sem = sem

    def __getattr__(self, name):
        eng, C, sem = self._eng, self._C, self._sem
        if name in _CHAIN_OPS:
            def op(*a, **k):
                C.wait_val(eng, sem, C.cnt[sem])
                instr = getattr(eng, name)(*a, **k)
                C.cnt[sem] += 1
                if C.pass2:
                    instr.then_inc(C.sems[sem], 1)
                return instr
            return op
        if name == "dma_start":
            def dma(*a, **k):
                C.wait_val(eng, sem, C.cnt[sem])
                return eng.dma_start(*a, **k)
            return dma
        return getattr(eng, name)


def build_program(debug=False):
    nc = bass.Bass("TRN2", target_bir_lowering=False)
    C = Ctx()

    xT = nc.dram_tensor("xT", [E, T], F32, kind="ExternalInput").ap()
    wqt = nc.dram_tensor("wqt", [E, D], F32, kind="ExternalInput").ap()
    wkt = nc.dram_tensor("wkt", [E, D], F32, kind="ExternalInput").ap()
    wvt = nc.dram_tensor("wvt", [E, D], F32, kind="ExternalInput").ap()
    wot = nc.dram_tensor("wot", [D, E], F32, kind="ExternalInput").ap()
    wg_nat = nc.dram_tensor("wg_nat", [I, 128], F32, kind="ExternalInput").ap()
    wu_nat = nc.dram_tensor("wu_nat", [I, 128], F32, kind="ExternalInput").ap()
    wd_nat = nc.dram_tensor("wd_nat", [E, 384], F32, kind="ExternalInput").ap()
    onesf = nc.dram_tensor("onesf", [128, 128], F32, kind="ExternalInput").ap()
    onesb = nc.dram_tensor("onesb", [128, 1], BF16, kind="ExternalInput").ap()
    identb = nc.dram_tensor("identb", [128, 128], BF16, kind="ExternalInput").ap()
    identf = nc.dram_tensor("identf", [128, 128], F32, kind="ExternalInput").ap()
    outT = nc.dram_tensor("outT", [E, T], F32, kind="ExternalOutput").ap()
    dbg = {}
    if debug:
        for n, sh, dt in [("dbg_cols", [128, 32], F32), ("dbg_misc", [1, 512], F32),
                          ("dbg_esum", [64, 64], F32), ("dbg_o0", [128, T], BF16),
                          ("dbg_up0", [128, T], BF16), ("dbg_attn0", [128, T], BF16),
                          ("dbg_kb", [128, TC * D], BF16), ("dbg_vb", [128, TC * D], BF16),
                          ("dbg_qt", [64, T], BF16), ("dbg_ctx", [64, T], BF16),
                          ("dbg_xq0", [128, T], BF16), ("dbg_dqT", [128, 1024], BF16),
                          ("dbg_sx", [1, 64], F32), ("dbg_xn0", [128, T], BF16),
                          ("dbg_trrow", [8, 128], F32), ("dbg_colsx", [128, 32], F32),
                          ("dbg_arout", [1, 256], F32), ("dbg_arbrow", [1, 128], F32),
                          ("dbg_bc", [128, 128], BF16),
                          ("dbg_arin1", [1, 128], F32), ("dbg_arout1", [1, 128], F32)]:
            dbg[n] = nc.dram_tensor(n, sh, dt, kind="ExternalOutput").ap()

    ag_in = {
        "g": nc.dram_tensor("ag_wg_in", [128, I], BF16),
        "u": nc.dram_tensor("ag_wu_in", [128, I], BF16),
        "d": nc.dram_tensor("ag_wd_in", [384, E], BF16),
    }
    ag_out = {
        "g": nc.dram_tensor("ag_wg_out", [E, I], BF16),
        "u": nc.dram_tensor("ag_wu_out", [E, I], BF16),
        "d": nc.dram_tensor("ag_wd_out", [I, E], BF16),
    }
    ag_e_in = nc.dram_tensor("ag_e_in", [D, D], F32)
    ag_e_out = nc.dram_tensor("ag_e_out", [D, D], F32)
    ar_in = [nc.dram_tensor(f"ar{j}_in", [1, 256], F32) for j in range(4)]
    ar_out = [nc.dram_tensor(f"ar{j}_out", [1, 256], F32) for j in range(4)]
    up_dram = nc.dram_tensor("up_dram", [I, T], BF16)
    attn_dram = nc.dram_tensor("attn_dram", [E, T], BF16)
    flat_dram = nc.dram_tensor("flat_dram", [4, 128], F32)

    st = contextlib.ExitStack()
    sb = lambda name, shape, dt=F32: st.enter_context(nc.sbuf_tensor(name, shape, dt)).ap()

    AR = sb("AR", [128, 49152], BF16)
    ARf = AR.bitcast(F32)
    x_ch = [ARf[:, 2048 * k:2048 * (k + 1)] for k in range(EC)]
    gt = [AR[:, 2048 * i:2048 * (i + 1)] for i in range(IC)]
    dq_nat = ARf[:, 16384:19456]
    dq_T = AR[:, 41984:45056]

    xn = sb("xn", [128, T * EC], BF16)
    xn_ch = [xn[:, 2048 * k:2048 * (k + 1)] for k in range(EC)]
    xq_ch = xn_ch
    xpref = xn.bitcast(F32)
    xpref_sl = [xpref[:, 2048 * j:2048 * (j + 1)] for j in range(2)]
    xsc_sl = [xpref[:, 4096:6144], xpref[:, 6144:8192]]

    SC = sb("SC", [128, 6144], F32)
    xnf = None
    SCt = SC[:, 0:2048]
    SCu = SC[:, 2048:4096]
    SCv = SC[:, 4096:6144]
    Rb = SCt
    Qb = SCu
    xtmp = SCv
    outstage = [SC[:, 0:2048], SC[:, 2048:4096]]
    x2b_sl = [SC.bitcast(BF16)[:, 8192:10240], SC.bitcast(BF16)[:, 10240:12288]]

    upst = sb("upst", [128, 4096], BF16)
    upst_sl = [upst[:, 2048 * j:2048 * (j + 1)] for j in range(2)]
    ctxT = sb("ctxT", [64, T], BF16)
    QTb = sb("QTb", [64, T], BF16)
    wstr = sb("wstr", [128, 2048], BF16)
    wstr_sl = [wstr[:, 1024 * j:1024 * (j + 1)] for j in range(2)]
    wstrD = sb("wstrD", [128, 6144], BF16)
    wstrD_sl = [wstrD[:, 3072 * j:3072 * (j + 1)] for j in range(2)]
    astage = sb("astage", [128, 4096], BF16)
    astage_sl = [astage[:, 2048 * j:2048 * (j + 1)] for j in range(2)]
    astageF = astage.bitcast(F32)
    Kb = sb("Kb", [128, TC * D], BF16)
    Vb = sb("Vb", [128, TC * D], BF16)

    onesf_sb = sb("onesf_sb", [128, 128], F32)
    onesb_sb = sb("onesb_sb", [128, 1], BF16)
    identb_sb = sb("identb_sb", [128, 128], BF16)
    identf_sb = sb("identf_sb", [128, 128], F32)
    wq_f = ARf[:, 22528:23040]
    wk_f = ARf[:, 23040:23552]
    wv_f = ARf[:, 23552:24064]
    wo_f = ARf[0:64, 19456:20480]
    wq_b = sb("wq_b", [128, 512], BF16)
    wk_b = sb("wk_b", [128, 512], BF16)
    wv_b = sb("wv_b", [128, 512], BF16)
    wo_b = sb("wo_b", [64, 1024], BF16)

    dqs = sb("dqs", [128, 288], F32)
    dq_bmax = dqs[:, 0:96]
    dq_s = dqs[:, 96:192]
    dq_rs = dqs[:, 192:288]

    cols = sb("cols", [128, 48], F32)
    maxcol = cols[:, 0:1]
    mctmp = cols[:, 1:2]
    gmaxcol = cols[:, 2:3]
    umaxcol = cols[:, 3:4]
    mmaxcol = cols[:, 4:5]
    rrmaxcol = cols[:, 5:6]
    Mcol = cols[:, 6:7]
    gtmp = cols[:, 7:8]
    negMcol = cols[:, 47:48]
    sx_col = cols[:, 8:9]
    rsx_col = cols[:, 9:10]
    sg_col = cols[:, 10:11]
    rsg_col = cols[:, 11:12]
    msg_col = cols[:, 12:13]
    rsu_col = cols[:, 13:14]
    cg_col = cols[:, 14:15]
    cr_col = cols[:, 15:16]
    so_col = cols[:, 16:17]
    rawc = cols[:, 17:18]
    raw2c = cols[:, 18:19]
    sg65_col = cols[:, 19:20]
    sg2_col = cols[:, 20:21]
    sup_col = cols[:, 21:22]
    tcol = cols[:, 22:23]
    pcol = cols[:, 23:24]
    rec8 = cols[:, 24:32]
    rec8b = cols[:, 32:40]
    rec8c = cols[:, 40:48]

    rows = SCv[0:1, :]
    sum_r = rows[0:1, 0:512]
    ssq_r = rows[0:1, 512:1024]
    rowa = rows[0:1, 1024:1536]
    rowb = rows[0:1, 1536:2048]
    misc = sb("misc", [1, 512], F32)
    srow = misc[0:1, 0:16]
    arrow = misc[0:1, 16:32]
    arb = misc[0:1, 32:48]
    arbrow = misc[0:1, 64:320]
    eps_sb = misc[0:1, 320:321]
    sclr = misc[0:1, 336:352]
    trrow = sb("trrow", [8, 256], F32)
    trsrc = sb("trsrc", [128, 8], F32)
    trsrc2 = sb("trsrc2", [128, 8], F32)
    e_loc = sb("e_loc", [64, 64], F32)
    e_sum = sb("e_sum", [64, 64], F32)
    e_bf = sb("e_bf", [64, 64], BF16)

    ps = [st.enter_context(nc.psum_tensor(f"psb{j}", [128, 512], F32)).ap()
          for j in range(8)]

    for name in ("dS", "dA", "dG", "CC", "tE", "vE", "sE", "gE"):
        C.sems[name] = st.enter_context(nc.semaphore(name))

    RG = [list(range(NCORES))]

    def recip_col(eng, dst_col, src_col):
        eng.tensor_copy(dq_bmax, src_col.broadcast_to([128, 96]))
        eng.reciprocal(dq_s, dq_bmax)
        return eng.tensor_copy(dst_col, dq_s[:, 0:1])


    def emit_sync(eng):
        for k in range(EC):
            C.inc(eng.dma_start(out=x_ch[k], in_=xT[128 * k:128 * (k + 1), :]), "dS", 16)
            C.ck(f"x{k}", "dS")
        for dst, src in ((wq_f, wqt), (wk_f, wkt), (wv_f, wvt)):
            C.inc(eng.dma_start(
                out=dst.rearrange("p (k d) -> p k d", d=D),
                in_=src.rearrange("(k p) d -> p k d", p=128)), "dS", 16)
        C.inc(eng.dma_start(out=wo_f[:, :], in_=wot), "dS", 16)
        C.inc(eng.dma_start(out=onesf_sb[:, :], in_=onesf), "dS", 16)
        C.inc(eng.dma_start(out=onesb_sb[:, :], in_=onesb), "dS", 16)
        C.inc(eng.dma_start(out=identb_sb[:, :], in_=identb), "dS", 16)
        C.inc(eng.dma_start(out=identf_sb[:, :], in_=identf), "dS", 16)
        C.ck("consts", "dS")
        wg_r = ag_out["g"].ap().rearrange("(k p) c -> p k c", p=128)
        for i in range(IC):
            C.wait(eng, "AGwg")
            if i >= 2:
                C.wait(eng, f"Amm{i - 2}")
            C.inc(eng.dma_start(
                out=wstr_sl[i % 2].rearrange("p (k c) -> p k c", c=128),
                in_=wg_r[:, :, 128 * i:128 * (i + 1)]), "dS", 16)
            C.ck(f"wA{i}", "dS")
        wu_r = ag_out["u"].ap().rearrange("(k p) c -> p k c", p=128)
        for i in range(IC):
            C.wait(eng, "AGwu")
            C.wait(eng, f"Bmm{i - 2}" if i >= 2 else "Amm23")
            C.inc(eng.dma_start(
                out=wstr_sl[i % 2].rearrange("p (k c) -> p k c", c=128),
                in_=wu_r[:, :, 128 * i:128 * (i + 1)]), "dS", 16)
            C.ck(f"wB{i}", "dS")
        wd_r = ag_out["d"].ap().rearrange("(i p) c -> p i c", p=128)
        for e in range(EC):
            C.wait(eng, "AGwd")
            if e >= 2:
                C.wait(eng, f"Dmm{e - 2}")
            C.inc(eng.dma_start(
                out=wstrD_sl[e % 2].rearrange("p (i c) -> p i c", c=128),
                in_=wd_r[:, :, 128 * e:128 * (e + 1)]), "dS", 16)
            C.ck(f"wD{e}", "dS")
            if e >= 2:
                C.wait(eng, f"oev{e - 2}")
                C.inc(eng.dma_start(out=outT[128 * (e - 2):128 * (e - 1), :],
                                    in_=outstage[e % 2]), "dS", 16)
                C.ck(f"out{e - 2}", "dS")
        for e in (6, 7):
            C.wait(eng, f"oev{e}")
            C.inc(eng.dma_start(out=outT[128 * e:128 * (e + 1), :],
                                in_=outstage[e % 2]), "dS", 16)
            C.ck(f"out{e}", "dS")
        C.wait(eng, "out7")

    def emit_gpsimd(eng):
        for k in range(EC):
            C.wait(eng, f"x{k}")
            C.inc(eng.dma_start(out=xn_ch[k], in_=x_ch[k]), "dG", 16)
            C.ck(f"xb{k}", "dG")
        C.wait(eng, "consts")
        C.inc(eng.dma_start(out=wq_b[:, :], in_=wq_f[:, :]), "dG", 16)
        C.inc(eng.dma_start(out=wk_b[:, :], in_=wk_f[:, :]), "dG", 16)
        C.inc(eng.dma_start(out=wv_b[:, :], in_=wv_f[:, :]), "dG", 16)
        C.inc(eng.dma_start(out=wo_b[:, :], in_=wo_f[:, :]), "dG", 16)
        C.ck("wcast", "dG")
        prev = None
        for wname in ("g", "u", "d"):
            if prev is not None:
                C.wait(eng, f"dqstage{prev}")
            if wname == "d":
                nat_in = wd_nat.rearrange("(t p) c -> p t c", p=128)
                nat_out = dq_nat.rearrange("p (t c) -> p t c", c=384)
            else:
                src = wg_nat if wname == "g" else wu_nat
                nat_in = src.rearrange("(t p) c -> p t c", p=128)
                nat_out = dq_nat.rearrange("p (t c) -> p t c", c=128)
            C.inc(eng.dma_start(out=nat_out, in_=nat_in), "dG", 16)
            C.ck(f"dqnat{wname}", "dG")
            C.wait(eng, f"dqT{wname}")
            if wname == "d":
                dTr = dq_T.rearrange("p (t j c) -> p j t c", j=3, c=128)
                for j in range(3):
                    C.inc(eng.dma_start(
                        out=ag_in["d"].ap()[128 * j:128 * (j + 1), :]
                            .rearrange("p (t c) -> p t c", c=128),
                        in_=dTr[:, j]), "dG", 16)
            else:
                C.inc(eng.dma_start(out=ag_in[wname][:, :], in_=dq_T), "dG", 16)
            C.ck(f"dqstage{wname}", "dG")
            if wname == "g":
                C.wait_val(eng, "dG", C.cnt["dG"])
                C.inc(eng.collective_compute(
                    "AllGather", OP.bypass, replica_groups=RG,
                    ins=[ag_in["g"].ap().opt()], outs=[ag_out["g"].ap().opt()]), "CC", 1)
                C.ck("AGwg", "CC")
            prev = wname

        C.wait(eng, "sxtrr")
        C.inc(eng.dma_start(out=ar_in[0][0:1, 0:128], in_=trrow[0:1, 0:128]), "dG", 16)
        C.wait_val(eng, "dG", C.cnt["dG"])
        C.inc(eng.collective_compute(
            "AllReduce", OP.max, replica_groups=RG,
            ins=[ar_in[0].ap().opt()], outs=[ar_out[0].ap().opt()]), "CC", 1)
        C.wait_val(eng, "CC", C.cnt["CC"])
        C.inc(eng.dma_start(out=arbrow[0:1, 0:128], in_=ar_out[0][0:1, 0:128]), "dG", 16)
        C.ck("sxback", "dG")
        if debug:
            C.wait(eng, "sxcols")
            C.inc(eng.dma_start(out=dbg["dbg_arout"], in_=ar_out[0][:, :]), "dG", 16)
            C.inc(eng.dma_start(out=dbg["dbg_bc"], in_=astage_sl[1][:, 0:128]), "dG", 16)
            C.inc(eng.dma_start(out=dbg["dbg_arbrow"], in_=arbrow[0:1, 0:128]), "dG", 16)
            C.inc(eng.dma_start(out=dbg["dbg_sx"], in_=misc[0:1, 0:64]), "dG", 16)
            C.inc(eng.dma_start(out=dbg["dbg_xn0"], in_=xn_ch[0]), "dG", 16)
            C.inc(eng.dma_start(out=dbg["dbg_trrow"], in_=trrow[:, 0:128]), "dG", 16)
            C.inc(eng.dma_start(out=dbg["dbg_colsx"], in_=cols[:, 0:32]), "dG", 16)
            C.ck("dbgsx", "dG")

        C.wait(eng, "eloc")
        C.inc(eng.dma_start(out=ag_e_in[:, :], in_=e_loc), "dG", 16)
        C.wait_val(eng, "dG", C.cnt["dG"])
        C.inc(eng.collective_compute(
            "AllReduce", OP.add, replica_groups=RG,
            ins=[ag_e_in.ap().opt()], outs=[ag_e_out.ap().opt()]), "CC", 1)
        C.wait_val(eng, "CC", C.cnt["CC"])
        C.inc(eng.dma_start(out=e_sum[:, :], in_=ag_e_out[:, :]), "dG", 16)
        C.ck("eback", "dG")
        C.wait_val(eng, "dG", C.cnt["dG"])
        C.inc(eng.collective_compute(
            "AllGather", OP.bypass, replica_groups=RG,
            ins=[ag_in["u"].ap().opt()], outs=[ag_out["u"].ap().opt()]), "CC", 1)
        C.ck("AGwu", "CC")
        C.inc(eng.collective_compute(
            "AllGather", OP.bypass, replica_groups=RG,
            ins=[ag_in["d"].ap().opt()], outs=[ag_out["d"].ap().opt()]), "CC", 1)
        C.ck("AGwd", "CC")

        C.wait(eng, "sgtrr")
        C.inc(eng.dma_start(out=ar_in[1][0:1, 0:128], in_=trrow[0:1, 0:128]), "dG", 16)
        C.wait_val(eng, "dG", C.cnt["dG"])
        if debug:
            C.inc(eng.dma_start(out=dbg["dbg_arin1"], in_=ar_in[1][0:1, 0:128]), "dG", 16)
            C.wait_val(eng, "dG", C.cnt["dG"])
        C.inc(eng.collective_compute(
            "AllReduce", OP.max, replica_groups=RG,
            ins=[ar_in[1].ap().opt()], outs=[ar_out[1].ap().opt()]), "CC", 1)
        C.wait_val(eng, "CC", C.cnt["CC"])
        C.inc(eng.dma_start(out=arbrow[0:1, 0:128], in_=ar_out[1][0:1, 0:128]), "dG", 16)
        if debug:
            C.inc(eng.dma_start(out=dbg["dbg_arout1"], in_=ar_out[1][0:1, 0:128]), "dG", 16)
        C.ck("sgback", "dG")

        C.wait(eng, "sutrr")
        C.inc(eng.dma_start(out=ar_in[2][0:1, 0:256], in_=trrow[0:1, 0:256]), "dG", 16)
        C.wait_val(eng, "dG", C.cnt["dG"])
        C.inc(eng.collective_compute(
            "AllReduce", OP.max, replica_groups=RG,
            ins=[ar_in[2].ap().opt()], outs=[ar_out[2].ap().opt()]), "CC", 1)
        C.wait_val(eng, "CC", C.cnt["CC"])
        C.inc(eng.dma_start(out=arbrow[0:1, 0:256], in_=ar_out[2][0:1, 0:256]), "dG", 16)
        C.ck("suback", "dG")

        C.wait(eng, "sucols")
        C.wait(eng, "upsp23")
        C.inc(eng.dma_start(out=upst_sl[1], in_=up_dram[128:256, :]), "dG", 16)
        C.ck("upldG1", "dG")
        for j in range(3, IC, 2):
            C.wait(eng, f"C2t6_{j - 2}")
            C.inc(eng.dma_start(out=upst_sl[1],
                                in_=up_dram[128 * j:128 * (j + 1), :]), "dG", 16)
            C.ck(f"upldG{j}", "dG")

        C.wait(eng, "sotrr")
        C.inc(eng.dma_start(out=ar_in[3][0:1, 0:128], in_=trrow[0:1, 0:128]), "dG", 16)
        C.wait_val(eng, "dG", C.cnt["dG"])
        C.inc(eng.collective_compute(
            "AllReduce", OP.max, replica_groups=RG,
            ins=[ar_in[3].ap().opt()], outs=[ar_out[3].ap().opt()]), "CC", 1)
        C.wait_val(eng, "CC", C.cnt["CC"])
        C.inc(eng.dma_start(out=arbrow[0:1, 0:128], in_=ar_out[3][0:1, 0:128]), "dG", 16)
        C.ck("soback", "dG")

        for e in range(EC):
            C.wait(eng, f"Dev1_{e}")
            C.wait(eng, f"apref{e}")
            for nb in range(NB):
                ti = eng.tensor_tensor(
                    outstage[e % 2][:, 512 * nb:512 * (nb + 1)],
                    outstage[e % 2][:, 512 * nb:512 * (nb + 1)],
                    astage_sl[e % 2][:, 512 * nb:512 * (nb + 1)], OP.add)
            C.inc(ti, "gE", 1)
            C.ck(f"oev{e}", "gE")
        C.wait(eng, "out7")
        if debug:
            C.inc(eng.dma_start(out=dbg["dbg_cols"], in_=cols[:, 0:32]), "dG", 16)
            C.inc(eng.dma_start(out=dbg["dbg_misc"], in_=misc), "dG", 16)
            C.inc(eng.dma_start(out=dbg["dbg_esum"], in_=e_sum[:, :]), "dG", 16)
            C.inc(eng.dma_start(out=dbg["dbg_o0"], in_=gt[0]), "dG", 16)
            C.inc(eng.dma_start(out=dbg["dbg_up0"], in_=up_dram[0:128, :]), "dG", 16)
            C.inc(eng.dma_start(out=dbg["dbg_attn0"], in_=attn_dram[0:128, :]), "dG", 16)
            C.inc(eng.dma_start(out=dbg["dbg_kb"], in_=Kb), "dG", 16)
            C.inc(eng.dma_start(out=dbg["dbg_vb"], in_=Vb), "dG", 16)
            C.inc(eng.dma_start(out=dbg["dbg_qt"], in_=QTb), "dG", 16)
            C.inc(eng.dma_start(out=dbg["dbg_ctx"], in_=ctxT), "dG", 16)
            C.inc(eng.dma_start(out=dbg["dbg_xq0"], in_=xn_ch[0]), "dG", 16)
            C.inc(eng.dma_start(out=dbg["dbg_dqT"], in_=ag_out["d"].ap()[0:128, :]), "dG", 16)
            C.wait_val(eng, "dG", C.cnt["dG"])

    def emit_tensor(eng):
        C.wait(eng, "consts")
        for wi, wname in enumerate(("g", "u", "d")):
            C.wait(eng, f"dq{wname}bf")
            for b in range(6):
                gi = wi * 6 + b
                if gi >= 2:
                    C.wait(eng, f"dqev{gi - 2}")
                for j in range(4):
                    t = b * 4 + j
                    ti = eng.transpose(ps[gi % 2][:, 128 * j:128 * (j + 1)],
                                       dq_nat[:, 128 * t:128 * (t + 1)],
                                       identf_sb[:, :])
                C.inc(ti, "tE", 1)
                C.ck(f"dqtr{gi}", "tE")
        C.wait(eng, "dqev17")
        for k in range(EC):
            C.wait(eng, f"xb{k}")
            C.wait(eng, f"x2_{k}")
            for nb in range(NB):
                eng.matmul(ps[nb][0:1, :], onesb_sb[:, 0:1],
                           xn_ch[k][:, 512 * nb:512 * (nb + 1)],
                           start=(k == 0), stop=(k == EC - 1))
            for nb in range(NB):
                ti = eng.matmul(ps[4 + nb][0:1, :], onesb_sb[:, 0:1],
                                x2b_sl[k % 2][:, 512 * nb:512 * (nb + 1)],
                                start=(k == 0), stop=(k == EC - 1))
            C.inc(ti, "tE", 1)
            C.ck(f"sums{k}", "tE")
        for nb in range(NB):
            C.wait(eng, f"rqrow{nb}")
            eng.matmul(ps[nb][:, :], onesf_sb[0:1, :], rowa, start=True, stop=True)
            ti = eng.matmul(ps[4 + nb][:, :], onesf_sb[0:1, :], rowb,
                            start=True, stop=True)
            C.inc(ti, "tE", 1)
            C.ck(f"rqmm{nb}", "tE")
        C.wait(eng, "xn7")
        C.wait(eng, "rqev")
        C.wait(eng, "wcast")
        for nb in range(NB):
            for k in range(EC):
                ti = eng.matmul(ps[nb][0:64, :], wq_b[:, 64 * k:64 * (k + 1)],
                                xn_ch[k][:, 512 * nb:512 * (nb + 1)],
                                start=(k == 0), stop=(k == EC - 1))
            C.inc(ti, "tE", 1)
            C.ck(f"qmm{nb}", "tE")
        for c in range(TC):
            if c >= 2:
                C.wait(eng, f"kvev{c - 2}")
            bk, bv = 4 + (c % 2) * 2, 5 + (c % 2) * 2
            for k in range(EC):
                eng.matmul(ps[bk][:, 0:64], xn_ch[k][:, 128 * c:128 * (c + 1)],
                           wk_b[:, 64 * k:64 * (k + 1)],
                           start=(k == 0), stop=(k == EC - 1))
                ti = eng.matmul(ps[bv][:, 0:64], xn_ch[k][:, 128 * c:128 * (c + 1)],
                                wv_b[:, 64 * k:64 * (k + 1)],
                                start=(k == 0), stop=(k == EC - 1))
            C.inc(ti, "tE", 1)
            C.ck(f"kvmm{c}", "tE")
        C.wait(eng, "qev")
        C.wait(eng, "sxcol")
        ti = eng.transpose(ps[0][0:8, 0:128], trsrc[:, :], identf_sb[:, :])
        C.inc(ti, "tE", 1)
        C.ck("sxtr", "tE")
        C.wait(eng, "sxback")
        ti = eng.matmul(ps[0][:, 0:128], onesf_sb[0:1, :], arbrow[0:1, 0:128],
                        start=True, stop=True)
        C.inc(ti, "tE", 1)
        C.ck("sxbc", "tE")
        C.wait(eng, "kvev15")
        for c in range(TC):
            ti = eng.matmul(ps[1][0:64, 0:64], Kb[:, 64 * c:64 * (c + 1)],
                            Vb[:, 64 * c:64 * (c + 1)],
                            start=(c == 0), stop=(c == TC - 1))
        C.inc(ti, "tE", 1)
        C.ck("emm", "tE")
        for i in range(IC):
            grp = (i % 2) * 4
            C.wait(eng, f"wA{i}")
            if i >= 2:
                C.wait(eng, f"actA{i - 2}")
            for k in range(EC):
                if i < 3:
                    C.wait(eng, f"xq{k}")
                for nb in range(NB):
                    ti = eng.matmul(ps[grp + nb][:, :],
                                    wstr_sl[i % 2][:, 128 * k:128 * (k + 1)],
                                    xq_ch[k][:, 512 * nb:512 * (nb + 1)],
                                    start=(k == 0), stop=(k == EC - 1))
            C.inc(ti, "tE", 1)
            C.ck(f"Amm{i}", "tE")
        C.wait(eng, "actA22")
        C.wait(eng, "gmaxcol")
        ti = eng.transpose(ps[0][0:8, 0:128], trsrc[:, :], identf_sb[:, :])
        C.inc(ti, "tE", 1)
        C.ck("sgtr", "tE")
        for i in range(IC):
            grp = (i % 2) * 4
            if i == 3:
                C.wait(eng, "sgback")
                C.wait(eng, "actB2")
                ti = eng.matmul(ps[0][:, 0:128], onesf_sb[0:1, :],
                                arbrow[0:1, 0:128], start=True, stop=True)
                C.inc(ti, "tE", 1)
                C.ck("sgbc", "tE")
            C.wait(eng, f"wB{i}")
            if i == 4:
                C.wait(eng, "sgcols")
            if i >= 2:
                C.wait(eng, f"actB{i - 2}")
            elif i == 0:
                C.wait(eng, "sgtrr")
            else:
                C.wait(eng, "actA23")
            for k in range(EC):
                for nb in range(NB):
                    ti = eng.matmul(ps[grp + nb][:, :],
                                    wstr_sl[i % 2][:, 128 * k:128 * (k + 1)],
                                    xq_ch[k][:, 512 * nb:512 * (nb + 1)],
                                    start=(k == 0), stop=(k == EC - 1))
            C.inc(ti, "tE", 1)
            C.ck(f"Bmm{i}", "tE")
        C.wait(eng, "umaxcol")
        C.wait(eng, "mmaxcol")
        C.wait(eng, "actB22")
        eng.transpose(ps[0][0:8, 0:128], trsrc[:, :], identf_sb[:, :])
        ti = eng.transpose(ps[1][0:8, 0:128], trsrc2[:, :], identf_sb[:, :])
        C.inc(ti, "tE", 1)
        C.ck("sutr", "tE")
        C.wait(eng, "ebf")
        C.wait(eng, "sxcols")
        C.wait(eng, "eev")
        C.wait(eng, "actB23")
        for nb in range(NB):
            ti = eng.matmul(ps[2 + nb][0:64, :], e_bf[:, :],
                            QTb[:, 512 * nb:512 * (nb + 1)], start=True, stop=True)
        C.inc(ti, "tE", 1)
        C.ck("cmmall", "tE")
        for e in range(EC):
            grp = 4 if e % 2 else 0
            C.wait(eng, f"aev{e - 2}" if e >= 2 else "cev")
            if e == 0:
                C.wait(eng, "sutrr")
            for nb in range(NB):
                ti = eng.matmul(ps[grp + nb][:, :], wo_b[:, 128 * e:128 * (e + 1)],
                                ctxT[:, 512 * nb:512 * (nb + 1)], start=True, stop=True)
            C.inc(ti, "tE", 1)
            C.ck(f"amm{e}", "tE")
        C.wait(eng, "suback")
        C.wait(eng, "aev6")
        C.wait(eng, "aev7")
        eng.matmul(ps[2][:, 0:128], onesf_sb[0:1, :], arbrow[0:1, 0:128],
                   start=True, stop=True)
        ti = eng.matmul(ps[3][:, 0:128], onesf_sb[0:1, :], arbrow[0:1, 128:256],
                        start=True, stop=True)
        C.inc(ti, "tE", 1)
        C.ck("subc", "tE")
        C.wait(eng, "sucols")
        C.wait(eng, "rrmaxcol")
        ti = eng.transpose(ps[0][0:8, 0:128], trsrc[:, :], identf_sb[:, :])
        C.inc(ti, "tE", 1)
        C.ck("sotr", "tE")
        C.wait(eng, "soback")
        ti = eng.matmul(ps[2][:, 0:128], onesf_sb[0:1, :], arbrow[0:1, 0:128],
                        start=True, stop=True)
        C.inc(ti, "tE", 1)
        C.ck("sobc", "tE")
        for e in range(EC):
            grp = (e % 2) * 4
            C.wait(eng, f"wD{e}")
            if e >= 2:
                C.wait(eng, f"Dev1_{e - 2}")
            else:
                C.wait(eng, f"actB{22 + e}")
            for i in range(IC):
                if e == 0:
                    C.wait(eng, f"o{i}")
                for nb in range(NB):
                    ti = eng.matmul(ps[grp + nb][:, :],
                                    wstrD_sl[e % 2][:, 128 * i:128 * (i + 1)],
                                    gt[i][:, 512 * nb:512 * (nb + 1)],
                                    start=(i == 0), stop=(i == IC - 1))
            C.inc(ti, "tE", 1)
            C.ck(f"Dmm{e}", "tE")

    def emit_vector(eng):
        eng.memset(eps_sb, LNEPS)
        eng.memset(Mcol, MAGIC)
        ti = eng.memset(negMcol, -MAGIC)
        C.inc(ti, "vE", 1)
        C.ck("memset", "vE")
        prev = None
        for wname in ("g", "u", "d"):
            C.wait(eng, f"dqnat{wname}")
            if prev is not None:
                C.wait(eng, f"dqstage{prev}")
            nat3 = dq_nat.rearrange("p (b c) -> p b c", c=32)
            eng.tensor_reduce(dq_bmax, nat3, AX.X, OP.max, apply_absolute_value=True)
            eng.tensor_scalar(dq_s, dq_bmax, 1.0 / QW, SEPS, OP.mult, OP.max)
            eng.reciprocal(dq_rs, dq_s)
            eng.tensor_tensor(nat3, nat3,
                              dq_rs[:, :, None].broadcast_to([128, 96, 32]), OP.mult)
            eng.tensor_scalar(dq_nat, dq_nat, MAGIC, MAGIC, OP.add, OP.subtract)
            ti = eng.tensor_tensor(nat3, nat3,
                                   dq_s[:, :, None].broadcast_to([128, 96, 32]), OP.mult)
            C.inc(ti, "vE", 1)
            C.ck(f"dq{wname}bf", "vE")
            prev = wname

        C.wait(eng, f"sums{EC - 1}")
        for nb in range(NB):
            if nb >= 1:
                C.wait(eng, f"rqmm{nb - 1}")
            eng.tensor_copy(sum_r, ps[nb][0:1, :])
            eng.tensor_copy(ssq_r, ps[4 + nb][0:1, :])
            eng.tensor_scalar(rowa, sum_r, 1.0 / E, None, OP.mult, OP.bypass)
            eng.tensor_tensor(rowb, rowa, rowa, OP.mult)
            eng.tensor_scalar(rowa, ssq_r, 1.0 / E, None, OP.mult, OP.bypass)
            ti = eng.tensor_tensor(rowb, rowa, rowb, OP.subtract)
            C.inc(ti, "vE", 1)
            C.ck(f"var{nb}", "vE")
            C.wait(eng, f"sqrt{nb}")
            eng.reciprocal(rowa, rowb)
            ti = eng.scalar_tensor_tensor(rowb, sum_r, 1.0 / E, rowa,
                                          OP.mult, OP.mult)
            C.inc(ti, "vE", 1)
            C.ck(f"rqrow{nb}", "vE")
        for nb in range(NB):
            C.wait(eng, f"rqmm{nb}")
            eng.tensor_copy(Rb[:, 512 * nb:512 * (nb + 1)], ps[nb][:, :])
            ti = eng.tensor_copy(Qb[:, 512 * nb:512 * (nb + 1)], ps[4 + nb][:, :])
        C.inc(ti, "vE", 1)
        C.ck("rqev", "vE")
        for k in range(EC):
            eng.tensor_tensor(xtmp, x_ch[k], Rb, OP.mult)
            ti = eng.tensor_tensor(xn_ch[k], xtmp, Qb, OP.subtract)
            C.inc(ti, "vE", 1)
            C.ck(f"xn{k}", "vE")
            eng.tensor_reduce(mctmp, xn_ch[k], AX.X, OP.max, apply_absolute_value=True)
            ti = (eng.tensor_tensor(maxcol, maxcol, mctmp, OP.max) if k else
                  eng.tensor_copy(maxcol, mctmp))
            C.inc(ti, "vE", 1)
        ti = eng.tensor_copy(trsrc[:, 0:1], maxcol)
        C.inc(ti, "vE", 1)
        C.ck("sxcol", "vE")
        C.wait(eng, "sxtr")
        ti = eng.tensor_copy(trrow[:, 0:128], ps[0][0:8, 0:128])
        C.inc(ti, "vE", 1)
        C.ck("sxtrr", "vE")
        C.wait(eng, "sxbc")
        eng.tensor_reduce(rawc, ps[0][:, 0:128], AX.X, OP.max)
        if debug:
            eng.tensor_copy(astage_sl[1][:, 0:128], ps[0][:, 0:128])
        eng.tensor_scalar(sx_col, rawc, 1.0 / QA, SEPS, OP.mult, OP.max)
        ti = recip_col(eng, rsx_col, sx_col)
        C.inc(ti, "vE", 1)
        C.ck("sxcols", "vE")
        C.wait(eng, "kvmm15")
        if debug:
            C.wait(eng, "dbgsx")
        for k in range(EC):
            eng.tensor_scalar(SCt, xn_ch[k], rsx_col, MAGIC, OP.mult, OP.add)
            ti = eng.tensor_scalar(xq_ch[k], SCt, MAGIC, sx_col, OP.subtract, OP.mult)
            C.inc(ti, "vE", 1)
            C.ck(f"xq{k}", "vE")
        C.wait(eng, "eback")
        ti = eng.tensor_scalar(e_bf[:, :], e_sum[:, :], C_ATTN, None, OP.mult, OP.bypass)
        C.inc(ti, "vE", 1)
        C.ck("ebf", "vE")
        for i in range(IC):
            C.wait(eng, f"actA{i}")
            eng.tensor_reduce(mctmp, gt[i], AX.X, OP.max, apply_absolute_value=True)
            ti = (eng.tensor_tensor(gmaxcol, gmaxcol, mctmp, OP.max) if i else
                  eng.tensor_copy(gmaxcol, mctmp))
            C.inc(ti, "vE", 1)
        ti = eng.tensor_copy(trsrc[:, 0:1], gmaxcol)
        C.inc(ti, "vE", 1)
        C.ck("gmaxcol", "vE")
        C.wait(eng, "sgtr")
        ti = eng.tensor_copy(trrow[:, 0:128], ps[0][0:8, 0:128])
        C.inc(ti, "vE", 1)
        C.ck("sgtrr", "vE")
        def vec_c1(j):
            C.wait(eng, f"c1r{j}")
            tb = SCt if j % 2 == 0 else SCv
            ub = SCu if j % 2 == 0 else astageF
            eng.tensor_scalar(ub, ub, MAGIC + SIGMAX, MAGIC, OP.min, OP.subtract)
            ti = eng.scalar_tensor_tensor(gt[j], tb, -MAGIC, ub, OP.add, OP.mult)
            C.inc(ti, "vE", 1)
            C.ck(f"C1m{j}", "vE")
            eng.tensor_reduce(mctmp, gt[j], AX.X, OP.max, apply_absolute_value=True)
            ti = (eng.tensor_tensor(mmaxcol, mmaxcol, mctmp, OP.max) if j else
                  eng.tensor_copy(mmaxcol, mctmp))
            C.inc(ti, "vE", 1)

        for i in range(IC):
            if i == 2:
                C.wait(eng, "sgbc")
                eng.tensor_reduce(rawc, ps[0][:, 0:128], AX.X, OP.max)
                eng.tensor_scalar(sg_col, rawc, 1.0 / QA, SEPS, OP.mult, OP.max)
                recip_col(eng, rsg_col, sg_col)
                eng.tensor_scalar(sg65_col, sg_col, 1.0 / SIG, None,
                                  OP.mult, OP.bypass)
                ti = eng.tensor_scalar(msg_col, sg_col, -MAGIC, None,
                                       OP.mult, OP.bypass)
                C.inc(ti, "vE", 1)
                C.ck("sgcols", "vE")
            C.wait(eng, f"actB{i}")
            eng.tensor_reduce(mctmp, upst_sl[i % 2], AX.X, OP.max,
                              apply_absolute_value=True)
            ti = (eng.tensor_tensor(umaxcol, umaxcol, mctmp, OP.max) if i else
                  eng.tensor_copy(umaxcol, mctmp))
            C.inc(ti, "vE", 1)
            C.ck(f"umax{i}", "vE")
            if i >= 2:
                vec_c1(i - 2)
        for j in range(IC - 2, IC):
            vec_c1(j)
        eng.tensor_copy(trsrc[:, 0:1], umaxcol)
        ti = eng.tensor_copy(trsrc2[:, 0:1], mmaxcol)
        C.inc(ti, "vE", 1)
        C.ck("umaxcol", "vE")
        C.ck("mmaxcol", "vE")
        C.wait(eng, "sutr")
        eng.tensor_copy(trrow[:, 0:128], ps[0][0:8, 0:128])
        ti = eng.tensor_copy(trrow[:, 128:256], ps[1][0:8, 0:128])
        C.inc(ti, "vE", 1)
        C.ck("sutrr", "vE")
        for e in range(1, EC, 2):
            C.wait(eng, f"amm{e}")
            if e >= 3:
                C.wait(eng, f"asp{e - 2}")
            for nb in range(NB):
                ti = eng.tensor_copy(astage_sl[1][:, 512 * nb:512 * (nb + 1)],
                                     ps[4 + nb][:, :])
            C.inc(ti, "vE", 1)
            C.ck(f"aev{e}", "vE")
        C.wait(eng, "subc")
        eng.tensor_reduce(rawc, ps[2][:, 0:128], AX.X, OP.max)
        eng.tensor_reduce(raw2c, ps[3][:, 0:128], AX.X, OP.max)
        eng.tensor_scalar(sup_col, rawc, 1.0 / QA, SEPS, OP.mult, OP.max)
        recip_col(eng, rsu_col, sup_col)
        eng.tensor_tensor(tcol, raw2c, sg65_col, OP.mult)
        eng.tensor_scalar(sg2_col, tcol, 1.0 / QA, SEPS, OP.mult, OP.max)
        recip_col(eng, tcol, sg2_col)
        ti = eng.tensor_tensor(cg_col, sg65_col, tcol, OP.mult)
        C.inc(ti, "vE", 1)
        C.ck("sucols", "vE")
        for i in range(IC):
            C.wait(eng, f"C2t6_{i}")
            buf = SCu if i % 2 == 0 else SCv
            t5b = SCt if i % 2 == 0 else xsc_sl[0]
            eng.tensor_scalar(buf, buf, MAGIC, None, OP.subtract, OP.bypass)
            C.wait(eng, f"C2t5_{i}")
            ti = eng.scalar_tensor_tensor(gt[i], t5b, -MAGIC, buf, OP.add, OP.mult)
            C.inc(ti, "vE", 1)
            C.ck(f"C2rr{i}", "vE")
            eng.tensor_reduce(mctmp, gt[i], AX.X, OP.max, apply_absolute_value=True)
            ti = (eng.tensor_tensor(rrmaxcol, rrmaxcol, mctmp, OP.max) if i else
                  eng.tensor_copy(rrmaxcol, mctmp))
            C.inc(ti, "vE", 1)
        ti = eng.tensor_copy(trsrc[:, 0:1], rrmaxcol)
        C.inc(ti, "vE", 1)
        C.ck("rrmaxcol", "vE")
        C.wait(eng, "sotr")
        ti = eng.tensor_copy(trrow[:, 0:128], ps[0][0:8, 0:128])
        C.inc(ti, "vE", 1)
        C.ck("sotrr", "vE")
        C.wait(eng, "sobc")
        eng.tensor_reduce(rawc, ps[2][:, 0:128], AX.X, OP.max)
        eng.tensor_tensor(pcol, sg2_col, sup_col, OP.mult)
        eng.tensor_tensor(tcol, rawc, pcol, OP.mult)
        eng.tensor_scalar(so_col, tcol, 1.0 / QA, SEPS, OP.mult, OP.max)
        recip_col(eng, tcol, so_col)
        ti = eng.tensor_tensor(cr_col, pcol, tcol, OP.mult)
        C.inc(ti, "vE", 1)
        C.ck("socols", "vE")
        C.ck("somax", "vE")
        for i in range(IC):
            if i % 2 == 0:
                C.wait(eng, f"ot7_{i}")
                t7b = xsc_sl[(i % 4) // 2]
                ti = eng.tensor_scalar(gt[i], t7b, MAGIC, so_col,
                                       OP.subtract, OP.mult)
            else:
                eng.tensor_scalar(SCt, gt[i], cr_col, MAGIC, OP.mult, OP.add)
                ti = eng.tensor_scalar(gt[i], SCt, MAGIC, so_col,
                                       OP.subtract, OP.mult)
            C.inc(ti, "vE", 1)
            C.ck(f"o{i}", "vE")
        for e in range(EC):
            C.wait(eng, f"Dmm{e}")
            C.wait(eng, f"xpref{e}")
            if e >= 2:
                C.wait(eng, f"out{e - 2}")
            grp = (e % 2) * 4
            for nb in range(NB):
                ti = eng.tensor_tensor(outstage[e % 2][:, 512 * nb:512 * (nb + 1)],
                                       ps[grp + nb][:, :],
                                       xpref_sl[e % 2][:, 512 * nb:512 * (nb + 1)],
                                       OP.add)
            C.inc(ti, "vE", 1)
            C.ck(f"Dev1_{e}", "vE")

    def emit_scalar(eng):
        for wi, wname in enumerate(("g", "u", "d")):
            for b in range(6):
                gi = wi * 6 + b
                C.wait(eng, f"dqtr{gi}")
                ti = eng.copy(dq_T[:, 512 * b:512 * (b + 1)], ps[gi % 2][:, :])
                C.inc(ti, "sE", 1)
                C.ck(f"dqev{gi}", "sE")
            C.ck(f"dqT{wname}", "sE")
        for k in range(EC):
            C.wait(eng, f"x{k}")
            if k >= 2:
                C.wait(eng, f"sums{k - 2}")
            ti = eng.activation(x2b_sl[k % 2], x_ch[k], AF.Square)
            C.inc(ti, "sE", 1)
            C.ck(f"x2_{k}", "sE")
        C.wait(eng, "memset")
        for nb in range(NB):
            C.wait(eng, f"var{nb}")
            ti = eng.activation(rowb, rowb, AF.Sqrt, bias=eps_sb, scale=1.0)
            C.inc(ti, "sE", 1)
            C.ck(f"sqrt{nb}", "sE")
        for nb in range(NB):
            C.wait(eng, f"qmm{nb}")
            ti = eng.copy(QTb[:, 512 * nb:512 * (nb + 1)], ps[nb][0:64, :])
        C.inc(ti, "sE", 1)
        C.ck("qev", "sE")
        for c in range(TC):
            C.wait(eng, f"kvmm{c}")
            bk, bv = 4 + (c % 2) * 2, 5 + (c % 2) * 2
            eng.copy(Kb[:, 64 * c:64 * (c + 1)], ps[bk][:, 0:64])
            ti = eng.copy(Vb[:, 64 * c:64 * (c + 1)], ps[bv][:, 0:64])
            C.inc(ti, "sE", 1)
            C.ck(f"kvev{c}", "sE")
        C.wait(eng, "emm")
        ti = eng.copy(e_loc[:, :], ps[1][0:64, 0:64])
        C.inc(ti, "sE", 1)
        C.ck("eloc", "sE")
        C.ck("eev", "sE")
        for i in range(IC):
            C.wait(eng, f"Amm{i}")
            grp = (i % 2) * 4
            for nb in range(NB):
                ti = eng.copy(gt[i][:, 512 * nb:512 * (nb + 1)], ps[grp + nb][:, :])
            C.inc(ti, "sE", 1)
            C.ck(f"actA{i}", "sE")
        spill0 = C.cnt["dA"]

        def act_c1(j):
            if j == 0:
                C.wait(eng, "sgcols")
            if j >= 2:
                C.wait(eng, f"C1m{j - 2}")
            tb = SCt if j % 2 == 0 else SCv
            ub = SCu if j % 2 == 0 else astageF
            ti = eng.activation(tb, gt[j], AF.Identity, bias=Mcol, scale=rsg_col)
            C.inc(ti, "sE", 1)
            C.ck(f"C1t{j}", "sE")
            ti = eng.activation(ub, tb, AF.Sigmoid, bias=msg_col, scale=sg_col)
            C.inc(ti, "sE", 1)
            C.ck(f"sig{j}", "sE")
            ti = eng.activation(ub, ub, AF.Identity, bias=Mcol, scale=float(SIG))
            C.inc(ti, "sE", 1)
            C.ck(f"c1r{j}", "sE")

        for i in range(IC):
            C.wait(eng, f"Bmm{i}")
            if i >= 2:
                C.wait_val(eng, "dA", spill0 + 16 * (i - 1))
                C.wait(eng, f"umax{i - 2}")
            grp = (i % 2) * 4
            for nb in range(NB):
                ti = eng.copy(upst_sl[i % 2][:, 512 * nb:512 * (nb + 1)],
                              ps[grp + nb][:, :])
            C.inc(ti, "sE", 1)
            C.ck(f"actB{i}", "sE")
            C.inc(eng.dma_start(out=up_dram[128 * i:128 * (i + 1), :],
                                in_=upst_sl[i % 2]), "dA", 16)
            C.ck(f"upsp{i}", "dA")
            if i >= 2:
                act_c1(i - 2)
        for j in range(IC - 2, IC):
            act_c1(j)
        C.wait(eng, "cmmall")
        for nb in range(NB):
            ti = eng.copy(ctxT[:, 512 * nb:512 * (nb + 1)], ps[2 + nb][0:64, :])
        C.inc(ti, "sE", 1)
        C.ck("cev", "sE")
        C.wait(eng, "C1m23")
        for e in range(EC):
            if e % 2 == 0:
                C.wait(eng, f"amm{e}")
                if e >= 2:
                    C.wait(eng, f"asp{e - 2}")
                for nb in range(NB):
                    ti = eng.copy(astage_sl[0][:, 512 * nb:512 * (nb + 1)],
                                  ps[nb][:, :])
                C.inc(ti, "sE", 1)
                C.ck(f"aev{e}", "sE")
            else:
                C.wait(eng, f"aev{e}")
            C.inc(eng.dma_start(out=attn_dram[128 * e:128 * (e + 1), :],
                                in_=astage_sl[e % 2]), "dA", 16)
            C.ck(f"asp{e}", "dA")
        C.wait(eng, "sucols")
        C.inc(eng.dma_start(out=upst_sl[0], in_=up_dram[0:128, :]), "dA", 16)
        C.ck("upldA0", "dA")
        for i in range(IC):
            if i >= 2:
                C.wait(eng, f"C2rr{i - 2}")
            t5b = SCt if i % 2 == 0 else xsc_sl[0]
            ti = eng.activation(t5b, gt[i], AF.Identity, bias=Mcol, scale=cg_col)
            C.inc(ti, "sE", 1)
            C.ck(f"C2t5_{i}", "sE")
            C.wait(eng, f"upldA{i}" if i % 2 == 0 else f"upldG{i}")
            buf = SCu if i % 2 == 0 else SCv
            ti = eng.activation(buf, upst_sl[i % 2], AF.Identity,
                                bias=Mcol, scale=rsu_col)
            C.inc(ti, "sE", 1)
            C.ck(f"C2t6_{i}", "sE")
            if i + 2 < IC and i % 2 == 0:
                C.inc(eng.dma_start(out=upst_sl[0],
                                    in_=up_dram[128 * (i + 2):128 * (i + 3), :]),
                      "dA", 16)
                C.ck(f"upldA{i + 2}", "dA")
        for i in range(0, IC, 2):
            if i == 0:
                C.wait(eng, "socols")
            if i >= 4:
                C.wait(eng, f"o{i - 4}")
            t7b = xsc_sl[(i % 4) // 2]
            ti = eng.activation(t7b, gt[i], AF.Identity, bias=Mcol, scale=cr_col)
            C.inc(ti, "sE", 1)
            C.ck(f"ot7_{i}", "sE")
        pf0 = C.cnt["dA"]
        for e in range(EC):
            if e == 0:
                C.wait(eng, "somax")
            if e >= 2:
                C.wait(eng, f"Dev1_{e - 2}")
                C.wait(eng, f"oev{e - 2}")
            C.inc(eng.dma_start(out=xpref_sl[e % 2],
                                in_=xT[128 * e:128 * (e + 1), :]), "dA", 16)
            C.inc(eng.dma_start(out=astage_sl[e % 2],
                                in_=attn_dram[128 * e:128 * (e + 1), :]), "dA", 16)
            C.wait_val(eng, "dA", pf0 + 32 * (e + 1))
            C.ck(f"xpref{e}", "dA")
            C.ck(f"apref{e}", "dA")

    C.reset(False)
    emit_sync(Mock())
    emit_vector(Chain(Mock(), C, "vE"))
    emit_scalar(Chain(Mock(), C, "sE"))
    emit_tensor(Mock())
    emit_gpsimd(Chain(Mock(), C, "gE"))

    C.reset(True)
    with nc.Block() as block:
        @block.sync
        def _(eng):
            emit_sync(eng)

        @block.vector
        def _(eng):
            emit_vector(Chain(eng, C, "vE"))

        @block.scalar
        def _(eng):
            emit_scalar(Chain(eng, C, "sE"))

        @block.tensor
        def _(eng):
            emit_tensor(eng)

        @block.gpsimd
        def _(eng):
            emit_gpsimd(Chain(eng, C, "gE"))

    st.close()
    return nc


_NC_CACHE = []


def kernel(L0_attn_x, L0_attn_wq, L0_attn_wk, L0_attn_wv, L0_attn_wo,
           L0_ffn_w_gate, L0_ffn_w_up, L0_ffn_w_down):
    x = np.ascontiguousarray(np.asarray(L0_attn_x, dtype=np.float32).reshape(16384, 1024))
    wq = np.asarray(L0_attn_wq, dtype=np.float32)
    wk = np.asarray(L0_attn_wk, dtype=np.float32)
    wv = np.asarray(L0_attn_wv, dtype=np.float32)
    wo = np.asarray(L0_attn_wo, dtype=np.float32)
    wg = np.asarray(L0_ffn_w_gate, dtype=np.float32)
    wu = np.asarray(L0_ffn_w_up, dtype=np.float32)
    wd = np.asarray(L0_ffn_w_down, dtype=np.float32)

    xT_full = np.ascontiguousarray(x.T)
    wqt = np.ascontiguousarray(wq.T)
    wkt = np.ascontiguousarray(wk.T)
    wvt = np.ascontiguousarray(wv.T)
    wot = np.ascontiguousarray(wo.T)
    onesf_h = np.ones((128, 128), dtype=np.float32)
    onesb_h = np.ones((128, 1), dtype=ml_dtypes.bfloat16)
    identb_h = np.eye(128, dtype=ml_dtypes.bfloat16)

    in_maps = []
    for r in range(NCORES):
        in_maps.append({
            "xT": np.ascontiguousarray(xT_full[:, T * r:T * (r + 1)]),
            "wqt": wqt, "wkt": wkt, "wvt": wvt, "wot": wot,
            "wg_nat": np.ascontiguousarray(wg[:, 128 * r:128 * (r + 1)]),
            "wu_nat": np.ascontiguousarray(wu[:, 128 * r:128 * (r + 1)]),
            "wd_nat": np.ascontiguousarray(wd[:, 384 * r:384 * (r + 1)]),
            "onesf": onesf_h, "onesb": onesb_h, "identb": identb_h,
            "identf": np.eye(128, dtype=np.float32),
        })

    if not _NC_CACHE:
        _NC_CACHE.append(build_program())
    nc = _NC_CACHE[0]
    res = run_bass_kernel_spmd(nc, in_maps, core_ids=list(range(NCORES)))
    outT_full = np.concatenate(
        [np.asarray(res.results[r]["outT"]) for r in range(NCORES)], axis=1)
    return np.ascontiguousarray(outT_full.T).reshape(1, 16384, 1024).astype(np.float32)


# version 14
# speedup vs baseline: 1.0174x; 1.0060x over previous
import contextlib
import numpy as np
import ml_dtypes

import concourse.bass as bass
import concourse.mybir as mybir
from concourse.bass_utils import run_bass_kernel_spmd

F32 = mybir.dt.float32
BF16 = mybir.dt.bfloat16
AF = mybir.ActivationFunctionType
OP = mybir.AluOpType
AX = mybir.AxisListType

NCORES = 8
T = 2048
E = 1024
EC = 8
D = 64
I = 3072
IC = 24
NB = 4
TC = 16

MAGIC = 12582912.0
QA = 32767.0
QW = 7.0
SIG = 65536.0
SIGMAX = 65535.0
C_ATTN = float(np.sqrt(3.0 / 64.0))
LNEPS = 1e-5
SEPS = 1e-12


class Ctx:

    def __init__(self):
        self.pass2 = False
        self.cnt = {}
        self.cks = {}
        self.sems = {}

    def reset(self, pass2):
        self.pass2 = pass2
        self.cnt = {k: 0 for k in ("dS", "dA", "dG", "CC", "tE", "vE", "sE", "gE")}

    AUTO = ("vE", "sE", "gE")

    def inc(self, instr, sem, n):
        if sem in self.AUTO:
            return
        self.cnt[sem] += n
        if self.pass2:
            instr.then_inc(self.sems[sem], n)

    def ck(self, label, sem):
        v = (sem, self.cnt[sem])
        if self.pass2:
            assert self.cks[label] == v, (label, self.cks[label], v)
        else:
            assert label not in self.cks, f"dup ck {label}"
            self.cks[label] = v

    def wait(self, eng, label):
        if self.pass2:
            sem, v = self.cks[label]
            eng.wait_ge(self.sems[sem], v)

    def wait_val(self, eng, sem, v):
        if self.pass2:
            eng.wait_ge(self.sems[sem], v)


class Mock:
    def __getattr__(self, name):
        return lambda *a, **k: None


_CHAIN_OPS = {
    "tensor_copy", "tensor_tensor", "tensor_scalar", "tensor_single_scalar",
    "tensor_reduce", "reduce_max", "reduce_sum", "scalar_tensor_tensor",
    "reciprocal", "memset", "memzero", "activation", "copy", "add", "mul",
    "square", "sqrt", "sign", "tensor_add", "tensor_sub", "tensor_mul",
    "tensor_max", "tensor_scalar_add", "tensor_scalar_sub", "tensor_scalar_mul",
    "tensor_scalar_max", "tensor_scalar_min", "transpose", "select",
}


class Chain:

    def __init__(self, eng, C, sem):
        self._eng = eng
        self._C = C
        self._sem = sem

    def __getattr__(self, name):
        eng, C, sem = self._eng, self._C, self._sem
        if name in _CHAIN_OPS:
            def op(*a, **k):
                C.wait_val(eng, sem, C.cnt[sem])
                instr = getattr(eng, name)(*a, **k)
                C.cnt[sem] += 1
                if C.pass2:
                    instr.then_inc(C.sems[sem], 1)
                return instr
            return op
        if name == "dma_start":
            def dma(*a, **k):
                C.wait_val(eng, sem, C.cnt[sem])
                return eng.dma_start(*a, **k)
            return dma
        return getattr(eng, name)


def build_program(debug=False):
    nc = bass.Bass("TRN2", target_bir_lowering=False)
    C = Ctx()

    xT = nc.dram_tensor("xT", [E, T], F32, kind="ExternalInput").ap()
    wqt = nc.dram_tensor("wqt", [E, D], F32, kind="ExternalInput").ap()
    wkt = nc.dram_tensor("wkt", [E, D], F32, kind="ExternalInput").ap()
    wvt = nc.dram_tensor("wvt", [E, D], F32, kind="ExternalInput").ap()
    wot = nc.dram_tensor("wot", [D, E], F32, kind="ExternalInput").ap()
    wg_nat = nc.dram_tensor("wg_nat", [I, 128], F32, kind="ExternalInput").ap()
    wu_nat = nc.dram_tensor("wu_nat", [I, 128], F32, kind="ExternalInput").ap()
    wd_nat = nc.dram_tensor("wd_nat", [E, 384], F32, kind="ExternalInput").ap()
    onesf = nc.dram_tensor("onesf", [128, 128], F32, kind="ExternalInput").ap()
    onesb = nc.dram_tensor("onesb", [128, 1], BF16, kind="ExternalInput").ap()
    identb = nc.dram_tensor("identb", [128, 128], BF16, kind="ExternalInput").ap()
    identf = nc.dram_tensor("identf", [128, 128], F32, kind="ExternalInput").ap()
    outT = nc.dram_tensor("outT", [E, T], F32, kind="ExternalOutput").ap()
    dbg = {}
    if debug:
        for n, sh, dt in [("dbg_cols", [128, 32], F32), ("dbg_misc", [1, 512], F32),
                          ("dbg_esum", [64, 64], F32), ("dbg_o0", [128, T], BF16),
                          ("dbg_up0", [128, T], BF16), ("dbg_attn0", [128, T], BF16),
                          ("dbg_kb", [128, TC * D], BF16), ("dbg_vb", [128, TC * D], BF16),
                          ("dbg_qt", [64, T], BF16), ("dbg_ctx", [64, T], BF16),
                          ("dbg_xq0", [128, T], BF16), ("dbg_dqT", [128, 1024], BF16),
                          ("dbg_sx", [1, 64], F32), ("dbg_xn0", [128, T], BF16),
                          ("dbg_trrow", [8, 128], F32), ("dbg_colsx", [128, 32], F32),
                          ("dbg_arout", [1, 256], F32), ("dbg_arbrow", [1, 128], F32),
                          ("dbg_bc", [128, 128], BF16),
                          ("dbg_arin1", [1, 128], F32), ("dbg_arout1", [1, 128], F32)]:
            dbg[n] = nc.dram_tensor(n, sh, dt, kind="ExternalOutput").ap()

    ag_in = {
        "g": nc.dram_tensor("ag_wg_in", [128, I], BF16),
        "u": nc.dram_tensor("ag_wu_in", [128, I], BF16),
        "d": nc.dram_tensor("ag_wd_in", [384, E], BF16),
    }
    ag_out = {
        "g": nc.dram_tensor("ag_wg_out", [E, I], BF16),
        "u": nc.dram_tensor("ag_wu_out", [E, I], BF16),
        "d": nc.dram_tensor("ag_wd_out", [I, E], BF16),
    }
    ag_e_in = nc.dram_tensor("ag_e_in", [D, D], F32)
    ag_e_out = nc.dram_tensor("ag_e_out", [D, D], F32)
    ar_in = [nc.dram_tensor(f"ar{j}_in", [1, 256], F32) for j in range(4)]
    ar_out = [nc.dram_tensor(f"ar{j}_out", [1, 256], F32) for j in range(4)]
    up_dram = nc.dram_tensor("up_dram", [I, T], BF16)
    attn_dram = nc.dram_tensor("attn_dram", [E, T], BF16)
    flat_dram = nc.dram_tensor("flat_dram", [4, 128], F32)

    st = contextlib.ExitStack()
    sb = lambda name, shape, dt=F32: st.enter_context(nc.sbuf_tensor(name, shape, dt)).ap()

    AR = sb("AR", [128, 49152], BF16)
    ARf = AR.bitcast(F32)
    x_ch = [ARf[:, 2048 * k:2048 * (k + 1)] for k in range(EC)]
    gt = [AR[:, 2048 * i:2048 * (i + 1)] for i in range(IC)]
    dq_nat = ARf[:, 16384:19456]
    dq_T = AR[:, 41984:45056]

    xn = sb("xn", [128, T * EC], BF16)
    xn_ch = [xn[:, 2048 * k:2048 * (k + 1)] for k in range(EC)]
    xq_ch = xn_ch
    xpref = xn.bitcast(F32)
    xpref_sl = [xpref[:, 2048 * j:2048 * (j + 1)] for j in range(2)]
    xsc_sl = [xpref[:, 4096:6144], xpref[:, 6144:8192]]

    SC = sb("SC", [128, 6144], F32)
    xnf = None
    SCt = SC[:, 0:2048]
    SCu = SC[:, 2048:4096]
    SCv = SC[:, 4096:6144]
    Rb = SCt
    Qb = SCu
    xtmp = SCv
    outstage = [SC[:, 0:2048], SC[:, 2048:4096]]
    x2b_sl = [SC.bitcast(BF16)[:, 8192:10240], SC.bitcast(BF16)[:, 10240:12288]]

    upst = sb("upst", [128, 4096], BF16)
    upst_sl = [upst[:, 2048 * j:2048 * (j + 1)] for j in range(2)]
    ctxT = sb("ctxT", [64, T], BF16)
    QTb = sb("QTb", [64, T], BF16)
    wstr = sb("wstr", [128, 2048], BF16)
    wstr_sl = [wstr[:, 1024 * j:1024 * (j + 1)] for j in range(2)]
    wstrD = sb("wstrD", [128, 6144], BF16)
    wstrD_sl = [wstrD[:, 3072 * j:3072 * (j + 1)] for j in range(2)]
    astage = sb("astage", [128, 4096], BF16)
    astage_sl = [astage[:, 2048 * j:2048 * (j + 1)] for j in range(2)]
    astageF = astage.bitcast(F32)
    Kb = sb("Kb", [128, TC * D], BF16)
    Vb = sb("Vb", [128, TC * D], BF16)

    onesf_sb = sb("onesf_sb", [128, 128], F32)
    onesb_sb = sb("onesb_sb", [128, 1], BF16)
    identb_sb = sb("identb_sb", [128, 128], BF16)
    identf_sb = sb("identf_sb", [128, 128], F32)
    wq_f = ARf[:, 22528:23040]
    wk_f = ARf[:, 23040:23552]
    wv_f = ARf[:, 23552:24064]
    wo_f = ARf[0:64, 19456:20480]
    wq_b = sb("wq_b", [128, 512], BF16)
    wk_b = sb("wk_b", [128, 512], BF16)
    wv_b = sb("wv_b", [128, 512], BF16)
    wo_b = sb("wo_b", [64, 1024], BF16)

    dqs = sb("dqs", [128, 288], F32)
    dq_bmax = dqs[:, 0:96]
    dq_s = dqs[:, 96:192]
    dq_rs = dqs[:, 192:288]

    cols = sb("cols", [128, 48], F32)
    maxcol = cols[:, 0:1]
    mctmp = cols[:, 1:2]
    gmaxcol = cols[:, 2:3]
    umaxcol = cols[:, 3:4]
    mmaxcol = cols[:, 4:5]
    rrmaxcol = cols[:, 5:6]
    Mcol = cols[:, 6:7]
    gtmp = cols[:, 7:8]
    negMcol = cols[:, 47:48]
    sx_col = cols[:, 8:9]
    rsx_col = cols[:, 9:10]
    sg_col = cols[:, 10:11]
    rsg_col = cols[:, 11:12]
    msg_col = cols[:, 12:13]
    rsu_col = cols[:, 13:14]
    cg_col = cols[:, 14:15]
    cr_col = cols[:, 15:16]
    so_col = cols[:, 16:17]
    rawc = cols[:, 17:18]
    raw2c = cols[:, 18:19]
    sg65_col = cols[:, 19:20]
    sg2_col = cols[:, 20:21]
    sup_col = cols[:, 21:22]
    tcol = cols[:, 22:23]
    pcol = cols[:, 23:24]
    rec8 = cols[:, 24:32]
    rec8b = cols[:, 32:40]
    rec8c = cols[:, 40:48]

    rows = SCv[0:1, :]
    sum_r = rows[0:1, 0:512]
    ssq_r = rows[0:1, 512:1024]
    rowa = rows[0:1, 1024:1536]
    rowb = rows[0:1, 1536:2048]
    misc = sb("misc", [1, 512], F32)
    srow = misc[0:1, 0:16]
    arrow = misc[0:1, 16:32]
    arb = misc[0:1, 32:48]
    arbrow = misc[0:1, 64:320]
    eps_sb = misc[0:1, 320:321]
    sclr = misc[0:1, 336:352]
    trrow = sb("trrow", [8, 256], F32)
    trsrc = sb("trsrc", [128, 8], F32)
    trsrc2 = sb("trsrc2", [128, 8], F32)
    e_loc = sb("e_loc", [64, 64], F32)
    e_sum = sb("e_sum", [64, 64], F32)
    e_bf = sb("e_bf", [64, 64], BF16)

    ps = [st.enter_context(nc.psum_tensor(f"psb{j}", [128, 512], F32)).ap()
          for j in range(8)]

    for name in ("dS", "dA", "dG", "CC", "tE", "vE", "sE", "gE"):
        C.sems[name] = st.enter_context(nc.semaphore(name))

    RG = [list(range(NCORES))]

    def recip_col(eng, dst_col, src_col):
        eng.tensor_copy(dq_bmax, src_col.broadcast_to([128, 96]))
        eng.reciprocal(dq_s, dq_bmax)
        return eng.tensor_copy(dst_col, dq_s[:, 0:1])


    def emit_sync(eng):
        for k in range(EC):
            C.inc(eng.dma_start(out=x_ch[k], in_=xT[128 * k:128 * (k + 1), :]), "dS", 16)
            C.ck(f"x{k}", "dS")
        for dst, src in ((wq_f, wqt), (wk_f, wkt), (wv_f, wvt)):
            C.inc(eng.dma_start(
                out=dst.rearrange("p (k d) -> p k d", d=D),
                in_=src.rearrange("(k p) d -> p k d", p=128)), "dS", 16)
        C.inc(eng.dma_start(out=wo_f[:, :], in_=wot), "dS", 16)
        C.inc(eng.dma_start(out=onesf_sb[:, :], in_=onesf), "dS", 16)
        C.inc(eng.dma_start(out=onesb_sb[:, :], in_=onesb), "dS", 16)
        C.inc(eng.dma_start(out=identb_sb[:, :], in_=identb), "dS", 16)
        C.inc(eng.dma_start(out=identf_sb[:, :], in_=identf), "dS", 16)
        C.ck("consts", "dS")
        wg_r = ag_out["g"].ap().rearrange("(k p) c -> p k c", p=128)
        for i in range(IC):
            C.wait(eng, "AGwg")
            if i >= 2:
                C.wait(eng, f"Amm{i - 2}")
            C.inc(eng.dma_start(
                out=wstr_sl[i % 2].rearrange("p (k c) -> p k c", c=128),
                in_=wg_r[:, :, 128 * i:128 * (i + 1)]), "dS", 16)
            C.ck(f"wA{i}", "dS")
        wu_r = ag_out["u"].ap().rearrange("(k p) c -> p k c", p=128)
        for i in range(IC):
            C.wait(eng, "AGwu")
            C.wait(eng, f"Bmm{i - 2}" if i >= 2 else "Amm23")
            C.inc(eng.dma_start(
                out=wstr_sl[i % 2].rearrange("p (k c) -> p k c", c=128),
                in_=wu_r[:, :, 128 * i:128 * (i + 1)]), "dS", 16)
            C.ck(f"wB{i}", "dS")
        wd_r = ag_out["d"].ap().rearrange("(i p) c -> p i c", p=128)
        for e in range(EC):
            C.wait(eng, "AGwd")
            if e >= 2:
                C.wait(eng, f"Dmm{e - 2}")
            C.inc(eng.dma_start(
                out=wstrD_sl[e % 2].rearrange("p (i c) -> p i c", c=128),
                in_=wd_r[:, :, 128 * e:128 * (e + 1)]), "dS", 16)
            C.ck(f"wD{e}", "dS")
            if e >= 2:
                C.wait(eng, f"oev{e - 2}")
                C.inc(eng.dma_start(out=outT[128 * (e - 2):128 * (e - 1), :],
                                    in_=outstage[e % 2]), "dS", 16)
                C.ck(f"out{e - 2}", "dS")
        for e in (6, 7):
            C.wait(eng, f"oev{e}")
            C.inc(eng.dma_start(out=outT[128 * e:128 * (e + 1), :],
                                in_=outstage[e % 2]), "dS", 16)
            C.ck(f"out{e}", "dS")
        C.wait(eng, "out7")

    def emit_gpsimd(eng):
        for k in range(EC):
            C.wait(eng, f"x{k}")
            C.inc(eng.dma_start(out=xn_ch[k], in_=x_ch[k]), "dG", 16)
            C.ck(f"xb{k}", "dG")
        C.wait(eng, "consts")
        C.inc(eng.dma_start(out=wq_b[:, :], in_=wq_f[:, :]), "dG", 16)
        C.inc(eng.dma_start(out=wk_b[:, :], in_=wk_f[:, :]), "dG", 16)
        C.inc(eng.dma_start(out=wv_b[:, :], in_=wv_f[:, :]), "dG", 16)
        C.inc(eng.dma_start(out=wo_b[:, :], in_=wo_f[:, :]), "dG", 16)
        C.ck("wcast", "dG")
        prev = None
        for wname in ("g", "u", "d"):
            if prev is not None:
                C.wait(eng, f"dqstage{prev}")
            if wname == "d":
                nat_in = wd_nat.rearrange("(t p) c -> p t c", p=128)
                nat_out = dq_nat.rearrange("p (t c) -> p t c", c=384)
            else:
                src = wg_nat if wname == "g" else wu_nat
                nat_in = src.rearrange("(t p) c -> p t c", p=128)
                nat_out = dq_nat.rearrange("p (t c) -> p t c", c=128)
            C.inc(eng.dma_start(out=nat_out, in_=nat_in), "dG", 16)
            C.ck(f"dqnat{wname}", "dG")
            C.wait(eng, f"dqT{wname}")
            if wname == "d":
                dTr = dq_T.rearrange("p (t j c) -> p j t c", j=3, c=128)
                for j in range(3):
                    C.inc(eng.dma_start(
                        out=ag_in["d"].ap()[128 * j:128 * (j + 1), :]
                            .rearrange("p (t c) -> p t c", c=128),
                        in_=dTr[:, j]), "dG", 16)
            else:
                C.inc(eng.dma_start(out=ag_in[wname][:, :], in_=dq_T), "dG", 16)
            C.ck(f"dqstage{wname}", "dG")
            if wname == "g":
                C.wait_val(eng, "dG", C.cnt["dG"])
                C.inc(eng.collective_compute(
                    "AllGather", OP.bypass, replica_groups=RG,
                    ins=[ag_in["g"].ap().opt()], outs=[ag_out["g"].ap().opt()]), "CC", 1)
                C.ck("AGwg", "CC")
            prev = wname

        C.wait(eng, "sxtrr")
        C.inc(eng.dma_start(out=ar_in[0][0:1, 0:128], in_=trrow[0:1, 0:128]), "dG", 16)
        C.wait_val(eng, "dG", C.cnt["dG"])
        C.inc(eng.collective_compute(
            "AllReduce", OP.max, replica_groups=RG,
            ins=[ar_in[0].ap().opt()], outs=[ar_out[0].ap().opt()]), "CC", 1)
        C.wait_val(eng, "CC", C.cnt["CC"])
        C.inc(eng.dma_start(out=arbrow[0:1, 0:128], in_=ar_out[0][0:1, 0:128]), "dG", 16)
        C.ck("sxback", "dG")
        if debug:
            C.wait(eng, "sxcols")
            C.inc(eng.dma_start(out=dbg["dbg_arout"], in_=ar_out[0][:, :]), "dG", 16)
            C.inc(eng.dma_start(out=dbg["dbg_bc"], in_=astage_sl[1][:, 0:128]), "dG", 16)
            C.inc(eng.dma_start(out=dbg["dbg_arbrow"], in_=arbrow[0:1, 0:128]), "dG", 16)
            C.inc(eng.dma_start(out=dbg["dbg_sx"], in_=misc[0:1, 0:64]), "dG", 16)
            C.inc(eng.dma_start(out=dbg["dbg_xn0"], in_=xn_ch[0]), "dG", 16)
            C.inc(eng.dma_start(out=dbg["dbg_trrow"], in_=trrow[:, 0:128]), "dG", 16)
            C.inc(eng.dma_start(out=dbg["dbg_colsx"], in_=cols[:, 0:32]), "dG", 16)
            C.ck("dbgsx", "dG")

        C.wait(eng, "eloc")
        C.inc(eng.dma_start(out=ag_e_in[:, :], in_=e_loc), "dG", 16)
        C.wait_val(eng, "dG", C.cnt["dG"])
        C.inc(eng.collective_compute(
            "AllReduce", OP.add, replica_groups=RG,
            ins=[ag_e_in.ap().opt()], outs=[ag_e_out.ap().opt()]), "CC", 1)
        C.wait_val(eng, "CC", C.cnt["CC"])
        C.inc(eng.dma_start(out=e_sum[:, :], in_=ag_e_out[:, :]), "dG", 16)
        C.ck("eback", "dG")
        C.wait_val(eng, "dG", C.cnt["dG"])
        C.inc(eng.collective_compute(
            "AllGather", OP.bypass, replica_groups=RG,
            ins=[ag_in["u"].ap().opt()], outs=[ag_out["u"].ap().opt()]), "CC", 1)
        C.ck("AGwu", "CC")
        C.inc(eng.collective_compute(
            "AllGather", OP.bypass, replica_groups=RG,
            ins=[ag_in["d"].ap().opt()], outs=[ag_out["d"].ap().opt()]), "CC", 1)
        C.ck("AGwd", "CC")

        C.wait(eng, "sgtrr")
        C.inc(eng.dma_start(out=ar_in[1][0:1, 0:128], in_=trrow[0:1, 0:128]), "dG", 16)
        C.wait_val(eng, "dG", C.cnt["dG"])
        if debug:
            C.inc(eng.dma_start(out=dbg["dbg_arin1"], in_=ar_in[1][0:1, 0:128]), "dG", 16)
            C.wait_val(eng, "dG", C.cnt["dG"])
        C.inc(eng.collective_compute(
            "AllReduce", OP.max, replica_groups=RG,
            ins=[ar_in[1].ap().opt()], outs=[ar_out[1].ap().opt()]), "CC", 1)
        C.wait_val(eng, "CC", C.cnt["CC"])
        C.inc(eng.dma_start(out=arbrow[0:1, 0:128], in_=ar_out[1][0:1, 0:128]), "dG", 16)
        if debug:
            C.inc(eng.dma_start(out=dbg["dbg_arout1"], in_=ar_out[1][0:1, 0:128]), "dG", 16)
        C.ck("sgback", "dG")

        C.wait(eng, "sutrr")
        C.inc(eng.dma_start(out=ar_in[2][0:1, 0:256], in_=trrow[0:1, 0:256]), "dG", 16)
        C.wait_val(eng, "dG", C.cnt["dG"])
        C.inc(eng.collective_compute(
            "AllReduce", OP.max, replica_groups=RG,
            ins=[ar_in[2].ap().opt()], outs=[ar_out[2].ap().opt()]), "CC", 1)
        C.wait_val(eng, "CC", C.cnt["CC"])
        C.inc(eng.dma_start(out=arbrow[0:1, 0:256], in_=ar_out[2][0:1, 0:256]), "dG", 16)
        C.ck("suback", "dG")

        C.wait(eng, "sucols")
        C.wait(eng, "upsp23")
        C.inc(eng.dma_start(out=upst_sl[1], in_=up_dram[128:256, :]), "dG", 16)
        C.ck("upldG1", "dG")
        for j in range(3, IC, 2):
            C.wait(eng, f"C2t6_{j - 2}")
            C.inc(eng.dma_start(out=upst_sl[1],
                                in_=up_dram[128 * j:128 * (j + 1), :]), "dG", 16)
            C.ck(f"upldG{j}", "dG")

        C.wait(eng, "sotrr")
        C.inc(eng.dma_start(out=ar_in[3][0:1, 0:128], in_=trrow[0:1, 0:128]), "dG", 16)
        C.wait_val(eng, "dG", C.cnt["dG"])
        C.inc(eng.collective_compute(
            "AllReduce", OP.max, replica_groups=RG,
            ins=[ar_in[3].ap().opt()], outs=[ar_out[3].ap().opt()]), "CC", 1)
        C.wait_val(eng, "CC", C.cnt["CC"])
        C.inc(eng.dma_start(out=arbrow[0:1, 0:128], in_=ar_out[3][0:1, 0:128]), "dG", 16)
        C.ck("soback", "dG")

        for e in range(EC):
            C.wait(eng, f"Dev1_{e}")
            C.wait(eng, f"apref{e}")
            for nb in range(NB):
                ti = eng.tensor_tensor(
                    outstage[e % 2][:, 512 * nb:512 * (nb + 1)],
                    outstage[e % 2][:, 512 * nb:512 * (nb + 1)],
                    astage_sl[e % 2][:, 512 * nb:512 * (nb + 1)], OP.add)
            C.inc(ti, "gE", 1)
            C.ck(f"oev{e}", "gE")
        C.wait(eng, "out7")
        if debug:
            C.inc(eng.dma_start(out=dbg["dbg_cols"], in_=cols[:, 0:32]), "dG", 16)
            C.inc(eng.dma_start(out=dbg["dbg_misc"], in_=misc), "dG", 16)
            C.inc(eng.dma_start(out=dbg["dbg_esum"], in_=e_sum[:, :]), "dG", 16)
            C.inc(eng.dma_start(out=dbg["dbg_o0"], in_=gt[0]), "dG", 16)
            C.inc(eng.dma_start(out=dbg["dbg_up0"], in_=up_dram[0:128, :]), "dG", 16)
            C.inc(eng.dma_start(out=dbg["dbg_attn0"], in_=attn_dram[0:128, :]), "dG", 16)
            C.inc(eng.dma_start(out=dbg["dbg_kb"], in_=Kb), "dG", 16)
            C.inc(eng.dma_start(out=dbg["dbg_vb"], in_=Vb), "dG", 16)
            C.inc(eng.dma_start(out=dbg["dbg_qt"], in_=QTb), "dG", 16)
            C.inc(eng.dma_start(out=dbg["dbg_ctx"], in_=ctxT), "dG", 16)
            C.inc(eng.dma_start(out=dbg["dbg_xq0"], in_=xn_ch[0]), "dG", 16)
            C.inc(eng.dma_start(out=dbg["dbg_dqT"], in_=ag_out["d"].ap()[0:128, :]), "dG", 16)
            C.wait_val(eng, "dG", C.cnt["dG"])

    def emit_tensor(eng):
        C.wait(eng, "consts")
        for wi, wname in enumerate(("g", "u", "d")):
            C.wait(eng, f"dq{wname}bf")
            for b in range(6):
                gi = wi * 6 + b
                if gi >= 2:
                    C.wait(eng, f"dqev{gi - 2}")
                for j in range(4):
                    t = b * 4 + j
                    ti = eng.transpose(ps[gi % 2][:, 128 * j:128 * (j + 1)],
                                       dq_nat[:, 128 * t:128 * (t + 1)],
                                       identf_sb[:, :])
                C.inc(ti, "tE", 1)
                C.ck(f"dqtr{gi}", "tE")
        C.wait(eng, "dqev17")
        for k in range(EC):
            C.wait(eng, f"xb{k}")
            C.wait(eng, f"x2_{k}")
            for nb in range(NB):
                eng.matmul(ps[nb][0:1, :], onesb_sb[:, 0:1],
                           xn_ch[k][:, 512 * nb:512 * (nb + 1)],
                           start=(k == 0), stop=(k == EC - 1))
            for nb in range(NB):
                ti = eng.matmul(ps[4 + nb][0:1, :], onesb_sb[:, 0:1],
                                x2b_sl[k % 2][:, 512 * nb:512 * (nb + 1)],
                                start=(k == 0), stop=(k == EC - 1))
            C.inc(ti, "tE", 1)
            C.ck(f"sums{k}", "tE")
        for nb in range(NB):
            C.wait(eng, f"rqrow{nb}")
            eng.matmul(ps[nb][:, :], onesf_sb[0:1, :], rowa, start=True, stop=True)
            ti = eng.matmul(ps[4 + nb][:, :], onesf_sb[0:1, :], rowb,
                            start=True, stop=True)
            C.inc(ti, "tE", 1)
            C.ck(f"rqmm{nb}", "tE")
        C.wait(eng, "xn7")
        C.wait(eng, "rqev")
        C.wait(eng, "wcast")
        for nb in range(NB):
            for k in range(EC):
                ti = eng.matmul(ps[nb][0:64, :], wq_b[:, 64 * k:64 * (k + 1)],
                                xn_ch[k][:, 512 * nb:512 * (nb + 1)],
                                start=(k == 0), stop=(k == EC - 1))
            C.inc(ti, "tE", 1)
            C.ck(f"qmm{nb}", "tE")
        for c in range(TC):
            if c >= 2:
                C.wait(eng, f"kvev{c - 2}")
            bk, bv = 4 + (c % 2) * 2, 5 + (c % 2) * 2
            for k in range(EC):
                eng.matmul(ps[bk][:, 0:64], xn_ch[k][:, 128 * c:128 * (c + 1)],
                           wk_b[:, 64 * k:64 * (k + 1)],
                           start=(k == 0), stop=(k == EC - 1))
                ti = eng.matmul(ps[bv][:, 0:64], xn_ch[k][:, 128 * c:128 * (c + 1)],
                                wv_b[:, 64 * k:64 * (k + 1)],
                                start=(k == 0), stop=(k == EC - 1))
            C.inc(ti, "tE", 1)
            C.ck(f"kvmm{c}", "tE")
        C.wait(eng, "qev")
        C.wait(eng, "sxcol")
        ti = eng.transpose(ps[0][0:8, 0:128], trsrc[:, :], identf_sb[:, :])
        C.inc(ti, "tE", 1)
        C.ck("sxtr", "tE")
        C.wait(eng, "sxback")
        ti = eng.matmul(ps[0][:, 0:128], onesf_sb[0:1, :], arbrow[0:1, 0:128],
                        start=True, stop=True)
        C.inc(ti, "tE", 1)
        C.ck("sxbc", "tE")
        C.wait(eng, "kvev15")
        for c in range(TC):
            ti = eng.matmul(ps[1][0:64, 0:64], Kb[:, 64 * c:64 * (c + 1)],
                            Vb[:, 64 * c:64 * (c + 1)],
                            start=(c == 0), stop=(c == TC - 1))
        C.inc(ti, "tE", 1)
        C.ck("emm", "tE")
        for i in range(IC):
            grp = (i % 2) * 4
            C.wait(eng, f"wA{i}")
            if i >= 2:
                C.wait(eng, f"actA{i - 2}")
            for k in range(EC):
                if i < 3:
                    C.wait(eng, f"xq{k}")
                for nb in range(NB):
                    ti = eng.matmul(ps[grp + nb][:, :],
                                    wstr_sl[i % 2][:, 128 * k:128 * (k + 1)],
                                    xq_ch[k][:, 512 * nb:512 * (nb + 1)],
                                    start=(k == 0), stop=(k == EC - 1))
            C.inc(ti, "tE", 1)
            C.ck(f"Amm{i}", "tE")
        C.wait(eng, "actA22")
        C.wait(eng, "gmaxcol")
        ti = eng.transpose(ps[0][0:8, 0:128], trsrc[:, :], identf_sb[:, :])
        C.inc(ti, "tE", 1)
        C.ck("sgtr", "tE")
        for i in range(IC):
            grp = (i % 2) * 4
            if i == 3:
                C.wait(eng, "sgback")
                C.wait(eng, "actB2")
                ti = eng.matmul(ps[0][:, 0:128], onesf_sb[0:1, :],
                                arbrow[0:1, 0:128], start=True, stop=True)
                C.inc(ti, "tE", 1)
                C.ck("sgbc", "tE")
            C.wait(eng, f"wB{i}")
            if i == 4:
                C.wait(eng, "sgcols")
            if i >= 2:
                C.wait(eng, f"actB{i - 2}")
            elif i == 0:
                C.wait(eng, "sgtrr")
            else:
                C.wait(eng, "actA23")
            for k in range(EC):
                for nb in range(NB):
                    ti = eng.matmul(ps[grp + nb][:, :],
                                    wstr_sl[i % 2][:, 128 * k:128 * (k + 1)],
                                    xq_ch[k][:, 512 * nb:512 * (nb + 1)],
                                    start=(k == 0), stop=(k == EC - 1))
            C.inc(ti, "tE", 1)
            C.ck(f"Bmm{i}", "tE")
        C.wait(eng, "umaxcol")
        C.wait(eng, "mmaxcol")
        C.wait(eng, "actB22")
        eng.transpose(ps[0][0:8, 0:128], trsrc[:, :], identf_sb[:, :])
        ti = eng.transpose(ps[1][0:8, 0:128], trsrc2[:, :], identf_sb[:, :])
        C.inc(ti, "tE", 1)
        C.ck("sutr", "tE")
        C.wait(eng, "ebf")
        C.wait(eng, "sxcols")
        C.wait(eng, "eev")
        C.wait(eng, "actB23")
        for nb in range(NB):
            ti = eng.matmul(ps[2 + nb][0:64, :], e_bf[:, :],
                            QTb[:, 512 * nb:512 * (nb + 1)], start=True, stop=True)
        C.inc(ti, "tE", 1)
        C.ck("cmmall", "tE")
        for e in range(EC):
            grp = 4 if e % 2 else 0
            C.wait(eng, f"aev{e - 2}" if e >= 2 else "cev")
            if e == 0:
                C.wait(eng, "sutrr")
            for nb in range(NB):
                ti = eng.matmul(ps[grp + nb][:, :], wo_b[:, 128 * e:128 * (e + 1)],
                                ctxT[:, 512 * nb:512 * (nb + 1)], start=True, stop=True)
            C.inc(ti, "tE", 1)
            C.ck(f"amm{e}", "tE")
        C.wait(eng, "suback")
        C.wait(eng, "aev6")
        C.wait(eng, "aev7")
        eng.matmul(ps[2][:, 0:128], onesf_sb[0:1, :], arbrow[0:1, 0:128],
                   start=True, stop=True)
        ti = eng.matmul(ps[3][:, 0:128], onesf_sb[0:1, :], arbrow[0:1, 128:256],
                        start=True, stop=True)
        C.inc(ti, "tE", 1)
        C.ck("subc", "tE")
        C.wait(eng, "sucols")
        C.wait(eng, "rrmaxcol")
        ti = eng.transpose(ps[0][0:8, 0:128], trsrc[:, :], identf_sb[:, :])
        C.inc(ti, "tE", 1)
        C.ck("sotr", "tE")
        C.wait(eng, "soback")
        ti = eng.matmul(ps[2][:, 0:128], onesf_sb[0:1, :], arbrow[0:1, 0:128],
                        start=True, stop=True)
        C.inc(ti, "tE", 1)
        C.ck("sobc", "tE")
        for e in range(EC):
            grp = (e % 2) * 4
            C.wait(eng, f"wD{e}")
            if e >= 2:
                C.wait(eng, f"Dev1_{e - 2}")
            else:
                C.wait(eng, f"actB{22 + e}")
            for i in range(IC):
                if e == 0:
                    C.wait(eng, f"o{i}")
                for nb in range(NB):
                    ti = eng.matmul(ps[grp + nb][:, :],
                                    wstrD_sl[e % 2][:, 128 * i:128 * (i + 1)],
                                    gt[i][:, 512 * nb:512 * (nb + 1)],
                                    start=(i == 0), stop=(i == IC - 1))
            C.inc(ti, "tE", 1)
            C.ck(f"Dmm{e}", "tE")

    def emit_vector(eng):
        eng.memset(eps_sb, LNEPS)
        eng.memset(Mcol, MAGIC)
        ti = eng.memset(negMcol, -MAGIC)
        C.inc(ti, "vE", 1)
        C.ck("memset", "vE")
        prev = None
        for wname in ("g", "u", "d"):
            C.wait(eng, f"dqnat{wname}")
            if prev is not None:
                C.wait(eng, f"dqstage{prev}")
            nat3 = dq_nat.rearrange("p (b c) -> p b c", c=32)
            eng.tensor_reduce(dq_bmax, nat3, AX.X, OP.max, apply_absolute_value=True)
            eng.tensor_scalar(dq_s, dq_bmax, 1.0 / QW, SEPS, OP.mult, OP.max)
            eng.reciprocal(dq_rs, dq_s)
            eng.tensor_tensor(nat3, nat3,
                              dq_rs[:, :, None].broadcast_to([128, 96, 32]), OP.mult)
            eng.tensor_scalar(dq_nat, dq_nat, MAGIC, MAGIC, OP.add, OP.subtract)
            ti = eng.tensor_tensor(nat3, nat3,
                                   dq_s[:, :, None].broadcast_to([128, 96, 32]), OP.mult)
            C.inc(ti, "vE", 1)
            C.ck(f"dq{wname}bf", "vE")
            prev = wname

        C.wait(eng, f"sums{EC - 1}")
        for nb in range(NB):
            if nb >= 1:
                C.wait(eng, f"rqmm{nb - 1}")
            eng.tensor_copy(sum_r, ps[nb][0:1, :])
            eng.tensor_copy(ssq_r, ps[4 + nb][0:1, :])
            eng.tensor_scalar(rowa, sum_r, 1.0 / E, None, OP.mult, OP.bypass)
            eng.tensor_tensor(rowb, rowa, rowa, OP.mult)
            eng.tensor_scalar(rowa, ssq_r, 1.0 / E, None, OP.mult, OP.bypass)
            ti = eng.tensor_tensor(rowb, rowa, rowb, OP.subtract)
            C.inc(ti, "vE", 1)
            C.ck(f"var{nb}", "vE")
            C.wait(eng, f"sqrt{nb}")
            eng.reciprocal(rowa, rowb)
            ti = eng.scalar_tensor_tensor(rowb, sum_r, 1.0 / E, rowa,
                                          OP.mult, OP.mult)
            C.inc(ti, "vE", 1)
            C.ck(f"rqrow{nb}", "vE")
        for nb in range(NB):
            C.wait(eng, f"rqmm{nb}")
            eng.tensor_copy(Rb[:, 512 * nb:512 * (nb + 1)], ps[nb][:, :])
            ti = eng.tensor_copy(Qb[:, 512 * nb:512 * (nb + 1)], ps[4 + nb][:, :])
        C.inc(ti, "vE", 1)
        C.ck("rqev", "vE")
        for k in range(EC):
            eng.tensor_tensor(xtmp, x_ch[k], Rb, OP.mult)
            ti = eng.tensor_tensor(xn_ch[k], xtmp, Qb, OP.subtract)
            C.inc(ti, "vE", 1)
            C.ck(f"xn{k}", "vE")
            eng.tensor_reduce(mctmp, xn_ch[k], AX.X, OP.max, apply_absolute_value=True)
            ti = (eng.tensor_tensor(maxcol, maxcol, mctmp, OP.max) if k else
                  eng.tensor_copy(maxcol, mctmp))
            C.inc(ti, "vE", 1)
        ti = eng.tensor_copy(trsrc[:, 0:1], maxcol)
        C.inc(ti, "vE", 1)
        C.ck("sxcol", "vE")
        C.wait(eng, "sxtr")
        ti = eng.tensor_copy(trrow[:, 0:128], ps[0][0:8, 0:128])
        C.inc(ti, "vE", 1)
        C.ck("sxtrr", "vE")
        C.wait(eng, "sxbc")
        eng.tensor_reduce(rawc, ps[0][:, 0:128], AX.X, OP.max)
        if debug:
            eng.tensor_copy(astage_sl[1][:, 0:128], ps[0][:, 0:128])
        eng.tensor_scalar(sx_col, rawc, 1.0 / QA, SEPS, OP.mult, OP.max)
        ti = recip_col(eng, rsx_col, sx_col)
        C.inc(ti, "vE", 1)
        C.ck("sxcols", "vE")
        C.wait(eng, "kvmm15")
        if debug:
            C.wait(eng, "dbgsx")
        for k in range(EC):
            eng.tensor_scalar(SCt, xn_ch[k], rsx_col, MAGIC, OP.mult, OP.add)
            ti = eng.tensor_scalar(xq_ch[k], SCt, MAGIC, sx_col, OP.subtract, OP.mult)
            C.inc(ti, "vE", 1)
            C.ck(f"xq{k}", "vE")
        C.wait(eng, "eback")
        ti = eng.tensor_scalar(e_bf[:, :], e_sum[:, :], C_ATTN, None, OP.mult, OP.bypass)
        C.inc(ti, "vE", 1)
        C.ck("ebf", "vE")
        for i in range(IC):
            C.wait(eng, f"actA{i}")
            eng.tensor_reduce(mctmp, gt[i], AX.X, OP.max, apply_absolute_value=True)
            ti = (eng.tensor_tensor(gmaxcol, gmaxcol, mctmp, OP.max) if i else
                  eng.tensor_copy(gmaxcol, mctmp))
            C.inc(ti, "vE", 1)
        ti = eng.tensor_copy(trsrc[:, 0:1], gmaxcol)
        C.inc(ti, "vE", 1)
        C.ck("gmaxcol", "vE")
        C.wait(eng, "sgtr")
        ti = eng.tensor_copy(trrow[:, 0:128], ps[0][0:8, 0:128])
        C.inc(ti, "vE", 1)
        C.ck("sgtrr", "vE")
        def vec_c1(j):
            C.wait(eng, f"c1r{j}")
            tb = SCt if j % 2 == 0 else SCv
            ub = SCu if j % 2 == 0 else astageF
            eng.tensor_scalar(ub, ub, MAGIC + SIGMAX, MAGIC, OP.min, OP.subtract)
            ti = eng.scalar_tensor_tensor(gt[j], tb, -MAGIC, ub, OP.add, OP.mult)
            C.inc(ti, "vE", 1)
            C.ck(f"C1m{j}", "vE")
            eng.tensor_reduce(mctmp, gt[j], AX.X, OP.max, apply_absolute_value=True)
            ti = (eng.tensor_tensor(mmaxcol, mmaxcol, mctmp, OP.max) if j else
                  eng.tensor_copy(mmaxcol, mctmp))
            C.inc(ti, "vE", 1)

        for i in range(IC):
            if i == 2:
                C.wait(eng, "sgbc")
                eng.tensor_reduce(rawc, ps[0][:, 0:128], AX.X, OP.max)
                eng.tensor_scalar(sg_col, rawc, 1.0 / QA, SEPS, OP.mult, OP.max)
                recip_col(eng, rsg_col, sg_col)
                eng.tensor_scalar(sg65_col, sg_col, 1.0 / SIG, None,
                                  OP.mult, OP.bypass)
                ti = eng.tensor_scalar(msg_col, sg_col, -MAGIC, None,
                                       OP.mult, OP.bypass)
                C.inc(ti, "vE", 1)
                C.ck("sgcols", "vE")
            C.wait(eng, f"actB{i}")
            eng.tensor_reduce(mctmp, upst_sl[i % 2], AX.X, OP.max,
                              apply_absolute_value=True)
            ti = (eng.tensor_tensor(umaxcol, umaxcol, mctmp, OP.max) if i else
                  eng.tensor_copy(umaxcol, mctmp))
            C.inc(ti, "vE", 1)
            C.ck(f"umax{i}", "vE")
            if i >= 2:
                vec_c1(i - 2)
        for j in range(IC - 2, IC):
            vec_c1(j)
        eng.tensor_copy(trsrc[:, 0:1], umaxcol)
        ti = eng.tensor_copy(trsrc2[:, 0:1], mmaxcol)
        C.inc(ti, "vE", 1)
        C.ck("umaxcol", "vE")
        C.ck("mmaxcol", "vE")
        C.wait(eng, "sutr")
        eng.tensor_copy(trrow[:, 0:128], ps[0][0:8, 0:128])
        ti = eng.tensor_copy(trrow[:, 128:256], ps[1][0:8, 0:128])
        C.inc(ti, "vE", 1)
        C.ck("sutrr", "vE")
        for e in range(1, EC, 2):
            C.wait(eng, f"amm{e}")
            if e >= 3:
                C.wait(eng, f"asp{e - 2}")
            for nb in range(NB):
                ti = eng.tensor_copy(astage_sl[1][:, 512 * nb:512 * (nb + 1)],
                                     ps[4 + nb][:, :])
            C.inc(ti, "vE", 1)
            C.ck(f"aev{e}", "vE")
        C.wait(eng, "subc")
        eng.tensor_reduce(rawc, ps[2][:, 0:128], AX.X, OP.max)
        eng.tensor_reduce(raw2c, ps[3][:, 0:128], AX.X, OP.max)
        eng.tensor_scalar(sup_col, rawc, 1.0 / QA, SEPS, OP.mult, OP.max)
        recip_col(eng, rsu_col, sup_col)
        eng.tensor_tensor(tcol, raw2c, sg65_col, OP.mult)
        eng.tensor_scalar(sg2_col, tcol, 1.0 / QA, SEPS, OP.mult, OP.max)
        recip_col(eng, tcol, sg2_col)
        ti = eng.tensor_tensor(cg_col, sg65_col, tcol, OP.mult)
        C.inc(ti, "vE", 1)
        C.ck("sucols", "vE")
        for i in range(IC):
            buf = SCu if i % 2 == 0 else SCv
            t5b = SCt if i % 2 == 0 else xsc_sl[0]
            if i % 2 == 0:
                C.wait(eng, f"C2t6_{i}")
                eng.tensor_scalar(buf, buf, MAGIC, None, OP.subtract, OP.bypass)
            else:
                C.wait(eng, f"C2g{i}")
            C.wait(eng, f"C2t5_{i}")
            ti = eng.scalar_tensor_tensor(gt[i], t5b, -MAGIC, buf, OP.add, OP.mult)
            C.inc(ti, "vE", 1)
            C.ck(f"C2rr{i}", "vE")
            eng.tensor_reduce(mctmp, gt[i], AX.X, OP.max, apply_absolute_value=True)
            ti = (eng.tensor_tensor(rrmaxcol, rrmaxcol, mctmp, OP.max) if i else
                  eng.tensor_copy(rrmaxcol, mctmp))
            C.inc(ti, "vE", 1)
        ti = eng.tensor_copy(trsrc[:, 0:1], rrmaxcol)
        C.inc(ti, "vE", 1)
        C.ck("rrmaxcol", "vE")
        C.wait(eng, "sotr")
        ti = eng.tensor_copy(trrow[:, 0:128], ps[0][0:8, 0:128])
        C.inc(ti, "vE", 1)
        C.ck("sotrr", "vE")
        C.wait(eng, "sobc")
        eng.tensor_reduce(rawc, ps[2][:, 0:128], AX.X, OP.max)
        eng.tensor_tensor(pcol, sg2_col, sup_col, OP.mult)
        eng.tensor_tensor(tcol, rawc, pcol, OP.mult)
        eng.tensor_scalar(so_col, tcol, 1.0 / QA, SEPS, OP.mult, OP.max)
        recip_col(eng, tcol, so_col)
        ti = eng.tensor_tensor(cr_col, pcol, tcol, OP.mult)
        C.inc(ti, "vE", 1)
        C.ck("socols", "vE")
        C.ck("somax", "vE")
        for i in range(IC):
            if i % 2 == 0:
                C.wait(eng, f"ot7_{i}")
                t7b = xsc_sl[(i % 4) // 2]
                ti = eng.tensor_scalar(gt[i], t7b, MAGIC, so_col,
                                       OP.subtract, OP.mult)
            else:
                eng.tensor_scalar(SCt, gt[i], cr_col, MAGIC, OP.mult, OP.add)
                ti = eng.tensor_scalar(gt[i], SCt, MAGIC, so_col,
                                       OP.subtract, OP.mult)
            C.inc(ti, "vE", 1)
            C.ck(f"o{i}", "vE")
        for e in range(EC):
            C.wait(eng, f"Dmm{e}")
            C.wait(eng, f"xpref{e}")
            if e >= 2:
                C.wait(eng, f"out{e - 2}")
            grp = (e % 2) * 4
            for nb in range(NB):
                ti = eng.tensor_tensor(outstage[e % 2][:, 512 * nb:512 * (nb + 1)],
                                       ps[grp + nb][:, :],
                                       xpref_sl[e % 2][:, 512 * nb:512 * (nb + 1)],
                                       OP.add)
            C.inc(ti, "vE", 1)
            C.ck(f"Dev1_{e}", "vE")

    def emit_scalar(eng):
        for wi, wname in enumerate(("g", "u", "d")):
            for b in range(6):
                gi = wi * 6 + b
                C.wait(eng, f"dqtr{gi}")
                ti = eng.copy(dq_T[:, 512 * b:512 * (b + 1)], ps[gi % 2][:, :])
                C.inc(ti, "sE", 1)
                C.ck(f"dqev{gi}", "sE")
            C.ck(f"dqT{wname}", "sE")
        for k in range(EC):
            C.wait(eng, f"x{k}")
            if k >= 2:
                C.wait(eng, f"sums{k - 2}")
            ti = eng.activation(x2b_sl[k % 2], x_ch[k], AF.Square)
            C.inc(ti, "sE", 1)
            C.ck(f"x2_{k}", "sE")
        C.wait(eng, "memset")
        for nb in range(NB):
            C.wait(eng, f"var{nb}")
            ti = eng.activation(rowb, rowb, AF.Sqrt, bias=eps_sb, scale=1.0)
            C.inc(ti, "sE", 1)
            C.ck(f"sqrt{nb}", "sE")
        for nb in range(NB):
            C.wait(eng, f"qmm{nb}")
            ti = eng.copy(QTb[:, 512 * nb:512 * (nb + 1)], ps[nb][0:64, :])
        C.inc(ti, "sE", 1)
        C.ck("qev", "sE")
        for c in range(TC):
            C.wait(eng, f"kvmm{c}")
            bk, bv = 4 + (c % 2) * 2, 5 + (c % 2) * 2
            eng.copy(Kb[:, 64 * c:64 * (c + 1)], ps[bk][:, 0:64])
            ti = eng.copy(Vb[:, 64 * c:64 * (c + 1)], ps[bv][:, 0:64])
            C.inc(ti, "sE", 1)
            C.ck(f"kvev{c}", "sE")
        C.wait(eng, "emm")
        ti = eng.copy(e_loc[:, :], ps[1][0:64, 0:64])
        C.inc(ti, "sE", 1)
        C.ck("eloc", "sE")
        C.ck("eev", "sE")
        for i in range(IC):
            C.wait(eng, f"Amm{i}")
            grp = (i % 2) * 4
            for nb in range(NB):
                ti = eng.copy(gt[i][:, 512 * nb:512 * (nb + 1)], ps[grp + nb][:, :])
            C.inc(ti, "sE", 1)
            C.ck(f"actA{i}", "sE")
        spill0 = C.cnt["dA"]

        def act_c1(j):
            if j == 0:
                C.wait(eng, "sgcols")
            if j >= 2:
                C.wait(eng, f"C1m{j - 2}")
            tb = SCt if j % 2 == 0 else SCv
            ub = SCu if j % 2 == 0 else astageF
            ti = eng.activation(tb, gt[j], AF.Identity, bias=Mcol, scale=rsg_col)
            C.inc(ti, "sE", 1)
            C.ck(f"C1t{j}", "sE")
            ti = eng.activation(ub, tb, AF.Sigmoid, bias=msg_col, scale=sg_col)
            C.inc(ti, "sE", 1)
            C.ck(f"sig{j}", "sE")
            ti = eng.activation(ub, ub, AF.Identity, bias=Mcol, scale=float(SIG))
            C.inc(ti, "sE", 1)
            C.ck(f"c1r{j}", "sE")

        for i in range(IC):
            C.wait(eng, f"Bmm{i}")
            if i >= 2:
                C.wait_val(eng, "dA", spill0 + 16 * (i - 1))
                C.wait(eng, f"umax{i - 2}")
            grp = (i % 2) * 4
            for nb in range(NB):
                ti = eng.copy(upst_sl[i % 2][:, 512 * nb:512 * (nb + 1)],
                              ps[grp + nb][:, :])
            C.inc(ti, "sE", 1)
            C.ck(f"actB{i}", "sE")
            C.inc(eng.dma_start(out=up_dram[128 * i:128 * (i + 1), :],
                                in_=upst_sl[i % 2]), "dA", 16)
            C.ck(f"upsp{i}", "dA")
            if i >= 2:
                act_c1(i - 2)
        for j in range(IC - 2, IC):
            act_c1(j)
        C.wait(eng, "cmmall")
        for nb in range(NB):
            ti = eng.copy(ctxT[:, 512 * nb:512 * (nb + 1)], ps[2 + nb][0:64, :])
        C.inc(ti, "sE", 1)
        C.ck("cev", "sE")
        C.wait(eng, "C1m23")
        for e in range(EC):
            if e % 2 == 0:
                C.wait(eng, f"amm{e}")
                if e >= 2:
                    C.wait(eng, f"asp{e - 2}")
                for nb in range(NB):
                    ti = eng.copy(astage_sl[0][:, 512 * nb:512 * (nb + 1)],
                                  ps[nb][:, :])
                C.inc(ti, "sE", 1)
                C.ck(f"aev{e}", "sE")
            else:
                C.wait(eng, f"aev{e}")
            C.inc(eng.dma_start(out=attn_dram[128 * e:128 * (e + 1), :],
                                in_=astage_sl[e % 2]), "dA", 16)
            C.ck(f"asp{e}", "dA")
        C.wait(eng, "sucols")
        C.inc(eng.dma_start(out=upst_sl[0], in_=up_dram[0:128, :]), "dA", 16)
        C.ck("upldA0", "dA")
        for i in range(IC):
            if i >= 2:
                C.wait(eng, f"C2rr{i - 2}")
            t5b = SCt if i % 2 == 0 else xsc_sl[0]
            ti = eng.activation(t5b, gt[i], AF.Identity, bias=Mcol, scale=cg_col)
            C.inc(ti, "sE", 1)
            C.ck(f"C2t5_{i}", "sE")
            C.wait(eng, f"upldA{i}" if i % 2 == 0 else f"upldG{i}")
            buf = SCu if i % 2 == 0 else SCv
            ti = eng.activation(buf, upst_sl[i % 2], AF.Identity,
                                bias=Mcol, scale=rsu_col)
            C.inc(ti, "sE", 1)
            C.ck(f"C2t6_{i}", "sE")
            if i % 2 == 1:
                ti = eng.activation(buf, buf, AF.Identity, bias=negMcol)
                C.inc(ti, "sE", 1)
                C.ck(f"C2g{i}", "sE")
            if i + 2 < IC and i % 2 == 0:
                C.inc(eng.dma_start(out=upst_sl[0],
                                    in_=up_dram[128 * (i + 2):128 * (i + 3), :]),
                      "dA", 16)
                C.ck(f"upldA{i + 2}", "dA")
        for i in range(0, IC, 2):
            if i == 0:
                C.wait(eng, "socols")
            if i >= 4:
                C.wait(eng, f"o{i - 4}")
            t7b = xsc_sl[(i % 4) // 2]
            ti = eng.activation(t7b, gt[i], AF.Identity, bias=Mcol, scale=cr_col)
            C.inc(ti, "sE", 1)
            C.ck(f"ot7_{i}", "sE")
        pf0 = C.cnt["dA"]
        for e in range(EC):
            if e == 0:
                C.wait(eng, "somax")
            if e >= 2:
                C.wait(eng, f"Dev1_{e - 2}")
                C.wait(eng, f"oev{e - 2}")
            C.inc(eng.dma_start(out=xpref_sl[e % 2],
                                in_=xT[128 * e:128 * (e + 1), :]), "dA", 16)
            C.inc(eng.dma_start(out=astage_sl[e % 2],
                                in_=attn_dram[128 * e:128 * (e + 1), :]), "dA", 16)
            C.wait_val(eng, "dA", pf0 + 32 * (e + 1))
            C.ck(f"xpref{e}", "dA")
            C.ck(f"apref{e}", "dA")

    C.reset(False)
    emit_sync(Mock())
    emit_vector(Chain(Mock(), C, "vE"))
    emit_scalar(Chain(Mock(), C, "sE"))
    emit_tensor(Mock())
    emit_gpsimd(Chain(Mock(), C, "gE"))

    C.reset(True)
    with nc.Block() as block:
        @block.sync
        def _(eng):
            emit_sync(eng)

        @block.vector
        def _(eng):
            emit_vector(Chain(eng, C, "vE"))

        @block.scalar
        def _(eng):
            emit_scalar(Chain(eng, C, "sE"))

        @block.tensor
        def _(eng):
            emit_tensor(eng)

        @block.gpsimd
        def _(eng):
            emit_gpsimd(Chain(eng, C, "gE"))

    st.close()
    return nc


_NC_CACHE = []


def kernel(L0_attn_x, L0_attn_wq, L0_attn_wk, L0_attn_wv, L0_attn_wo,
           L0_ffn_w_gate, L0_ffn_w_up, L0_ffn_w_down):
    x = np.ascontiguousarray(np.asarray(L0_attn_x, dtype=np.float32).reshape(16384, 1024))
    wq = np.asarray(L0_attn_wq, dtype=np.float32)
    wk = np.asarray(L0_attn_wk, dtype=np.float32)
    wv = np.asarray(L0_attn_wv, dtype=np.float32)
    wo = np.asarray(L0_attn_wo, dtype=np.float32)
    wg = np.asarray(L0_ffn_w_gate, dtype=np.float32)
    wu = np.asarray(L0_ffn_w_up, dtype=np.float32)
    wd = np.asarray(L0_ffn_w_down, dtype=np.float32)

    xT_full = np.ascontiguousarray(x.T)
    wqt = np.ascontiguousarray(wq.T)
    wkt = np.ascontiguousarray(wk.T)
    wvt = np.ascontiguousarray(wv.T)
    wot = np.ascontiguousarray(wo.T)
    onesf_h = np.ones((128, 128), dtype=np.float32)
    onesb_h = np.ones((128, 1), dtype=ml_dtypes.bfloat16)
    identb_h = np.eye(128, dtype=ml_dtypes.bfloat16)

    in_maps = []
    for r in range(NCORES):
        in_maps.append({
            "xT": np.ascontiguousarray(xT_full[:, T * r:T * (r + 1)]),
            "wqt": wqt, "wkt": wkt, "wvt": wvt, "wot": wot,
            "wg_nat": np.ascontiguousarray(wg[:, 128 * r:128 * (r + 1)]),
            "wu_nat": np.ascontiguousarray(wu[:, 128 * r:128 * (r + 1)]),
            "wd_nat": np.ascontiguousarray(wd[:, 384 * r:384 * (r + 1)]),
            "onesf": onesf_h, "onesb": onesb_h, "identb": identb_h,
            "identf": np.eye(128, dtype=np.float32),
        })

    if not _NC_CACHE:
        _NC_CACHE.append(build_program())
    nc = _NC_CACHE[0]
    res = run_bass_kernel_spmd(nc, in_maps, core_ids=list(range(NCORES)))
    outT_full = np.concatenate(
        [np.asarray(res.results[r]["outT"]) for r in range(NCORES)], axis=1)
    return np.ascontiguousarray(outT_full.T).reshape(1, 16384, 1024).astype(np.float32)


# version 15 (final)
# speedup vs baseline: 1.0292x; 1.0116x over previous
import contextlib
import numpy as np
import ml_dtypes

import concourse.bass as bass
import concourse.mybir as mybir
from concourse.bass_utils import run_bass_kernel_spmd

F32 = mybir.dt.float32
BF16 = mybir.dt.bfloat16
AF = mybir.ActivationFunctionType
OP = mybir.AluOpType
AX = mybir.AxisListType

NCORES = 8
T = 2048
E = 1024
EC = 8
D = 64
I = 3072
IC = 24
NB = 4
TC = 16

MAGIC = 12582912.0
QA = 32767.0
QW = 7.0
SIG = 65536.0
SIGMAX = 65535.0
C_ATTN = float(np.sqrt(3.0 / 64.0))
LNEPS = 1e-5
SEPS = 1e-12


class Ctx:

    def __init__(self):
        self.pass2 = False
        self.cnt = {}
        self.cks = {}
        self.sems = {}

    def reset(self, pass2):
        self.pass2 = pass2
        self.cnt = {k: 0 for k in ("dS", "dA", "dG", "CC", "tE", "vE", "sE", "gE")}

    AUTO = ("vE", "sE", "gE")

    def inc(self, instr, sem, n):
        if sem in self.AUTO:
            return
        self.cnt[sem] += n
        if self.pass2:
            instr.then_inc(self.sems[sem], n)

    def ck(self, label, sem):
        v = (sem, self.cnt[sem])
        if self.pass2:
            assert self.cks[label] == v, (label, self.cks[label], v)
        else:
            assert label not in self.cks, f"dup ck {label}"
            self.cks[label] = v

    def wait(self, eng, label):
        if self.pass2:
            sem, v = self.cks[label]
            eng.wait_ge(self.sems[sem], v)

    def wait_val(self, eng, sem, v):
        if self.pass2:
            eng.wait_ge(self.sems[sem], v)


class Mock:
    def __getattr__(self, name):
        return lambda *a, **k: None


_CHAIN_OPS = {
    "tensor_copy", "tensor_tensor", "tensor_scalar", "tensor_single_scalar",
    "tensor_reduce", "reduce_max", "reduce_sum", "scalar_tensor_tensor",
    "reciprocal", "memset", "memzero", "activation", "copy", "add", "mul",
    "square", "sqrt", "sign", "tensor_add", "tensor_sub", "tensor_mul",
    "tensor_max", "tensor_scalar_add", "tensor_scalar_sub", "tensor_scalar_mul",
    "tensor_scalar_max", "tensor_scalar_min", "transpose", "select",
}


class Chain:

    def __init__(self, eng, C, sem):
        self._eng = eng
        self._C = C
        self._sem = sem

    def __getattr__(self, name):
        eng, C, sem = self._eng, self._C, self._sem
        if name in _CHAIN_OPS:
            def op(*a, **k):
                C.wait_val(eng, sem, C.cnt[sem])
                instr = getattr(eng, name)(*a, **k)
                C.cnt[sem] += 1
                if C.pass2:
                    instr.then_inc(C.sems[sem], 1)
                return instr
            return op
        if name == "dma_start":
            def dma(*a, **k):
                C.wait_val(eng, sem, C.cnt[sem])
                return eng.dma_start(*a, **k)
            return dma
        return getattr(eng, name)


def build_program(debug=False):
    nc = bass.Bass("TRN2", target_bir_lowering=False)
    C = Ctx()

    xT = nc.dram_tensor("xT", [E, T], F32, kind="ExternalInput").ap()
    wqt = nc.dram_tensor("wqt", [E, D], F32, kind="ExternalInput").ap()
    wkt = nc.dram_tensor("wkt", [E, D], F32, kind="ExternalInput").ap()
    wvt = nc.dram_tensor("wvt", [E, D], F32, kind="ExternalInput").ap()
    wot = nc.dram_tensor("wot", [D, E], F32, kind="ExternalInput").ap()
    wg_nat = nc.dram_tensor("wg_nat", [I, 128], F32, kind="ExternalInput").ap()
    wu_nat = nc.dram_tensor("wu_nat", [I, 128], F32, kind="ExternalInput").ap()
    wd_nat = nc.dram_tensor("wd_nat", [E, 384], F32, kind="ExternalInput").ap()
    onesf = nc.dram_tensor("onesf", [128, 128], F32, kind="ExternalInput").ap()
    onesb = nc.dram_tensor("onesb", [128, 1], BF16, kind="ExternalInput").ap()
    identb = nc.dram_tensor("identb", [128, 128], BF16, kind="ExternalInput").ap()
    identf = nc.dram_tensor("identf", [128, 128], F32, kind="ExternalInput").ap()
    outT = nc.dram_tensor("outT", [E, T], F32, kind="ExternalOutput").ap()
    dbg = {}
    if debug:
        for n, sh, dt in [("dbg_cols", [128, 32], F32), ("dbg_misc", [1, 512], F32),
                          ("dbg_esum", [64, 64], F32), ("dbg_o0", [128, T], BF16),
                          ("dbg_up0", [128, T], BF16), ("dbg_attn0", [128, T], BF16),
                          ("dbg_kb", [128, TC * D], BF16), ("dbg_vb", [128, TC * D], BF16),
                          ("dbg_qt", [64, T], BF16), ("dbg_ctx", [64, T], BF16),
                          ("dbg_xq0", [128, T], BF16), ("dbg_dqT", [128, 1024], BF16),
                          ("dbg_sx", [1, 64], F32), ("dbg_xn0", [128, T], BF16),
                          ("dbg_trrow", [8, 128], F32), ("dbg_colsx", [128, 32], F32),
                          ("dbg_arout", [1, 256], F32), ("dbg_arbrow", [1, 128], F32),
                          ("dbg_bc", [128, 128], BF16),
                          ("dbg_arin1", [1, 128], F32), ("dbg_arout1", [1, 128], F32)]:
            dbg[n] = nc.dram_tensor(n, sh, dt, kind="ExternalOutput").ap()

    ag_in = {
        "g": nc.dram_tensor("ag_wg_in", [128, I], BF16),
        "u": nc.dram_tensor("ag_wu_in", [128, I], BF16),
        "d": nc.dram_tensor("ag_wd_in", [384, E], BF16),
    }
    ag_out = {
        "g": nc.dram_tensor("ag_wg_out", [E, I], BF16),
        "u": nc.dram_tensor("ag_wu_out", [E, I], BF16),
        "d": nc.dram_tensor("ag_wd_out", [I, E], BF16),
    }
    ag_e_in = nc.dram_tensor("ag_e_in", [D, D], F32)
    ag_e_out = nc.dram_tensor("ag_e_out", [D, D], F32)
    ar_in = [nc.dram_tensor(f"ar{j}_in", [1, 256], F32) for j in range(4)]
    ar_out = [nc.dram_tensor(f"ar{j}_out", [1, 256], F32) for j in range(4)]
    up_dram = nc.dram_tensor("up_dram", [I, T], BF16)
    attn_dram = nc.dram_tensor("attn_dram", [E, T], BF16)
    flat_dram = nc.dram_tensor("flat_dram", [4, 128], F32)

    st = contextlib.ExitStack()
    sb = lambda name, shape, dt=F32: st.enter_context(nc.sbuf_tensor(name, shape, dt)).ap()

    AR = sb("AR", [128, 49152], BF16)
    ARf = AR.bitcast(F32)
    x_ch = [ARf[:, 2048 * k:2048 * (k + 1)] for k in range(EC)]
    gt = [AR[:, 2048 * i:2048 * (i + 1)] for i in range(IC)]
    dq_nat = ARf[:, 16384:19456]
    dq_T = AR[:, 41984:45056]

    xn = sb("xn", [128, T * EC], BF16)
    xn_ch = [xn[:, 2048 * k:2048 * (k + 1)] for k in range(EC)]
    xq_ch = xn_ch
    xpref = xn.bitcast(F32)
    xpref_sl = [xpref[:, 2048 * j:2048 * (j + 1)] for j in range(2)]
    xsc_sl = [xpref[:, 4096:6144], xpref[:, 6144:8192]]

    SC = sb("SC", [128, 6144], F32)
    xnf = None
    SCt = SC[:, 0:2048]
    SCu = SC[:, 2048:4096]
    SCv = SC[:, 4096:6144]
    Rb = SCt
    Qb = SCu
    xtmp = SCv
    outstage = [SC[:, 0:2048], SC[:, 2048:4096]]
    x2b_sl = [SC.bitcast(BF16)[:, 8192:10240], SC.bitcast(BF16)[:, 10240:12288]]

    upst = sb("upst", [128, 4096], BF16)
    upst_sl = [upst[:, 2048 * j:2048 * (j + 1)] for j in range(2)]
    ctxT = sb("ctxT", [64, T], BF16)
    QTb = sb("QTb", [64, T], BF16)
    wstr = sb("wstr", [128, 2048], BF16)
    wstr_sl = [wstr[:, 1024 * j:1024 * (j + 1)] for j in range(2)]
    wstrD = sb("wstrD", [128, 6144], BF16)
    wstrD_sl = [wstrD[:, 3072 * j:3072 * (j + 1)] for j in range(2)]
    astage = sb("astage", [128, 4096], BF16)
    astage_sl = [astage[:, 2048 * j:2048 * (j + 1)] for j in range(2)]
    astageF = astage.bitcast(F32)
    Kb = sb("Kb", [128, TC * D], BF16)
    Vb = sb("Vb", [128, TC * D], BF16)

    onesf_sb = sb("onesf_sb", [128, 128], F32)
    onesb_sb = sb("onesb_sb", [128, 1], BF16)
    identb_sb = sb("identb_sb", [128, 128], BF16)
    identf_sb = sb("identf_sb", [128, 128], F32)
    wq_f = ARf[:, 22528:23040]
    wk_f = ARf[:, 23040:23552]
    wv_f = ARf[:, 23552:24064]
    wo_f = ARf[0:64, 19456:20480]
    wq_b = sb("wq_b", [128, 512], BF16)
    wk_b = sb("wk_b", [128, 512], BF16)
    wv_b = sb("wv_b", [128, 512], BF16)
    wo_b = sb("wo_b", [64, 1024], BF16)

    dqs = sb("dqs", [128, 288], F32)
    dq_bmax = dqs[:, 0:96]
    dq_s = dqs[:, 96:192]
    dq_rs = dqs[:, 192:288]

    cols = sb("cols", [128, 48], F32)
    maxcol = cols[:, 0:1]
    mctmp = cols[:, 1:2]
    gmaxcol = cols[:, 2:3]
    umaxcol = cols[:, 3:4]
    mmaxcol = cols[:, 4:5]
    rrmaxcol = cols[:, 5:6]
    Mcol = cols[:, 6:7]
    gtmp = cols[:, 7:8]
    negMcol = cols[:, 47:48]
    sx_col = cols[:, 8:9]
    rsx_col = cols[:, 9:10]
    sg_col = cols[:, 10:11]
    rsg_col = cols[:, 11:12]
    msg_col = cols[:, 12:13]
    rsu_col = cols[:, 13:14]
    cg_col = cols[:, 14:15]
    cr_col = cols[:, 15:16]
    so_col = cols[:, 16:17]
    rawc = cols[:, 17:18]
    raw2c = cols[:, 18:19]
    sg65_col = cols[:, 19:20]
    sg2_col = cols[:, 20:21]
    sup_col = cols[:, 21:22]
    tcol = cols[:, 22:23]
    pcol = cols[:, 23:24]
    rec8 = cols[:, 24:32]
    rec8b = cols[:, 32:40]
    rec8c = cols[:, 40:48]

    rows = SCv[0:1, :]
    sum_r = rows[0:1, 0:512]
    ssq_r = rows[0:1, 512:1024]
    rowa = rows[0:1, 1024:1536]
    rowb = rows[0:1, 1536:2048]
    misc = sb("misc", [1, 512], F32)
    srow = misc[0:1, 0:16]
    arrow = misc[0:1, 16:32]
    arb = misc[0:1, 32:48]
    arbrow = misc[0:1, 64:320]
    eps_sb = misc[0:1, 320:321]
    sclr = misc[0:1, 336:352]
    trrow = sb("trrow", [8, 256], F32)
    trsrc = sb("trsrc", [128, 8], F32)
    trsrc2 = sb("trsrc2", [128, 8], F32)
    e_loc = sb("e_loc", [64, 64], F32)
    e_sum = sb("e_sum", [64, 64], F32)
    e_bf = sb("e_bf", [64, 64], BF16)

    ps = [st.enter_context(nc.psum_tensor(f"psb{j}", [128, 512], F32)).ap()
          for j in range(8)]

    for name in ("dS", "dA", "dG", "CC", "tE", "vE", "sE", "gE"):
        C.sems[name] = st.enter_context(nc.semaphore(name))

    RG = [list(range(NCORES))]

    def recip_col(eng, dst_col, src_col):
        eng.tensor_copy(dq_bmax, src_col.broadcast_to([128, 96]))
        eng.reciprocal(dq_s, dq_bmax)
        return eng.tensor_copy(dst_col, dq_s[:, 0:1])


    def emit_sync(eng):
        for k in range(EC):
            C.inc(eng.dma_start(out=x_ch[k], in_=xT[128 * k:128 * (k + 1), :]), "dS", 16)
            C.ck(f"x{k}", "dS")
        for dst, src in ((wq_f, wqt), (wk_f, wkt), (wv_f, wvt)):
            C.inc(eng.dma_start(
                out=dst.rearrange("p (k d) -> p k d", d=D),
                in_=src.rearrange("(k p) d -> p k d", p=128)), "dS", 16)
        C.inc(eng.dma_start(out=wo_f[:, :], in_=wot), "dS", 16)
        C.inc(eng.dma_start(out=onesf_sb[:, :], in_=onesf), "dS", 16)
        C.inc(eng.dma_start(out=onesb_sb[:, :], in_=onesb), "dS", 16)
        C.inc(eng.dma_start(out=identb_sb[:, :], in_=identb), "dS", 16)
        C.inc(eng.dma_start(out=identf_sb[:, :], in_=identf), "dS", 16)
        C.ck("consts", "dS")
        wg_r = ag_out["g"].ap().rearrange("(k p) c -> p k c", p=128)
        for i in range(IC):
            C.wait(eng, "AGwg")
            if i >= 2:
                C.wait(eng, f"Amm{i - 2}")
            C.inc(eng.dma_start(
                out=wstr_sl[i % 2].rearrange("p (k c) -> p k c", c=128),
                in_=wg_r[:, :, 128 * i:128 * (i + 1)]), "dS", 16)
            C.ck(f"wA{i}", "dS")
        wu_r = ag_out["u"].ap().rearrange("(k p) c -> p k c", p=128)
        for i in range(IC):
            C.wait(eng, "AGwu")
            C.wait(eng, f"Bmm{i - 2}" if i >= 2 else "Amm23")
            C.inc(eng.dma_start(
                out=wstr_sl[i % 2].rearrange("p (k c) -> p k c", c=128),
                in_=wu_r[:, :, 128 * i:128 * (i + 1)]), "dS", 16)
            C.ck(f"wB{i}", "dS")
        wd_r = ag_out["d"].ap().rearrange("(i p) c -> p i c", p=128)
        for e in range(EC):
            C.wait(eng, "AGwd")
            if e >= 2:
                C.wait(eng, f"Dmm{e - 2}")
            C.inc(eng.dma_start(
                out=wstrD_sl[e % 2].rearrange("p (i c) -> p i c", c=128),
                in_=wd_r[:, :, 128 * e:128 * (e + 1)]), "dS", 16)
            C.ck(f"wD{e}", "dS")
            if e >= 2:
                C.wait(eng, f"oev{e - 2}")
                C.inc(eng.dma_start(out=outT[128 * (e - 2):128 * (e - 1), :],
                                    in_=outstage[e % 2]), "dS", 16)
                C.ck(f"out{e - 2}", "dS")
        for e in (6, 7):
            C.wait(eng, f"oev{e}")
            C.inc(eng.dma_start(out=outT[128 * e:128 * (e + 1), :],
                                in_=outstage[e % 2]), "dS", 16)
            C.ck(f"out{e}", "dS")
        C.wait(eng, "out7")

    def emit_gpsimd(eng):
        for k in range(EC):
            C.wait(eng, f"x{k}")
            C.inc(eng.dma_start(out=xn_ch[k], in_=x_ch[k]), "dG", 16)
            C.ck(f"xb{k}", "dG")
        C.wait(eng, "consts")
        C.inc(eng.dma_start(out=wq_b[:, :], in_=wq_f[:, :]), "dG", 16)
        C.inc(eng.dma_start(out=wk_b[:, :], in_=wk_f[:, :]), "dG", 16)
        C.inc(eng.dma_start(out=wv_b[:, :], in_=wv_f[:, :]), "dG", 16)
        C.inc(eng.dma_start(out=wo_b[:, :], in_=wo_f[:, :]), "dG", 16)
        C.ck("wcast", "dG")
        prev = None
        for wname in ("g", "u", "d"):
            if prev is not None:
                C.wait(eng, f"dqstage{prev}")
            if wname == "d":
                nat_in = wd_nat.rearrange("(t p) c -> p t c", p=128)
                nat_out = dq_nat.rearrange("p (t c) -> p t c", c=384)
            else:
                src = wg_nat if wname == "g" else wu_nat
                nat_in = src.rearrange("(t p) c -> p t c", p=128)
                nat_out = dq_nat.rearrange("p (t c) -> p t c", c=128)
            C.inc(eng.dma_start(out=nat_out, in_=nat_in), "dG", 16)
            C.ck(f"dqnat{wname}", "dG")
            C.wait(eng, f"dqT{wname}")
            if wname == "d":
                dTr = dq_T.rearrange("p (t j c) -> p j t c", j=3, c=128)
                for j in range(3):
                    C.inc(eng.dma_start(
                        out=ag_in["d"].ap()[128 * j:128 * (j + 1), :]
                            .rearrange("p (t c) -> p t c", c=128),
                        in_=dTr[:, j]), "dG", 16)
            else:
                C.inc(eng.dma_start(out=ag_in[wname][:, :], in_=dq_T), "dG", 16)
            C.ck(f"dqstage{wname}", "dG")
            if wname == "g":
                C.wait_val(eng, "dG", C.cnt["dG"])
                C.inc(eng.collective_compute(
                    "AllGather", OP.bypass, replica_groups=RG,
                    ins=[ag_in["g"].ap().opt()], outs=[ag_out["g"].ap().opt()]), "CC", 1)
                C.ck("AGwg", "CC")
            prev = wname

        C.wait(eng, "sxtrr")
        C.inc(eng.dma_start(out=ar_in[0][0:1, 0:128], in_=trrow[0:1, 0:128]), "dG", 16)
        C.wait_val(eng, "dG", C.cnt["dG"])
        C.inc(eng.collective_compute(
            "AllReduce", OP.max, replica_groups=RG,
            ins=[ar_in[0].ap().opt()], outs=[ar_out[0].ap().opt()]), "CC", 1)
        C.wait_val(eng, "CC", C.cnt["CC"])
        C.inc(eng.dma_start(out=arbrow[0:1, 0:128], in_=ar_out[0][0:1, 0:128]), "dG", 16)
        C.ck("sxback", "dG")
        if debug:
            C.wait(eng, "sxcols")
            C.inc(eng.dma_start(out=dbg["dbg_arout"], in_=ar_out[0][:, :]), "dG", 16)
            C.inc(eng.dma_start(out=dbg["dbg_bc"], in_=astage_sl[1][:, 0:128]), "dG", 16)
            C.inc(eng.dma_start(out=dbg["dbg_arbrow"], in_=arbrow[0:1, 0:128]), "dG", 16)
            C.inc(eng.dma_start(out=dbg["dbg_sx"], in_=misc[0:1, 0:64]), "dG", 16)
            C.inc(eng.dma_start(out=dbg["dbg_xn0"], in_=xn_ch[0]), "dG", 16)
            C.inc(eng.dma_start(out=dbg["dbg_trrow"], in_=trrow[:, 0:128]), "dG", 16)
            C.inc(eng.dma_start(out=dbg["dbg_colsx"], in_=cols[:, 0:32]), "dG", 16)
            C.ck("dbgsx", "dG")

        C.wait(eng, "eloc")
        C.inc(eng.dma_start(out=ag_e_in[:, :], in_=e_loc), "dG", 16)
        C.wait_val(eng, "dG", C.cnt["dG"])
        C.inc(eng.collective_compute(
            "AllReduce", OP.add, replica_groups=RG,
            ins=[ag_e_in.ap().opt()], outs=[ag_e_out.ap().opt()]), "CC", 1)
        C.wait_val(eng, "CC", C.cnt["CC"])
        C.inc(eng.dma_start(out=e_sum[:, :], in_=ag_e_out[:, :]), "dG", 16)
        C.ck("eback", "dG")
        C.wait_val(eng, "dG", C.cnt["dG"])
        C.inc(eng.collective_compute(
            "AllGather", OP.bypass, replica_groups=RG,
            ins=[ag_in["u"].ap().opt()], outs=[ag_out["u"].ap().opt()]), "CC", 1)
        C.ck("AGwu", "CC")
        C.inc(eng.collective_compute(
            "AllGather", OP.bypass, replica_groups=RG,
            ins=[ag_in["d"].ap().opt()], outs=[ag_out["d"].ap().opt()]), "CC", 1)
        C.ck("AGwd", "CC")

        C.wait(eng, "sgtrr")
        C.inc(eng.dma_start(out=ar_in[1][0:1, 0:128], in_=trrow[0:1, 0:128]), "dG", 16)
        C.wait_val(eng, "dG", C.cnt["dG"])
        if debug:
            C.inc(eng.dma_start(out=dbg["dbg_arin1"], in_=ar_in[1][0:1, 0:128]), "dG", 16)
            C.wait_val(eng, "dG", C.cnt["dG"])
        C.inc(eng.collective_compute(
            "AllReduce", OP.max, replica_groups=RG,
            ins=[ar_in[1].ap().opt()], outs=[ar_out[1].ap().opt()]), "CC", 1)
        C.wait_val(eng, "CC", C.cnt["CC"])
        C.inc(eng.dma_start(out=arbrow[0:1, 0:128], in_=ar_out[1][0:1, 0:128]), "dG", 16)
        if debug:
            C.inc(eng.dma_start(out=dbg["dbg_arout1"], in_=ar_out[1][0:1, 0:128]), "dG", 16)
        C.ck("sgback", "dG")

        for i in range(1, IC, 2):
            C.wait(eng, f"actB{i}")
            C.inc(eng.dma_start(out=up_dram[128 * i:128 * (i + 1), :],
                                in_=upst_sl[1]), "dG", 16)
            C.ck(f"upsp{i}", "dG")

        C.wait(eng, "sutrr")
        C.inc(eng.dma_start(out=ar_in[2][0:1, 0:256], in_=trrow[0:1, 0:256]), "dG", 16)
        C.wait_val(eng, "dG", C.cnt["dG"])
        C.inc(eng.collective_compute(
            "AllReduce", OP.max, replica_groups=RG,
            ins=[ar_in[2].ap().opt()], outs=[ar_out[2].ap().opt()]), "CC", 1)
        C.wait_val(eng, "CC", C.cnt["CC"])
        C.inc(eng.dma_start(out=arbrow[0:1, 0:256], in_=ar_out[2][0:1, 0:256]), "dG", 16)
        C.ck("suback", "dG")

        C.wait(eng, "sucols")
        C.wait(eng, "upsp23")
        C.inc(eng.dma_start(out=upst_sl[1], in_=up_dram[128:256, :]), "dG", 16)
        C.ck("upldG1", "dG")
        for j in range(3, IC, 2):
            C.wait(eng, f"C2t6_{j - 2}")
            C.inc(eng.dma_start(out=upst_sl[1],
                                in_=up_dram[128 * j:128 * (j + 1), :]), "dG", 16)
            C.ck(f"upldG{j}", "dG")

        C.wait(eng, "sotrr")
        C.inc(eng.dma_start(out=ar_in[3][0:1, 0:128], in_=trrow[0:1, 0:128]), "dG", 16)
        C.wait_val(eng, "dG", C.cnt["dG"])
        C.inc(eng.collective_compute(
            "AllReduce", OP.max, replica_groups=RG,
            ins=[ar_in[3].ap().opt()], outs=[ar_out[3].ap().opt()]), "CC", 1)
        C.wait_val(eng, "CC", C.cnt["CC"])
        C.inc(eng.dma_start(out=arbrow[0:1, 0:128], in_=ar_out[3][0:1, 0:128]), "dG", 16)
        C.ck("soback", "dG")

        for e in range(EC):
            C.wait(eng, f"Dev1_{e}")
            C.wait(eng, f"apref{e}")
            for nb in range(NB):
                ti = eng.tensor_tensor(
                    outstage[e % 2][:, 512 * nb:512 * (nb + 1)],
                    outstage[e % 2][:, 512 * nb:512 * (nb + 1)],
                    astage_sl[e % 2][:, 512 * nb:512 * (nb + 1)], OP.add)
            C.inc(ti, "gE", 1)
            C.ck(f"oev{e}", "gE")
        C.wait(eng, "out7")
        if debug:
            C.inc(eng.dma_start(out=dbg["dbg_cols"], in_=cols[:, 0:32]), "dG", 16)
            C.inc(eng.dma_start(out=dbg["dbg_misc"], in_=misc), "dG", 16)
            C.inc(eng.dma_start(out=dbg["dbg_esum"], in_=e_sum[:, :]), "dG", 16)
            C.inc(eng.dma_start(out=dbg["dbg_o0"], in_=gt[0]), "dG", 16)
            C.inc(eng.dma_start(out=dbg["dbg_up0"], in_=up_dram[0:128, :]), "dG", 16)
            C.inc(eng.dma_start(out=dbg["dbg_attn0"], in_=attn_dram[0:128, :]), "dG", 16)
            C.inc(eng.dma_start(out=dbg["dbg_kb"], in_=Kb), "dG", 16)
            C.inc(eng.dma_start(out=dbg["dbg_vb"], in_=Vb), "dG", 16)
            C.inc(eng.dma_start(out=dbg["dbg_qt"], in_=QTb), "dG", 16)
            C.inc(eng.dma_start(out=dbg["dbg_ctx"], in_=ctxT), "dG", 16)
            C.inc(eng.dma_start(out=dbg["dbg_xq0"], in_=xn_ch[0]), "dG", 16)
            C.inc(eng.dma_start(out=dbg["dbg_dqT"], in_=ag_out["d"].ap()[0:128, :]), "dG", 16)
            C.wait_val(eng, "dG", C.cnt["dG"])

    def emit_tensor(eng):
        C.wait(eng, "consts")
        for wi, wname in enumerate(("g", "u", "d")):
            C.wait(eng, f"dq{wname}bf")
            for b in range(6):
                gi = wi * 6 + b
                if gi >= 2:
                    C.wait(eng, f"dqev{gi - 2}")
                for j in range(4):
                    t = b * 4 + j
                    ti = eng.transpose(ps[gi % 2][:, 128 * j:128 * (j + 1)],
                                       dq_nat[:, 128 * t:128 * (t + 1)],
                                       identf_sb[:, :])
                C.inc(ti, "tE", 1)
                C.ck(f"dqtr{gi}", "tE")
        C.wait(eng, "dqev17")
        for k in range(EC):
            C.wait(eng, f"xb{k}")
            C.wait(eng, f"x2_{k}")
            for nb in range(NB):
                eng.matmul(ps[nb][0:1, :], onesb_sb[:, 0:1],
                           xn_ch[k][:, 512 * nb:512 * (nb + 1)],
                           start=(k == 0), stop=(k == EC - 1))
            for nb in range(NB):
                ti = eng.matmul(ps[4 + nb][0:1, :], onesb_sb[:, 0:1],
                                x2b_sl[k % 2][:, 512 * nb:512 * (nb + 1)],
                                start=(k == 0), stop=(k == EC - 1))
            C.inc(ti, "tE", 1)
            C.ck(f"sums{k}", "tE")
        for nb in range(NB):
            C.wait(eng, f"rqrow{nb}")
            eng.matmul(ps[nb][:, :], onesf_sb[0:1, :], rowa, start=True, stop=True)
            ti = eng.matmul(ps[4 + nb][:, :], onesf_sb[0:1, :], rowb,
                            start=True, stop=True)
            C.inc(ti, "tE", 1)
            C.ck(f"rqmm{nb}", "tE")
        C.wait(eng, "xn7")
        C.wait(eng, "rqev")
        C.wait(eng, "wcast")
        for nb in range(NB):
            for k in range(EC):
                ti = eng.matmul(ps[nb][0:64, :], wq_b[:, 64 * k:64 * (k + 1)],
                                xn_ch[k][:, 512 * nb:512 * (nb + 1)],
                                start=(k == 0), stop=(k == EC - 1))
            C.inc(ti, "tE", 1)
            C.ck(f"qmm{nb}", "tE")
        for c in range(TC):
            if c >= 2:
                C.wait(eng, f"kvev{c - 2}")
            bk, bv = 4 + (c % 2) * 2, 5 + (c % 2) * 2
            for k in range(EC):
                eng.matmul(ps[bk][:, 0:64], xn_ch[k][:, 128 * c:128 * (c + 1)],
                           wk_b[:, 64 * k:64 * (k + 1)],
                           start=(k == 0), stop=(k == EC - 1))
                ti = eng.matmul(ps[bv][:, 0:64], xn_ch[k][:, 128 * c:128 * (c + 1)],
                                wv_b[:, 64 * k:64 * (k + 1)],
                                start=(k == 0), stop=(k == EC - 1))
            C.inc(ti, "tE", 1)
            C.ck(f"kvmm{c}", "tE")
        C.wait(eng, "qev")
        C.wait(eng, "sxcol")
        ti = eng.transpose(ps[0][0:8, 0:128], trsrc[:, :], identf_sb[:, :])
        C.inc(ti, "tE", 1)
        C.ck("sxtr", "tE")
        C.wait(eng, "sxback")
        ti = eng.matmul(ps[0][:, 0:128], onesf_sb[0:1, :], arbrow[0:1, 0:128],
                        start=True, stop=True)
        C.inc(ti, "tE", 1)
        C.ck("sxbc", "tE")
        C.wait(eng, "kvev15")
        for c in range(TC):
            ti = eng.matmul(ps[1][0:64, 0:64], Kb[:, 64 * c:64 * (c + 1)],
                            Vb[:, 64 * c:64 * (c + 1)],
                            start=(c == 0), stop=(c == TC - 1))
        C.inc(ti, "tE", 1)
        C.ck("emm", "tE")
        for i in range(IC):
            grp = (i % 2) * 4
            C.wait(eng, f"wA{i}")
            if i >= 2:
                C.wait(eng, f"actA{i - 2}")
            for k in range(EC):
                if i < 3:
                    C.wait(eng, f"xq{k}")
                for nb in range(NB):
                    ti = eng.matmul(ps[grp + nb][:, :],
                                    wstr_sl[i % 2][:, 128 * k:128 * (k + 1)],
                                    xq_ch[k][:, 512 * nb:512 * (nb + 1)],
                                    start=(k == 0), stop=(k == EC - 1))
            C.inc(ti, "tE", 1)
            C.ck(f"Amm{i}", "tE")
        C.wait(eng, "actA22")
        C.wait(eng, "gmaxcol")
        ti = eng.transpose(ps[0][0:8, 0:128], trsrc[:, :], identf_sb[:, :])
        C.inc(ti, "tE", 1)
        C.ck("sgtr", "tE")
        for i in range(IC):
            grp = (i % 2) * 4
            if i == 3:
                C.wait(eng, "sgback")
                C.wait(eng, "actB2")
                ti = eng.matmul(ps[0][:, 0:128], onesf_sb[0:1, :],
                                arbrow[0:1, 0:128], start=True, stop=True)
                C.inc(ti, "tE", 1)
                C.ck("sgbc", "tE")
            C.wait(eng, f"wB{i}")
            if i == 4:
                C.wait(eng, "sgcols")
            if i >= 2:
                C.wait(eng, f"actB{i - 2}")
            elif i == 0:
                C.wait(eng, "sgtrr")
            else:
                C.wait(eng, "actA23")
            for k in range(EC):
                for nb in range(NB):
                    ti = eng.matmul(ps[grp + nb][:, :],
                                    wstr_sl[i % 2][:, 128 * k:128 * (k + 1)],
                                    xq_ch[k][:, 512 * nb:512 * (nb + 1)],
                                    start=(k == 0), stop=(k == EC - 1))
            C.inc(ti, "tE", 1)
            C.ck(f"Bmm{i}", "tE")
        C.wait(eng, "umaxcol")
        C.wait(eng, "mmaxcol")
        C.wait(eng, "actB22")
        eng.transpose(ps[0][0:8, 0:128], trsrc[:, :], identf_sb[:, :])
        ti = eng.transpose(ps[1][0:8, 0:128], trsrc2[:, :], identf_sb[:, :])
        C.inc(ti, "tE", 1)
        C.ck("sutr", "tE")
        C.wait(eng, "ebf")
        C.wait(eng, "sxcols")
        C.wait(eng, "eev")
        C.wait(eng, "actB23")
        for nb in range(NB):
            ti = eng.matmul(ps[2 + nb][0:64, :], e_bf[:, :],
                            QTb[:, 512 * nb:512 * (nb + 1)], start=True, stop=True)
        C.inc(ti, "tE", 1)
        C.ck("cmmall", "tE")
        for e in range(EC):
            grp = 4 if e % 2 else 0
            C.wait(eng, f"aev{e - 2}" if e >= 2 else "cev")
            if e == 0:
                C.wait(eng, "sutrr")
            for nb in range(NB):
                ti = eng.matmul(ps[grp + nb][:, :], wo_b[:, 128 * e:128 * (e + 1)],
                                ctxT[:, 512 * nb:512 * (nb + 1)], start=True, stop=True)
            C.inc(ti, "tE", 1)
            C.ck(f"amm{e}", "tE")
        C.wait(eng, "suback")
        C.wait(eng, "aev6")
        C.wait(eng, "aev7")
        eng.matmul(ps[2][:, 0:128], onesf_sb[0:1, :], arbrow[0:1, 0:128],
                   start=True, stop=True)
        ti = eng.matmul(ps[3][:, 0:128], onesf_sb[0:1, :], arbrow[0:1, 128:256],
                        start=True, stop=True)
        C.inc(ti, "tE", 1)
        C.ck("subc", "tE")
        C.wait(eng, "sucols")
        C.wait(eng, "rrmaxcol")
        ti = eng.transpose(ps[0][0:8, 0:128], trsrc[:, :], identf_sb[:, :])
        C.inc(ti, "tE", 1)
        C.ck("sotr", "tE")
        C.wait(eng, "soback")
        ti = eng.matmul(ps[2][:, 0:128], onesf_sb[0:1, :], arbrow[0:1, 0:128],
                        start=True, stop=True)
        C.inc(ti, "tE", 1)
        C.ck("sobc", "tE")
        for e in range(EC):
            grp = (e % 2) * 4
            C.wait(eng, f"wD{e}")
            if e >= 2:
                C.wait(eng, f"Dev1_{e - 2}")
            else:
                C.wait(eng, f"actB{22 + e}")
            for i in range(IC):
                if e == 0:
                    C.wait(eng, f"o{i}")
                for nb in range(NB):
                    ti = eng.matmul(ps[grp + nb][:, :],
                                    wstrD_sl[e % 2][:, 128 * i:128 * (i + 1)],
                                    gt[i][:, 512 * nb:512 * (nb + 1)],
                                    start=(i == 0), stop=(i == IC - 1))
            C.inc(ti, "tE", 1)
            C.ck(f"Dmm{e}", "tE")

    def emit_vector(eng):
        eng.memset(eps_sb, LNEPS)
        eng.memset(Mcol, MAGIC)
        ti = eng.memset(negMcol, -MAGIC)
        C.inc(ti, "vE", 1)
        C.ck("memset", "vE")
        prev = None
        for wname in ("g", "u", "d"):
            C.wait(eng, f"dqnat{wname}")
            if prev is not None:
                C.wait(eng, f"dqstage{prev}")
            nat3 = dq_nat.rearrange("p (b c) -> p b c", c=32)
            eng.tensor_reduce(dq_bmax, nat3, AX.X, OP.max, apply_absolute_value=True)
            eng.tensor_scalar(dq_s, dq_bmax, 1.0 / QW, SEPS, OP.mult, OP.max)
            eng.reciprocal(dq_rs, dq_s)
            eng.tensor_tensor(nat3, nat3,
                              dq_rs[:, :, None].broadcast_to([128, 96, 32]), OP.mult)
            eng.tensor_scalar(dq_nat, dq_nat, MAGIC, MAGIC, OP.add, OP.subtract)
            ti = eng.tensor_tensor(nat3, nat3,
                                   dq_s[:, :, None].broadcast_to([128, 96, 32]), OP.mult)
            C.inc(ti, "vE", 1)
            C.ck(f"dq{wname}bf", "vE")
            prev = wname

        C.wait(eng, f"sums{EC - 1}")
        for nb in range(NB):
            if nb >= 1:
                C.wait(eng, f"rqmm{nb - 1}")
            eng.tensor_copy(sum_r, ps[nb][0:1, :])
            eng.tensor_copy(ssq_r, ps[4 + nb][0:1, :])
            eng.tensor_scalar(rowa, sum_r, 1.0 / E, None, OP.mult, OP.bypass)
            eng.tensor_tensor(rowb, rowa, rowa, OP.mult)
            eng.tensor_scalar(rowa, ssq_r, 1.0 / E, None, OP.mult, OP.bypass)
            ti = eng.tensor_tensor(rowb, rowa, rowb, OP.subtract)
            C.inc(ti, "vE", 1)
            C.ck(f"var{nb}", "vE")
            C.wait(eng, f"sqrt{nb}")
            eng.reciprocal(rowa, rowb)
            ti = eng.scalar_tensor_tensor(rowb, sum_r, 1.0 / E, rowa,
                                          OP.mult, OP.mult)
            C.inc(ti, "vE", 1)
            C.ck(f"rqrow{nb}", "vE")
        for nb in range(NB):
            C.wait(eng, f"rqmm{nb}")
            eng.tensor_copy(Rb[:, 512 * nb:512 * (nb + 1)], ps[nb][:, :])
            ti = eng.tensor_copy(Qb[:, 512 * nb:512 * (nb + 1)], ps[4 + nb][:, :])
        C.inc(ti, "vE", 1)
        C.ck("rqev", "vE")
        for k in range(EC):
            eng.tensor_tensor(xtmp, x_ch[k], Rb, OP.mult)
            ti = eng.tensor_tensor(xn_ch[k], xtmp, Qb, OP.subtract)
            C.inc(ti, "vE", 1)
            C.ck(f"xn{k}", "vE")
            eng.tensor_reduce(mctmp, xn_ch[k], AX.X, OP.max, apply_absolute_value=True)
            ti = (eng.tensor_tensor(maxcol, maxcol, mctmp, OP.max) if k else
                  eng.tensor_copy(maxcol, mctmp))
            C.inc(ti, "vE", 1)
        ti = eng.tensor_copy(trsrc[:, 0:1], maxcol)
        C.inc(ti, "vE", 1)
        C.ck("sxcol", "vE")
        C.wait(eng, "sxtr")
        ti = eng.tensor_copy(trrow[:, 0:128], ps[0][0:8, 0:128])
        C.inc(ti, "vE", 1)
        C.ck("sxtrr", "vE")
        C.wait(eng, "sxbc")
        eng.tensor_reduce(rawc, ps[0][:, 0:128], AX.X, OP.max)
        if debug:
            eng.tensor_copy(astage_sl[1][:, 0:128], ps[0][:, 0:128])
        eng.tensor_scalar(sx_col, rawc, 1.0 / QA, SEPS, OP.mult, OP.max)
        ti = recip_col(eng, rsx_col, sx_col)
        C.inc(ti, "vE", 1)
        C.ck("sxcols", "vE")
        C.wait(eng, "kvmm15")
        if debug:
            C.wait(eng, "dbgsx")
        for k in range(EC):
            eng.tensor_scalar(SCt, xn_ch[k], rsx_col, MAGIC, OP.mult, OP.add)
            ti = eng.tensor_scalar(xq_ch[k], SCt, MAGIC, sx_col, OP.subtract, OP.mult)
            C.inc(ti, "vE", 1)
            C.ck(f"xq{k}", "vE")
        C.wait(eng, "eback")
        ti = eng.tensor_scalar(e_bf[:, :], e_sum[:, :], C_ATTN, None, OP.mult, OP.bypass)
        C.inc(ti, "vE", 1)
        C.ck("ebf", "vE")
        for i in range(IC):
            C.wait(eng, f"actA{i}")
            eng.tensor_reduce(mctmp, gt[i], AX.X, OP.max, apply_absolute_value=True)
            ti = (eng.tensor_tensor(gmaxcol, gmaxcol, mctmp, OP.max) if i else
                  eng.tensor_copy(gmaxcol, mctmp))
            C.inc(ti, "vE", 1)
        ti = eng.tensor_copy(trsrc[:, 0:1], gmaxcol)
        C.inc(ti, "vE", 1)
        C.ck("gmaxcol", "vE")
        C.wait(eng, "sgtr")
        ti = eng.tensor_copy(trrow[:, 0:128], ps[0][0:8, 0:128])
        C.inc(ti, "vE", 1)
        C.ck("sgtrr", "vE")
        def vec_c1(j):
            C.wait(eng, f"c1r{j}")
            tb = SCt if j % 2 == 0 else SCv
            ub = SCu if j % 2 == 0 else astageF
            eng.tensor_scalar(ub, ub, MAGIC + SIGMAX, MAGIC, OP.min, OP.subtract)
            ti = eng.scalar_tensor_tensor(gt[j], tb, -MAGIC, ub, OP.add, OP.mult)
            C.inc(ti, "vE", 1)
            C.ck(f"C1m{j}", "vE")
            eng.tensor_reduce(mctmp, gt[j], AX.X, OP.max, apply_absolute_value=True)
            ti = (eng.tensor_tensor(mmaxcol, mmaxcol, mctmp, OP.max) if j else
                  eng.tensor_copy(mmaxcol, mctmp))
            C.inc(ti, "vE", 1)

        for i in range(IC):
            if i == 2:
                C.wait(eng, "sgbc")
                eng.tensor_reduce(rawc, ps[0][:, 0:128], AX.X, OP.max)
                eng.tensor_scalar(sg_col, rawc, 1.0 / QA, SEPS, OP.mult, OP.max)
                recip_col(eng, rsg_col, sg_col)
                eng.tensor_scalar(sg65_col, sg_col, 1.0 / SIG, None,
                                  OP.mult, OP.bypass)
                ti = eng.tensor_scalar(msg_col, sg_col, -MAGIC, None,
                                       OP.mult, OP.bypass)
                C.inc(ti, "vE", 1)
                C.ck("sgcols", "vE")
            C.wait(eng, f"actB{i}")
            eng.tensor_reduce(mctmp, upst_sl[i % 2], AX.X, OP.max,
                              apply_absolute_value=True)
            ti = (eng.tensor_tensor(umaxcol, umaxcol, mctmp, OP.max) if i else
                  eng.tensor_copy(umaxcol, mctmp))
            C.inc(ti, "vE", 1)
            C.ck(f"umax{i}", "vE")
            if i >= 2:
                vec_c1(i - 2)
        for j in range(IC - 2, IC):
            vec_c1(j)
        eng.tensor_copy(trsrc[:, 0:1], umaxcol)
        ti = eng.tensor_copy(trsrc2[:, 0:1], mmaxcol)
        C.inc(ti, "vE", 1)
        C.ck("umaxcol", "vE")
        C.ck("mmaxcol", "vE")
        C.wait(eng, "sutr")
        eng.tensor_copy(trrow[:, 0:128], ps[0][0:8, 0:128])
        ti = eng.tensor_copy(trrow[:, 128:256], ps[1][0:8, 0:128])
        C.inc(ti, "vE", 1)
        C.ck("sutrr", "vE")
        for e in range(1, EC, 2):
            C.wait(eng, f"amm{e}")
            if e >= 3:
                C.wait(eng, f"asp{e - 2}")
            for nb in range(NB):
                ti = eng.tensor_copy(astage_sl[1][:, 512 * nb:512 * (nb + 1)],
                                     ps[4 + nb][:, :])
            C.inc(ti, "vE", 1)
            C.ck(f"aev{e}", "vE")
        C.wait(eng, "subc")
        eng.tensor_reduce(rawc, ps[2][:, 0:128], AX.X, OP.max)
        eng.tensor_reduce(raw2c, ps[3][:, 0:128], AX.X, OP.max)
        eng.tensor_scalar(sup_col, rawc, 1.0 / QA, SEPS, OP.mult, OP.max)
        recip_col(eng, rsu_col, sup_col)
        eng.tensor_tensor(tcol, raw2c, sg65_col, OP.mult)
        eng.tensor_scalar(sg2_col, tcol, 1.0 / QA, SEPS, OP.mult, OP.max)
        recip_col(eng, tcol, sg2_col)
        ti = eng.tensor_tensor(cg_col, sg65_col, tcol, OP.mult)
        C.inc(ti, "vE", 1)
        C.ck("sucols", "vE")
        for i in range(IC):
            buf = SCu if i % 2 == 0 else SCv
            t5b = SCt if i % 2 == 0 else xsc_sl[0]
            if i % 2 == 0:
                C.wait(eng, f"C2t6_{i}")
                eng.tensor_scalar(buf, buf, MAGIC, None, OP.subtract, OP.bypass)
            else:
                C.wait(eng, f"C2g{i}")
            C.wait(eng, f"C2t5_{i}")
            ti = eng.scalar_tensor_tensor(gt[i], t5b, -MAGIC, buf, OP.add, OP.mult)
            C.inc(ti, "vE", 1)
            C.ck(f"C2rr{i}", "vE")
            eng.tensor_reduce(mctmp, gt[i], AX.X, OP.max, apply_absolute_value=True)
            ti = (eng.tensor_tensor(rrmaxcol, rrmaxcol, mctmp, OP.max) if i else
                  eng.tensor_copy(rrmaxcol, mctmp))
            C.inc(ti, "vE", 1)
        ti = eng.tensor_copy(trsrc[:, 0:1], rrmaxcol)
        C.inc(ti, "vE", 1)
        C.ck("rrmaxcol", "vE")
        C.wait(eng, "sotr")
        ti = eng.tensor_copy(trrow[:, 0:128], ps[0][0:8, 0:128])
        C.inc(ti, "vE", 1)
        C.ck("sotrr", "vE")
        C.wait(eng, "sobc")
        eng.tensor_reduce(rawc, ps[2][:, 0:128], AX.X, OP.max)
        eng.tensor_tensor(pcol, sg2_col, sup_col, OP.mult)
        eng.tensor_tensor(tcol, rawc, pcol, OP.mult)
        eng.tensor_scalar(so_col, tcol, 1.0 / QA, SEPS, OP.mult, OP.max)
        recip_col(eng, tcol, so_col)
        ti = eng.tensor_tensor(cr_col, pcol, tcol, OP.mult)
        C.inc(ti, "vE", 1)
        C.ck("socols", "vE")
        C.ck("somax", "vE")
        for i in range(IC):
            if i % 2 == 0:
                C.wait(eng, f"ot7_{i}")
                t7b = xsc_sl[(i % 4) // 2]
                ti = eng.tensor_scalar(gt[i], t7b, MAGIC, so_col,
                                       OP.subtract, OP.mult)
            else:
                eng.tensor_scalar(SCt, gt[i], cr_col, MAGIC, OP.mult, OP.add)
                ti = eng.tensor_scalar(gt[i], SCt, MAGIC, so_col,
                                       OP.subtract, OP.mult)
            C.inc(ti, "vE", 1)
            C.ck(f"o{i}", "vE")
        for e in range(EC):
            C.wait(eng, f"Dmm{e}")
            C.wait(eng, f"xpref{e}")
            if e >= 2:
                C.wait(eng, f"out{e - 2}")
            grp = (e % 2) * 4
            for nb in range(NB):
                ti = eng.tensor_tensor(outstage[e % 2][:, 512 * nb:512 * (nb + 1)],
                                       ps[grp + nb][:, :],
                                       xpref_sl[e % 2][:, 512 * nb:512 * (nb + 1)],
                                       OP.add)
            C.inc(ti, "vE", 1)
            C.ck(f"Dev1_{e}", "vE")

    def emit_scalar(eng):
        for wi, wname in enumerate(("g", "u", "d")):
            for b in range(6):
                gi = wi * 6 + b
                C.wait(eng, f"dqtr{gi}")
                ti = eng.copy(dq_T[:, 512 * b:512 * (b + 1)], ps[gi % 2][:, :])
                C.inc(ti, "sE", 1)
                C.ck(f"dqev{gi}", "sE")
            C.ck(f"dqT{wname}", "sE")
        for k in range(EC):
            C.wait(eng, f"x{k}")
            if k >= 2:
                C.wait(eng, f"sums{k - 2}")
            ti = eng.activation(x2b_sl[k % 2], x_ch[k], AF.Square)
            C.inc(ti, "sE", 1)
            C.ck(f"x2_{k}", "sE")
        C.wait(eng, "memset")
        for nb in range(NB):
            C.wait(eng, f"var{nb}")
            ti = eng.activation(rowb, rowb, AF.Sqrt, bias=eps_sb, scale=1.0)
            C.inc(ti, "sE", 1)
            C.ck(f"sqrt{nb}", "sE")
        for nb in range(NB):
            C.wait(eng, f"qmm{nb}")
            ti = eng.copy(QTb[:, 512 * nb:512 * (nb + 1)], ps[nb][0:64, :])
        C.inc(ti, "sE", 1)
        C.ck("qev", "sE")
        for c in range(TC):
            C.wait(eng, f"kvmm{c}")
            bk, bv = 4 + (c % 2) * 2, 5 + (c % 2) * 2
            eng.copy(Kb[:, 64 * c:64 * (c + 1)], ps[bk][:, 0:64])
            ti = eng.copy(Vb[:, 64 * c:64 * (c + 1)], ps[bv][:, 0:64])
            C.inc(ti, "sE", 1)
            C.ck(f"kvev{c}", "sE")
        C.wait(eng, "emm")
        ti = eng.copy(e_loc[:, :], ps[1][0:64, 0:64])
        C.inc(ti, "sE", 1)
        C.ck("eloc", "sE")
        C.ck("eev", "sE")
        for i in range(IC):
            C.wait(eng, f"Amm{i}")
            grp = (i % 2) * 4
            for nb in range(NB):
                ti = eng.copy(gt[i][:, 512 * nb:512 * (nb + 1)], ps[grp + nb][:, :])
            C.inc(ti, "sE", 1)
            C.ck(f"actA{i}", "sE")

        def act_c1(j):
            if j == 0:
                C.wait(eng, "sgcols")
            if j >= 2:
                C.wait(eng, f"C1m{j - 2}")
            tb = SCt if j % 2 == 0 else SCv
            ub = SCu if j % 2 == 0 else astageF
            ti = eng.activation(tb, gt[j], AF.Identity, bias=Mcol, scale=rsg_col)
            C.inc(ti, "sE", 1)
            C.ck(f"C1t{j}", "sE")
            ti = eng.activation(ub, tb, AF.Sigmoid, bias=msg_col, scale=sg_col)
            C.inc(ti, "sE", 1)
            C.ck(f"sig{j}", "sE")
            ti = eng.activation(ub, ub, AF.Identity, bias=Mcol, scale=float(SIG))
            C.inc(ti, "sE", 1)
            C.ck(f"c1r{j}", "sE")

        for i in range(IC):
            C.wait(eng, f"Bmm{i}")
            if i >= 2:
                C.wait(eng, f"upsp{i - 2}")
                C.wait(eng, f"umax{i - 2}")
            grp = (i % 2) * 4
            for nb in range(NB):
                ti = eng.copy(upst_sl[i % 2][:, 512 * nb:512 * (nb + 1)],
                              ps[grp + nb][:, :])
            C.inc(ti, "sE", 1)
            C.ck(f"actB{i}", "sE")
            if i % 2 == 0:
                C.inc(eng.dma_start(out=up_dram[128 * i:128 * (i + 1), :],
                                    in_=upst_sl[0]), "dA", 16)
                C.ck(f"upsp{i}", "dA")
            if i >= 2:
                act_c1(i - 2)
        for j in range(IC - 2, IC):
            act_c1(j)
        C.wait(eng, "cmmall")
        for nb in range(NB):
            ti = eng.copy(ctxT[:, 512 * nb:512 * (nb + 1)], ps[2 + nb][0:64, :])
        C.inc(ti, "sE", 1)
        C.ck("cev", "sE")
        C.wait(eng, "C1m23")
        for e in range(EC):
            if e % 2 == 0:
                C.wait(eng, f"amm{e}")
                if e >= 2:
                    C.wait(eng, f"asp{e - 2}")
                for nb in range(NB):
                    ti = eng.copy(astage_sl[0][:, 512 * nb:512 * (nb + 1)],
                                  ps[nb][:, :])
                C.inc(ti, "sE", 1)
                C.ck(f"aev{e}", "sE")
            else:
                C.wait(eng, f"aev{e}")
            C.inc(eng.dma_start(out=attn_dram[128 * e:128 * (e + 1), :],
                                in_=astage_sl[e % 2]), "dA", 16)
            C.ck(f"asp{e}", "dA")
        C.wait(eng, "sucols")
        C.inc(eng.dma_start(out=upst_sl[0], in_=up_dram[0:128, :]), "dA", 16)
        C.ck("upldA0", "dA")
        for i in range(IC):
            if i >= 2:
                C.wait(eng, f"C2rr{i - 2}")
            t5b = SCt if i % 2 == 0 else xsc_sl[0]
            ti = eng.activation(t5b, gt[i], AF.Identity, bias=Mcol, scale=cg_col)
            C.inc(ti, "sE", 1)
            C.ck(f"C2t5_{i}", "sE")
            C.wait(eng, f"upldA{i}" if i % 2 == 0 else f"upldG{i}")
            buf = SCu if i % 2 == 0 else SCv
            ti = eng.activation(buf, upst_sl[i % 2], AF.Identity,
                                bias=Mcol, scale=rsu_col)
            C.inc(ti, "sE", 1)
            C.ck(f"C2t6_{i}", "sE")
            if i % 2 == 1:
                ti = eng.activation(buf, buf, AF.Identity, bias=negMcol)
                C.inc(ti, "sE", 1)
                C.ck(f"C2g{i}", "sE")
            if i + 2 < IC and i % 2 == 0:
                C.inc(eng.dma_start(out=upst_sl[0],
                                    in_=up_dram[128 * (i + 2):128 * (i + 3), :]),
                      "dA", 16)
                C.ck(f"upldA{i + 2}", "dA")
        for i in range(0, IC, 2):
            if i == 0:
                C.wait(eng, "socols")
            if i >= 4:
                C.wait(eng, f"o{i - 4}")
            t7b = xsc_sl[(i % 4) // 2]
            ti = eng.activation(t7b, gt[i], AF.Identity, bias=Mcol, scale=cr_col)
            C.inc(ti, "sE", 1)
            C.ck(f"ot7_{i}", "sE")
        pf0 = C.cnt["dA"]
        for e in range(EC):
            if e == 0:
                C.wait(eng, "somax")
            if e >= 2:
                C.wait(eng, f"Dev1_{e - 2}")
                C.wait(eng, f"oev{e - 2}")
            C.inc(eng.dma_start(out=xpref_sl[e % 2],
                                in_=xT[128 * e:128 * (e + 1), :]), "dA", 16)
            C.inc(eng.dma_start(out=astage_sl[e % 2],
                                in_=attn_dram[128 * e:128 * (e + 1), :]), "dA", 16)
            C.wait_val(eng, "dA", pf0 + 32 * (e + 1))
            C.ck(f"xpref{e}", "dA")
            C.ck(f"apref{e}", "dA")

    C.reset(False)
    emit_sync(Mock())
    emit_vector(Chain(Mock(), C, "vE"))
    emit_scalar(Chain(Mock(), C, "sE"))
    emit_tensor(Mock())
    emit_gpsimd(Chain(Mock(), C, "gE"))

    C.reset(True)
    with nc.Block() as block:
        @block.sync
        def _(eng):
            emit_sync(eng)

        @block.vector
        def _(eng):
            emit_vector(Chain(eng, C, "vE"))

        @block.scalar
        def _(eng):
            emit_scalar(Chain(eng, C, "sE"))

        @block.tensor
        def _(eng):
            emit_tensor(eng)

        @block.gpsimd
        def _(eng):
            emit_gpsimd(Chain(eng, C, "gE"))

    st.close()
    return nc


_NC_CACHE = []


def kernel(L0_attn_x, L0_attn_wq, L0_attn_wk, L0_attn_wv, L0_attn_wo,
           L0_ffn_w_gate, L0_ffn_w_up, L0_ffn_w_down):
    x = np.ascontiguousarray(np.asarray(L0_attn_x, dtype=np.float32).reshape(16384, 1024))
    wq = np.asarray(L0_attn_wq, dtype=np.float32)
    wk = np.asarray(L0_attn_wk, dtype=np.float32)
    wv = np.asarray(L0_attn_wv, dtype=np.float32)
    wo = np.asarray(L0_attn_wo, dtype=np.float32)
    wg = np.asarray(L0_ffn_w_gate, dtype=np.float32)
    wu = np.asarray(L0_ffn_w_up, dtype=np.float32)
    wd = np.asarray(L0_ffn_w_down, dtype=np.float32)

    xT_full = np.ascontiguousarray(x.T)
    wqt = np.ascontiguousarray(wq.T)
    wkt = np.ascontiguousarray(wk.T)
    wvt = np.ascontiguousarray(wv.T)
    wot = np.ascontiguousarray(wo.T)
    onesf_h = np.ones((128, 128), dtype=np.float32)
    onesb_h = np.ones((128, 1), dtype=ml_dtypes.bfloat16)
    identb_h = np.eye(128, dtype=ml_dtypes.bfloat16)

    in_maps = []
    for r in range(NCORES):
        in_maps.append({
            "xT": np.ascontiguousarray(xT_full[:, T * r:T * (r + 1)]),
            "wqt": wqt, "wkt": wkt, "wvt": wvt, "wot": wot,
            "wg_nat": np.ascontiguousarray(wg[:, 128 * r:128 * (r + 1)]),
            "wu_nat": np.ascontiguousarray(wu[:, 128 * r:128 * (r + 1)]),
            "wd_nat": np.ascontiguousarray(wd[:, 384 * r:384 * (r + 1)]),
            "onesf": onesf_h, "onesb": onesb_h, "identb": identb_h,
            "identf": np.eye(128, dtype=np.float32),
        })

    if not _NC_CACHE:
        _NC_CACHE.append(build_program())
    nc = _NC_CACHE[0]
    res = run_bass_kernel_spmd(nc, in_maps, core_ids=list(range(NCORES)))
    outT_full = np.concatenate(
        [np.asarray(res.results[r]["outT"]) for r in range(NCORES)], axis=1)
    return np.ascontiguousarray(outT_full.T).reshape(1, 16384, 1024).astype(np.float32)


# version 16
# speedup vs baseline: 1.0313x; 1.0021x over previous
import contextlib
import numpy as np
import ml_dtypes

import concourse.bass as bass
import concourse.mybir as mybir
from concourse.bass_utils import run_bass_kernel_spmd

F32 = mybir.dt.float32
BF16 = mybir.dt.bfloat16
AF = mybir.ActivationFunctionType
OP = mybir.AluOpType
AX = mybir.AxisListType

NCORES = 8
T = 2048
E = 1024
EC = 8
D = 64
I = 3072
IC = 24
NB = 4
TC = 16

MAGIC = 12582912.0
QA = 32767.0
QW = 7.0
SIG = 65536.0
SIGMAX = 65535.0
C_ATTN = float(np.sqrt(3.0 / 64.0))
LNEPS = 1e-5
SEPS = 1e-12


class Ctx:

    def __init__(self):
        self.pass2 = False
        self.cnt = {}
        self.cks = {}
        self.sems = {}

    def reset(self, pass2):
        self.pass2 = pass2
        self.cnt = {k: 0 for k in ("dS", "dA", "dG", "CC", "tE", "vE", "sE", "gE")}

    AUTO = ("vE", "sE", "gE")

    def inc(self, instr, sem, n):
        if sem in self.AUTO:
            return
        self.cnt[sem] += n
        if self.pass2:
            instr.then_inc(self.sems[sem], n)

    def ck(self, label, sem):
        v = (sem, self.cnt[sem])
        if self.pass2:
            assert self.cks[label] == v, (label, self.cks[label], v)
        else:
            assert label not in self.cks, f"dup ck {label}"
            self.cks[label] = v

    def wait(self, eng, label):
        if self.pass2:
            sem, v = self.cks[label]
            eng.wait_ge(self.sems[sem], v)

    def wait_val(self, eng, sem, v):
        if self.pass2:
            eng.wait_ge(self.sems[sem], v)


class Mock:
    def __getattr__(self, name):
        return lambda *a, **k: None


_CHAIN_OPS = {
    "tensor_copy", "tensor_tensor", "tensor_scalar", "tensor_single_scalar",
    "tensor_reduce", "reduce_max", "reduce_sum", "scalar_tensor_tensor",
    "reciprocal", "memset", "memzero", "activation", "copy", "add", "mul",
    "square", "sqrt", "sign", "tensor_add", "tensor_sub", "tensor_mul",
    "tensor_max", "tensor_scalar_add", "tensor_scalar_sub", "tensor_scalar_mul",
    "tensor_scalar_max", "tensor_scalar_min", "transpose", "select",
}


class Chain:

    def __init__(self, eng, C, sem):
        self._eng = eng
        self._C = C
        self._sem = sem

    def __getattr__(self, name):
        eng, C, sem = self._eng, self._C, self._sem
        if name in _CHAIN_OPS:
            def op(*a, **k):
                C.wait_val(eng, sem, C.cnt[sem])
                instr = getattr(eng, name)(*a, **k)
                C.cnt[sem] += 1
                if C.pass2:
                    instr.then_inc(C.sems[sem], 1)
                return instr
            return op
        if name == "dma_start":
            def dma(*a, **k):
                C.wait_val(eng, sem, C.cnt[sem])
                return eng.dma_start(*a, **k)
            return dma
        return getattr(eng, name)


def build_program(debug=False):
    nc = bass.Bass("TRN2", target_bir_lowering=False)
    C = Ctx()

    xT = nc.dram_tensor("xT", [E, T], F32, kind="ExternalInput").ap()
    wqt = nc.dram_tensor("wqt", [E, D], F32, kind="ExternalInput").ap()
    wkt = nc.dram_tensor("wkt", [E, D], F32, kind="ExternalInput").ap()
    wvt = nc.dram_tensor("wvt", [E, D], F32, kind="ExternalInput").ap()
    wot = nc.dram_tensor("wot", [D, E], F32, kind="ExternalInput").ap()
    wg_nat = nc.dram_tensor("wg_nat", [I, 128], F32, kind="ExternalInput").ap()
    wu_nat = nc.dram_tensor("wu_nat", [I, 128], F32, kind="ExternalInput").ap()
    wd_nat = nc.dram_tensor("wd_nat", [E, 384], F32, kind="ExternalInput").ap()
    onesf = nc.dram_tensor("onesf", [128, 128], F32, kind="ExternalInput").ap()
    onesb = nc.dram_tensor("onesb", [128, 1], BF16, kind="ExternalInput").ap()
    identb = nc.dram_tensor("identb", [128, 128], BF16, kind="ExternalInput").ap()
    identf = nc.dram_tensor("identf", [128, 128], F32, kind="ExternalInput").ap()
    outT = nc.dram_tensor("outT", [E, T], F32, kind="ExternalOutput").ap()
    dbg = {}
    if debug:
        for n, sh, dt in [("dbg_cols", [128, 32], F32), ("dbg_misc", [1, 512], F32),
                          ("dbg_esum", [64, 64], F32), ("dbg_o0", [128, T], BF16),
                          ("dbg_up0", [128, T], BF16), ("dbg_attn0", [128, T], BF16),
                          ("dbg_kb", [128, TC * D], BF16), ("dbg_vb", [128, TC * D], BF16),
                          ("dbg_qt", [64, T], BF16), ("dbg_ctx", [64, T], BF16),
                          ("dbg_xq0", [128, T], BF16), ("dbg_dqT", [128, 1024], BF16),
                          ("dbg_sx", [1, 64], F32), ("dbg_xn0", [128, T], BF16),
                          ("dbg_trrow", [8, 128], F32), ("dbg_colsx", [128, 32], F32),
                          ("dbg_arout", [1, 256], F32), ("dbg_arbrow", [1, 128], F32),
                          ("dbg_bc", [128, 128], BF16),
                          ("dbg_arin1", [1, 128], F32), ("dbg_arout1", [1, 128], F32)]:
            dbg[n] = nc.dram_tensor(n, sh, dt, kind="ExternalOutput").ap()

    ag_in = {
        "g": nc.dram_tensor("ag_wg_in", [128, I], BF16),
        "u": nc.dram_tensor("ag_wu_in", [128, I], BF16),
        "d": nc.dram_tensor("ag_wd_in", [384, E], BF16),
    }
    ag_out = {
        "g": nc.dram_tensor("ag_wg_out", [E, I], BF16),
        "u": nc.dram_tensor("ag_wu_out", [E, I], BF16),
        "d": nc.dram_tensor("ag_wd_out", [I, E], BF16),
    }
    ag_e_in = nc.dram_tensor("ag_e_in", [D, D], F32)
    ag_e_out = nc.dram_tensor("ag_e_out", [D, D], F32)
    ar_in = [nc.dram_tensor(f"ar{j}_in", [1, 256], F32) for j in range(4)]
    ar_out = [nc.dram_tensor(f"ar{j}_out", [1, 256], F32) for j in range(4)]
    up_dram = nc.dram_tensor("up_dram", [I, T], BF16)
    attn_dram = nc.dram_tensor("attn_dram", [E, T], BF16)
    flat_dram = nc.dram_tensor("flat_dram", [4, 128], F32)

    st = contextlib.ExitStack()
    sb = lambda name, shape, dt=F32: st.enter_context(nc.sbuf_tensor(name, shape, dt)).ap()

    AR = sb("AR", [128, 49152], BF16)
    ARf = AR.bitcast(F32)
    x_ch = [ARf[:, 2048 * k:2048 * (k + 1)] for k in range(EC)]
    gt = [AR[:, 2048 * i:2048 * (i + 1)] for i in range(IC)]
    dq_nat = ARf[:, 16384:19456]
    dq_T = AR[:, 41984:45056]

    xn = sb("xn", [128, T * EC], BF16)
    xn_ch = [xn[:, 2048 * k:2048 * (k + 1)] for k in range(EC)]
    xq_ch = xn_ch
    xpref = xn.bitcast(F32)
    xpref_sl = [xpref[:, 2048 * j:2048 * (j + 1)] for j in range(2)]
    xsc_sl = [xpref[:, 4096:6144], xpref[:, 6144:8192]]

    SC = sb("SC", [128, 6144], F32)
    xnf = None
    SCt = SC[:, 0:2048]
    SCu = SC[:, 2048:4096]
    SCv = SC[:, 4096:6144]
    Rb = SCt
    Qb = SCu
    xtmp = SCv
    outstage = [SC[:, 0:2048], SC[:, 2048:4096]]
    x2b_sl = [SC.bitcast(BF16)[:, 8192:10240], SC.bitcast(BF16)[:, 10240:12288]]

    upst = sb("upst", [128, 4096], BF16)
    upst_sl = [upst[:, 2048 * j:2048 * (j + 1)] for j in range(2)]
    ctxT = sb("ctxT", [64, T], BF16)
    QTb = sb("QTb", [64, T], BF16)
    wstr = sb("wstr", [128, 2048], BF16)
    wstr_sl = [wstr[:, 1024 * j:1024 * (j + 1)] for j in range(2)]
    wstrD = sb("wstrD", [128, 6144], BF16)
    wstrD_sl = [wstrD[:, 3072 * j:3072 * (j + 1)] for j in range(2)]
    astage = sb("astage", [128, 4096], BF16)
    astage_sl = [astage[:, 2048 * j:2048 * (j + 1)] for j in range(2)]
    astageF = astage.bitcast(F32)
    Kb = sb("Kb", [128, TC * D], BF16)
    Vb = sb("Vb", [128, TC * D], BF16)

    onesf_sb = sb("onesf_sb", [128, 128], F32)
    onesb_sb = sb("onesb_sb", [128, 1], BF16)
    identb_sb = sb("identb_sb", [128, 128], BF16)
    identf_sb = sb("identf_sb", [128, 128], F32)
    wq_f = ARf[:, 22528:23040]
    wk_f = ARf[:, 23040:23552]
    wv_f = ARf[:, 23552:24064]
    wo_f = ARf[0:64, 19456:20480]
    wq_b = sb("wq_b", [128, 512], BF16)
    wk_b = sb("wk_b", [128, 512], BF16)
    wv_b = sb("wv_b", [128, 512], BF16)
    wo_b = sb("wo_b", [64, 1024], BF16)

    dqs = sb("dqs", [128, 288], F32)
    dq_bmax = dqs[:, 0:96]
    dq_s = dqs[:, 96:192]
    dq_rs = dqs[:, 192:288]

    cols = sb("cols", [128, 48], F32)
    maxcol = cols[:, 0:1]
    mctmp = cols[:, 1:2]
    gmaxcol = cols[:, 2:3]
    umaxcol = cols[:, 3:4]
    mmaxcol = cols[:, 4:5]
    rrmaxcol = cols[:, 5:6]
    Mcol = cols[:, 6:7]
    gtmp = cols[:, 7:8]
    negMcol = cols[:, 47:48]
    sx_col = cols[:, 8:9]
    rsx_col = cols[:, 9:10]
    sg_col = cols[:, 10:11]
    rsg_col = cols[:, 11:12]
    msg_col = cols[:, 12:13]
    rsu_col = cols[:, 13:14]
    cg_col = cols[:, 14:15]
    cr_col = cols[:, 15:16]
    so_col = cols[:, 16:17]
    rawc = cols[:, 17:18]
    raw2c = cols[:, 18:19]
    sg65_col = cols[:, 19:20]
    sg2_col = cols[:, 20:21]
    sup_col = cols[:, 21:22]
    tcol = cols[:, 22:23]
    pcol = cols[:, 23:24]
    rec8 = cols[:, 24:32]
    rec8b = cols[:, 32:40]
    rec8c = cols[:, 40:48]

    rows = SCv[0:1, :]
    sum_r = rows[0:1, 0:512]
    ssq_r = rows[0:1, 512:1024]
    rowa = rows[0:1, 1024:1536]
    rowb = rows[0:1, 1536:2048]
    misc = sb("misc", [1, 512], F32)
    srow = misc[0:1, 0:16]
    arrow = misc[0:1, 16:32]
    arb = misc[0:1, 32:48]
    arbrow = misc[0:1, 64:320]
    eps_sb = misc[0:1, 320:321]
    sclr = misc[0:1, 336:352]
    trrow = sb("trrow", [8, 256], F32)
    trsrc = sb("trsrc", [128, 8], F32)
    trsrc2 = sb("trsrc2", [128, 8], F32)
    e_loc = sb("e_loc", [64, 64], F32)
    e_sum = sb("e_sum", [64, 64], F32)
    e_bf = sb("e_bf", [64, 64], BF16)

    ps = [st.enter_context(nc.psum_tensor(f"psb{j}", [128, 512], F32)).ap()
          for j in range(8)]

    for name in ("dS", "dA", "dG", "CC", "tE", "vE", "sE", "gE"):
        C.sems[name] = st.enter_context(nc.semaphore(name))

    RG = [list(range(NCORES))]

    def recip_col(eng, dst_col, src_col):
        eng.tensor_copy(dq_bmax, src_col.broadcast_to([128, 96]))
        eng.reciprocal(dq_s, dq_bmax)
        return eng.tensor_copy(dst_col, dq_s[:, 0:1])


    def emit_sync(eng):
        for k in range(EC):
            C.inc(eng.dma_start(out=x_ch[k], in_=xT[128 * k:128 * (k + 1), :]), "dS", 16)
            C.ck(f"x{k}", "dS")
        for dst, src in ((wq_f, wqt), (wk_f, wkt), (wv_f, wvt)):
            C.inc(eng.dma_start(
                out=dst.rearrange("p (k d) -> p k d", d=D),
                in_=src.rearrange("(k p) d -> p k d", p=128)), "dS", 16)
        C.inc(eng.dma_start(out=wo_f[:, :], in_=wot), "dS", 16)
        C.inc(eng.dma_start(out=onesf_sb[:, :], in_=onesf), "dS", 16)
        C.inc(eng.dma_start(out=onesb_sb[:, :], in_=onesb), "dS", 16)
        C.inc(eng.dma_start(out=identb_sb[:, :], in_=identb), "dS", 16)
        C.inc(eng.dma_start(out=identf_sb[:, :], in_=identf), "dS", 16)
        C.ck("consts", "dS")
        wg_r = ag_out["g"].ap().rearrange("(k p) c -> p k c", p=128)
        for i in range(IC):
            C.wait(eng, "AGwg")
            if i >= 2:
                C.wait(eng, f"Amm{i - 2}")
            C.inc(eng.dma_start(
                out=wstr_sl[i % 2].rearrange("p (k c) -> p k c", c=128),
                in_=wg_r[:, :, 128 * i:128 * (i + 1)]), "dS", 16)
            C.ck(f"wA{i}", "dS")
        wu_r = ag_out["u"].ap().rearrange("(k p) c -> p k c", p=128)
        for i in range(IC):
            C.wait(eng, "AGwu")
            C.wait(eng, f"Bmm{i - 2}" if i >= 2 else "Amm23")
            C.inc(eng.dma_start(
                out=wstr_sl[i % 2].rearrange("p (k c) -> p k c", c=128),
                in_=wu_r[:, :, 128 * i:128 * (i + 1)]), "dS", 16)
            C.ck(f"wB{i}", "dS")
        wd_r = ag_out["d"].ap().rearrange("(i p) c -> p i c", p=128)
        for e in range(EC):
            C.wait(eng, "AGwd")
            if e >= 2:
                C.wait(eng, f"Dmm{e - 2}")
            C.inc(eng.dma_start(
                out=wstrD_sl[e % 2].rearrange("p (i c) -> p i c", c=128),
                in_=wd_r[:, :, 128 * e:128 * (e + 1)]), "dS", 16)
            C.ck(f"wD{e}", "dS")
            if e >= 2:
                C.wait(eng, f"oev{e - 2}")
                C.inc(eng.dma_start(out=outT[128 * (e - 2):128 * (e - 1), :],
                                    in_=outstage[e % 2]), "dS", 16)
                C.ck(f"out{e - 2}", "dS")
        for e in (6, 7):
            C.wait(eng, f"oev{e}")
            C.inc(eng.dma_start(out=outT[128 * e:128 * (e + 1), :],
                                in_=outstage[e % 2]), "dS", 16)
            C.ck(f"out{e}", "dS")
        C.wait(eng, "out7")

    def emit_gpsimd(eng):
        for k in range(EC):
            C.wait(eng, f"x{k}")
            C.inc(eng.dma_start(out=xn_ch[k], in_=x_ch[k]), "dG", 16)
            C.ck(f"xb{k}", "dG")
        C.wait(eng, "consts")
        C.inc(eng.dma_start(out=wq_b[:, :], in_=wq_f[:, :]), "dG", 16)
        C.inc(eng.dma_start(out=wk_b[:, :], in_=wk_f[:, :]), "dG", 16)
        C.inc(eng.dma_start(out=wv_b[:, :], in_=wv_f[:, :]), "dG", 16)
        C.inc(eng.dma_start(out=wo_b[:, :], in_=wo_f[:, :]), "dG", 16)
        C.ck("wcast", "dG")
        prev = None
        for wname in ("g", "u", "d"):
            if prev is not None:
                C.wait(eng, f"dqstage{prev}")
            if wname == "d":
                nat_in = wd_nat.rearrange("(t p) c -> p t c", p=128)
                nat_out = dq_nat.rearrange("p (t c) -> p t c", c=384)
            else:
                src = wg_nat if wname == "g" else wu_nat
                nat_in = src.rearrange("(t p) c -> p t c", p=128)
                nat_out = dq_nat.rearrange("p (t c) -> p t c", c=128)
            C.inc(eng.dma_start(out=nat_out, in_=nat_in), "dG", 16)
            C.ck(f"dqnat{wname}", "dG")
            C.wait(eng, f"dqT{wname}")
            if wname == "d":
                dTr = dq_T.rearrange("p (t j c) -> p j t c", j=3, c=128)
                for j in range(3):
                    C.inc(eng.dma_start(
                        out=ag_in["d"].ap()[128 * j:128 * (j + 1), :]
                            .rearrange("p (t c) -> p t c", c=128),
                        in_=dTr[:, j]), "dG", 16)
            else:
                C.inc(eng.dma_start(out=ag_in[wname][:, :], in_=dq_T), "dG", 16)
            C.ck(f"dqstage{wname}", "dG")
            if wname == "g":
                C.wait_val(eng, "dG", C.cnt["dG"])
                C.inc(eng.collective_compute(
                    "AllGather", OP.bypass, replica_groups=RG,
                    ins=[ag_in["g"].ap().opt()], outs=[ag_out["g"].ap().opt()]), "CC", 1)
                C.ck("AGwg", "CC")
            prev = wname

        C.wait(eng, "sxtrr")
        C.inc(eng.dma_start(out=ar_in[0][0:1, 0:128], in_=trrow[0:1, 0:128]), "dG", 16)
        C.wait_val(eng, "dG", C.cnt["dG"])
        C.inc(eng.collective_compute(
            "AllReduce", OP.max, replica_groups=RG,
            ins=[ar_in[0].ap().opt()], outs=[ar_out[0].ap().opt()]), "CC", 1)
        C.wait_val(eng, "CC", C.cnt["CC"])
        C.inc(eng.dma_start(out=arbrow[0:1, 0:128], in_=ar_out[0][0:1, 0:128]), "dG", 16)
        C.ck("sxback", "dG")
        if debug:
            C.wait(eng, "sxcols")
            C.inc(eng.dma_start(out=dbg["dbg_arout"], in_=ar_out[0][:, :]), "dG", 16)
            C.inc(eng.dma_start(out=dbg["dbg_bc"], in_=astage_sl[1][:, 0:128]), "dG", 16)
            C.inc(eng.dma_start(out=dbg["dbg_arbrow"], in_=arbrow[0:1, 0:128]), "dG", 16)
            C.inc(eng.dma_start(out=dbg["dbg_sx"], in_=misc[0:1, 0:64]), "dG", 16)
            C.inc(eng.dma_start(out=dbg["dbg_xn0"], in_=xn_ch[0]), "dG", 16)
            C.inc(eng.dma_start(out=dbg["dbg_trrow"], in_=trrow[:, 0:128]), "dG", 16)
            C.inc(eng.dma_start(out=dbg["dbg_colsx"], in_=cols[:, 0:32]), "dG", 16)
            C.ck("dbgsx", "dG")

        C.wait(eng, "eloc")
        C.inc(eng.dma_start(out=ag_e_in[:, :], in_=e_loc), "dG", 16)
        C.wait_val(eng, "dG", C.cnt["dG"])
        C.inc(eng.collective_compute(
            "AllReduce", OP.add, replica_groups=RG,
            ins=[ag_e_in.ap().opt()], outs=[ag_e_out.ap().opt()]), "CC", 1)
        C.wait_val(eng, "CC", C.cnt["CC"])
        C.inc(eng.dma_start(out=e_sum[:, :], in_=ag_e_out[:, :]), "dG", 16)
        C.ck("eback", "dG")
        C.wait_val(eng, "dG", C.cnt["dG"])
        C.inc(eng.collective_compute(
            "AllGather", OP.bypass, replica_groups=RG,
            ins=[ag_in["u"].ap().opt()], outs=[ag_out["u"].ap().opt()]), "CC", 1)
        C.ck("AGwu", "CC")
        C.inc(eng.collective_compute(
            "AllGather", OP.bypass, replica_groups=RG,
            ins=[ag_in["d"].ap().opt()], outs=[ag_out["d"].ap().opt()]), "CC", 1)
        C.ck("AGwd", "CC")

        C.wait(eng, "sgtrr")
        C.inc(eng.dma_start(out=ar_in[1][0:1, 0:128], in_=trrow[0:1, 0:128]), "dG", 16)
        C.wait_val(eng, "dG", C.cnt["dG"])
        if debug:
            C.inc(eng.dma_start(out=dbg["dbg_arin1"], in_=ar_in[1][0:1, 0:128]), "dG", 16)
            C.wait_val(eng, "dG", C.cnt["dG"])
        C.inc(eng.collective_compute(
            "AllReduce", OP.max, replica_groups=RG,
            ins=[ar_in[1].ap().opt()], outs=[ar_out[1].ap().opt()]), "CC", 1)
        C.wait_val(eng, "CC", C.cnt["CC"])
        C.inc(eng.dma_start(out=arbrow[0:1, 0:128], in_=ar_out[1][0:1, 0:128]), "dG", 16)
        if debug:
            C.inc(eng.dma_start(out=dbg["dbg_arout1"], in_=ar_out[1][0:1, 0:128]), "dG", 16)
        C.ck("sgback", "dG")

        for i in range(1, IC, 2):
            C.wait(eng, f"actB{i}")
            C.inc(eng.dma_start(out=up_dram[128 * i:128 * (i + 1), :],
                                in_=upst_sl[1]), "dG", 16)
            C.ck(f"upsp{i}", "dG")

        C.wait(eng, "sutrr")
        C.inc(eng.dma_start(out=ar_in[2][0:1, 0:256], in_=trrow[0:1, 0:256]), "dG", 16)
        C.wait_val(eng, "dG", C.cnt["dG"])
        C.inc(eng.collective_compute(
            "AllReduce", OP.max, replica_groups=RG,
            ins=[ar_in[2].ap().opt()], outs=[ar_out[2].ap().opt()]), "CC", 1)
        C.wait_val(eng, "CC", C.cnt["CC"])
        C.inc(eng.dma_start(out=arbrow[0:1, 0:256], in_=ar_out[2][0:1, 0:256]), "dG", 16)
        C.ck("suback", "dG")

        C.wait(eng, "sucols")
        C.wait(eng, "upsp23")
        C.inc(eng.dma_start(out=upst_sl[1], in_=up_dram[128:256, :]), "dG", 16)
        C.ck("upldG1", "dG")
        for j in range(3, IC, 2):
            C.wait(eng, f"C2t6_{j - 2}")
            C.inc(eng.dma_start(out=upst_sl[1],
                                in_=up_dram[128 * j:128 * (j + 1), :]), "dG", 16)
            C.ck(f"upldG{j}", "dG")

        C.wait(eng, "sotrr")
        C.inc(eng.dma_start(out=ar_in[3][0:1, 0:128], in_=trrow[0:1, 0:128]), "dG", 16)
        C.wait_val(eng, "dG", C.cnt["dG"])
        C.inc(eng.collective_compute(
            "AllReduce", OP.max, replica_groups=RG,
            ins=[ar_in[3].ap().opt()], outs=[ar_out[3].ap().opt()]), "CC", 1)
        C.wait_val(eng, "CC", C.cnt["CC"])
        C.inc(eng.dma_start(out=arbrow[0:1, 0:128], in_=ar_out[3][0:1, 0:128]), "dG", 16)
        C.ck("soback", "dG")

        C.wait(eng, "asp7")
        for e in range(2):
            C.inc(eng.dma_start(out=astage_sl[e],
                                in_=attn_dram[128 * e:128 * (e + 1), :]),
                  "dG", 16)
            C.ck(f"aprefG{e}", "dG")
        for e in range(EC):
            C.wait(eng, f"Dev1_{e}")
            C.wait(eng, f"aprefG{e}")
            for nb in range(NB):
                ti = eng.tensor_tensor(
                    outstage[e % 2][:, 512 * nb:512 * (nb + 1)],
                    outstage[e % 2][:, 512 * nb:512 * (nb + 1)],
                    astage_sl[e % 2][:, 512 * nb:512 * (nb + 1)], OP.add)
            C.inc(ti, "gE", 1)
            C.ck(f"oev{e}", "gE")
            if e + 2 < EC:
                C.inc(eng.dma_start(out=astage_sl[e % 2],
                                    in_=attn_dram[128 * (e + 2):128 * (e + 3), :]),
                      "dG", 16)
                C.ck(f"aprefG{e + 2}", "dG")
        C.wait(eng, "out7")
        if debug:
            C.inc(eng.dma_start(out=dbg["dbg_cols"], in_=cols[:, 0:32]), "dG", 16)
            C.inc(eng.dma_start(out=dbg["dbg_misc"], in_=misc), "dG", 16)
            C.inc(eng.dma_start(out=dbg["dbg_esum"], in_=e_sum[:, :]), "dG", 16)
            C.inc(eng.dma_start(out=dbg["dbg_o0"], in_=gt[0]), "dG", 16)
            C.inc(eng.dma_start(out=dbg["dbg_up0"], in_=up_dram[0:128, :]), "dG", 16)
            C.inc(eng.dma_start(out=dbg["dbg_attn0"], in_=attn_dram[0:128, :]), "dG", 16)
            C.inc(eng.dma_start(out=dbg["dbg_kb"], in_=Kb), "dG", 16)
            C.inc(eng.dma_start(out=dbg["dbg_vb"], in_=Vb), "dG", 16)
            C.inc(eng.dma_start(out=dbg["dbg_qt"], in_=QTb), "dG", 16)
            C.inc(eng.dma_start(out=dbg["dbg_ctx"], in_=ctxT), "dG", 16)
            C.inc(eng.dma_start(out=dbg["dbg_xq0"], in_=xn_ch[0]), "dG", 16)
            C.inc(eng.dma_start(out=dbg["dbg_dqT"], in_=ag_out["d"].ap()[0:128, :]), "dG", 16)
            C.wait_val(eng, "dG", C.cnt["dG"])

    def emit_tensor(eng):
        C.wait(eng, "consts")
        for wi, wname in enumerate(("g", "u", "d")):
            C.wait(eng, f"dq{wname}bf")
            for b in range(6):
                gi = wi * 6 + b
                if gi >= 2:
                    C.wait(eng, f"dqev{gi - 2}")
                for j in range(4):
                    t = b * 4 + j
                    ti = eng.transpose(ps[gi % 2][:, 128 * j:128 * (j + 1)],
                                       dq_nat[:, 128 * t:128 * (t + 1)],
                                       identf_sb[:, :])
                C.inc(ti, "tE", 1)
                C.ck(f"dqtr{gi}", "tE")
        C.wait(eng, "dqev17")
        for k in range(EC):
            C.wait(eng, f"xb{k}")
            C.wait(eng, f"x2_{k}")
            for nb in range(NB):
                eng.matmul(ps[nb][0:1, :], onesb_sb[:, 0:1],
                           xn_ch[k][:, 512 * nb:512 * (nb + 1)],
                           start=(k == 0), stop=(k == EC - 1))
            for nb in range(NB):
                ti = eng.matmul(ps[4 + nb][0:1, :], onesb_sb[:, 0:1],
                                x2b_sl[k % 2][:, 512 * nb:512 * (nb + 1)],
                                start=(k == 0), stop=(k == EC - 1))
            C.inc(ti, "tE", 1)
            C.ck(f"sums{k}", "tE")
        for nb in range(NB):
            C.wait(eng, f"rqrow{nb}")
            eng.matmul(ps[nb][:, :], onesf_sb[0:1, :], rowa, start=True, stop=True)
            ti = eng.matmul(ps[4 + nb][:, :], onesf_sb[0:1, :], rowb,
                            start=True, stop=True)
            C.inc(ti, "tE", 1)
            C.ck(f"rqmm{nb}", "tE")
        C.wait(eng, "xn7")
        C.wait(eng, "rqev")
        C.wait(eng, "wcast")
        for nb in range(NB):
            for k in range(EC):
                ti = eng.matmul(ps[nb][0:64, :], wq_b[:, 64 * k:64 * (k + 1)],
                                xn_ch[k][:, 512 * nb:512 * (nb + 1)],
                                start=(k == 0), stop=(k == EC - 1))
            C.inc(ti, "tE", 1)
            C.ck(f"qmm{nb}", "tE")
        for c in range(TC):
            if c >= 2:
                C.wait(eng, f"kvev{c - 2}")
            bk, bv = 4 + (c % 2) * 2, 5 + (c % 2) * 2
            for k in range(EC):
                eng.matmul(ps[bk][:, 0:64], xn_ch[k][:, 128 * c:128 * (c + 1)],
                           wk_b[:, 64 * k:64 * (k + 1)],
                           start=(k == 0), stop=(k == EC - 1))
                ti = eng.matmul(ps[bv][:, 0:64], xn_ch[k][:, 128 * c:128 * (c + 1)],
                                wv_b[:, 64 * k:64 * (k + 1)],
                                start=(k == 0), stop=(k == EC - 1))
            C.inc(ti, "tE", 1)
            C.ck(f"kvmm{c}", "tE")
        C.wait(eng, "qev")
        C.wait(eng, "sxcol")
        ti = eng.transpose(ps[0][0:8, 0:128], trsrc[:, :], identf_sb[:, :])
        C.inc(ti, "tE", 1)
        C.ck("sxtr", "tE")
        C.wait(eng, "sxback")
        ti = eng.matmul(ps[0][:, 0:128], onesf_sb[0:1, :], arbrow[0:1, 0:128],
                        start=True, stop=True)
        C.inc(ti, "tE", 1)
        C.ck("sxbc", "tE")
        C.wait(eng, "kvev15")
        for c in range(TC):
            ti = eng.matmul(ps[1][0:64, 0:64], Kb[:, 64 * c:64 * (c + 1)],
                            Vb[:, 64 * c:64 * (c + 1)],
                            start=(c == 0), stop=(c == TC - 1))
        C.inc(ti, "tE", 1)
        C.ck("emm", "tE")
        for i in range(IC):
            grp = (i % 2) * 4
            C.wait(eng, f"wA{i}")
            if i >= 2:
                C.wait(eng, f"actA{i - 2}")
            for k in range(EC):
                if i < 3:
                    C.wait(eng, f"xq{k}")
                for nb in range(NB):
                    ti = eng.matmul(ps[grp + nb][:, :],
                                    wstr_sl[i % 2][:, 128 * k:128 * (k + 1)],
                                    xq_ch[k][:, 512 * nb:512 * (nb + 1)],
                                    start=(k == 0), stop=(k == EC - 1))
            C.inc(ti, "tE", 1)
            C.ck(f"Amm{i}", "tE")
        C.wait(eng, "actA22")
        C.wait(eng, "gmaxcol")
        ti = eng.transpose(ps[0][0:8, 0:128], trsrc[:, :], identf_sb[:, :])
        C.inc(ti, "tE", 1)
        C.ck("sgtr", "tE")
        for i in range(IC):
            grp = (i % 2) * 4
            if i == 3:
                C.wait(eng, "sgback")
                C.wait(eng, "actB2")
                ti = eng.matmul(ps[0][:, 0:128], onesf_sb[0:1, :],
                                arbrow[0:1, 0:128], start=True, stop=True)
                C.inc(ti, "tE", 1)
                C.ck("sgbc", "tE")
            C.wait(eng, f"wB{i}")
            if i == 4:
                C.wait(eng, "sgcols")
            if i >= 2:
                C.wait(eng, f"actB{i - 2}")
            elif i == 0:
                C.wait(eng, "sgtrr")
            else:
                C.wait(eng, "actA23")
            for k in range(EC):
                for nb in range(NB):
                    ti = eng.matmul(ps[grp + nb][:, :],
                                    wstr_sl[i % 2][:, 128 * k:128 * (k + 1)],
                                    xq_ch[k][:, 512 * nb:512 * (nb + 1)],
                                    start=(k == 0), stop=(k == EC - 1))
            C.inc(ti, "tE", 1)
            C.ck(f"Bmm{i}", "tE")
        C.wait(eng, "umaxcol")
        C.wait(eng, "mmaxcol")
        C.wait(eng, "actB22")
        eng.transpose(ps[0][0:8, 0:128], trsrc[:, :], identf_sb[:, :])
        ti = eng.transpose(ps[1][0:8, 0:128], trsrc2[:, :], identf_sb[:, :])
        C.inc(ti, "tE", 1)
        C.ck("sutr", "tE")
        C.wait(eng, "ebf")
        C.wait(eng, "sxcols")
        C.wait(eng, "eev")
        C.wait(eng, "actB23")
        for nb in range(NB):
            ti = eng.matmul(ps[2 + nb][0:64, :], e_bf[:, :],
                            QTb[:, 512 * nb:512 * (nb + 1)], start=True, stop=True)
        C.inc(ti, "tE", 1)
        C.ck("cmmall", "tE")
        for e in range(EC):
            grp = 4 if e % 2 else 0
            C.wait(eng, f"aev{e - 2}" if e >= 2 else "cev")
            if e == 0:
                C.wait(eng, "sutrr")
            for nb in range(NB):
                ti = eng.matmul(ps[grp + nb][:, :], wo_b[:, 128 * e:128 * (e + 1)],
                                ctxT[:, 512 * nb:512 * (nb + 1)], start=True, stop=True)
            C.inc(ti, "tE", 1)
            C.ck(f"amm{e}", "tE")
        C.wait(eng, "suback")
        C.wait(eng, "aev6")
        C.wait(eng, "aev7")
        eng.matmul(ps[2][:, 0:128], onesf_sb[0:1, :], arbrow[0:1, 0:128],
                   start=True, stop=True)
        ti = eng.matmul(ps[3][:, 0:128], onesf_sb[0:1, :], arbrow[0:1, 128:256],
                        start=True, stop=True)
        C.inc(ti, "tE", 1)
        C.ck("subc", "tE")
        C.wait(eng, "sucols")
        C.wait(eng, "rrmaxcol")
        ti = eng.transpose(ps[0][0:8, 0:128], trsrc[:, :], identf_sb[:, :])
        C.inc(ti, "tE", 1)
        C.ck("sotr", "tE")
        C.wait(eng, "soback")
        ti = eng.matmul(ps[2][:, 0:128], onesf_sb[0:1, :], arbrow[0:1, 0:128],
                        start=True, stop=True)
        C.inc(ti, "tE", 1)
        C.ck("sobc", "tE")
        for e in range(EC):
            grp = (e % 2) * 4
            C.wait(eng, f"wD{e}")
            if e >= 2:
                C.wait(eng, f"Dev1_{e - 2}")
            else:
                C.wait(eng, f"actB{22 + e}")
            for i in range(IC):
                if e == 0:
                    C.wait(eng, f"o{i}")
                for nb in range(NB):
                    ti = eng.matmul(ps[grp + nb][:, :],
                                    wstrD_sl[e % 2][:, 128 * i:128 * (i + 1)],
                                    gt[i][:, 512 * nb:512 * (nb + 1)],
                                    start=(i == 0), stop=(i == IC - 1))
            C.inc(ti, "tE", 1)
            C.ck(f"Dmm{e}", "tE")

    def emit_vector(eng):
        eng.memset(eps_sb, LNEPS)
        eng.memset(Mcol, MAGIC)
        ti = eng.memset(negMcol, -MAGIC)
        C.inc(ti, "vE", 1)
        C.ck("memset", "vE")
        prev = None
        for wname in ("g", "u", "d"):
            C.wait(eng, f"dqnat{wname}")
            if prev is not None:
                C.wait(eng, f"dqstage{prev}")
            nat3 = dq_nat.rearrange("p (b c) -> p b c", c=32)
            eng.tensor_reduce(dq_bmax, nat3, AX.X, OP.max, apply_absolute_value=True)
            eng.tensor_scalar(dq_s, dq_bmax, 1.0 / QW, SEPS, OP.mult, OP.max)
            eng.reciprocal(dq_rs, dq_s)
            eng.tensor_tensor(nat3, nat3,
                              dq_rs[:, :, None].broadcast_to([128, 96, 32]), OP.mult)
            eng.tensor_scalar(dq_nat, dq_nat, MAGIC, MAGIC, OP.add, OP.subtract)
            ti = eng.tensor_tensor(nat3, nat3,
                                   dq_s[:, :, None].broadcast_to([128, 96, 32]), OP.mult)
            C.inc(ti, "vE", 1)
            C.ck(f"dq{wname}bf", "vE")
            prev = wname

        C.wait(eng, f"sums{EC - 1}")
        for nb in range(NB):
            if nb >= 1:
                C.wait(eng, f"rqmm{nb - 1}")
            eng.tensor_copy(sum_r, ps[nb][0:1, :])
            eng.tensor_copy(ssq_r, ps[4 + nb][0:1, :])
            eng.tensor_scalar(rowa, sum_r, 1.0 / E, None, OP.mult, OP.bypass)
            eng.tensor_tensor(rowb, rowa, rowa, OP.mult)
            eng.tensor_scalar(rowa, ssq_r, 1.0 / E, None, OP.mult, OP.bypass)
            ti = eng.tensor_tensor(rowb, rowa, rowb, OP.subtract)
            C.inc(ti, "vE", 1)
            C.ck(f"var{nb}", "vE")
            C.wait(eng, f"sqrt{nb}")
            eng.reciprocal(rowa, rowb)
            ti = eng.scalar_tensor_tensor(rowb, sum_r, 1.0 / E, rowa,
                                          OP.mult, OP.mult)
            C.inc(ti, "vE", 1)
            C.ck(f"rqrow{nb}", "vE")
        for nb in range(NB):
            C.wait(eng, f"rqmm{nb}")
            eng.tensor_copy(Rb[:, 512 * nb:512 * (nb + 1)], ps[nb][:, :])
            ti = eng.tensor_copy(Qb[:, 512 * nb:512 * (nb + 1)], ps[4 + nb][:, :])
        C.inc(ti, "vE", 1)
        C.ck("rqev", "vE")
        for k in range(EC):
            eng.tensor_tensor(xtmp, x_ch[k], Rb, OP.mult)
            ti = eng.tensor_tensor(xn_ch[k], xtmp, Qb, OP.subtract)
            C.inc(ti, "vE", 1)
            C.ck(f"xn{k}", "vE")
            eng.tensor_reduce(mctmp, xn_ch[k], AX.X, OP.max, apply_absolute_value=True)
            ti = (eng.tensor_tensor(maxcol, maxcol, mctmp, OP.max) if k else
                  eng.tensor_copy(maxcol, mctmp))
            C.inc(ti, "vE", 1)
        ti = eng.tensor_copy(trsrc[:, 0:1], maxcol)
        C.inc(ti, "vE", 1)
        C.ck("sxcol", "vE")
        C.wait(eng, "sxtr")
        ti = eng.tensor_copy(trrow[:, 0:128], ps[0][0:8, 0:128])
        C.inc(ti, "vE", 1)
        C.ck("sxtrr", "vE")
        C.wait(eng, "sxbc")
        eng.tensor_reduce(rawc, ps[0][:, 0:128], AX.X, OP.max)
        if debug:
            eng.tensor_copy(astage_sl[1][:, 0:128], ps[0][:, 0:128])
        eng.tensor_scalar(sx_col, rawc, 1.0 / QA, SEPS, OP.mult, OP.max)
        ti = recip_col(eng, rsx_col, sx_col)
        C.inc(ti, "vE", 1)
        C.ck("sxcols", "vE")
        C.wait(eng, "kvmm15")
        if debug:
            C.wait(eng, "dbgsx")
        for k in range(EC):
            eng.tensor_scalar(SCt, xn_ch[k], rsx_col, MAGIC, OP.mult, OP.add)
            ti = eng.tensor_scalar(xq_ch[k], SCt, MAGIC, sx_col, OP.subtract, OP.mult)
            C.inc(ti, "vE", 1)
            C.ck(f"xq{k}", "vE")
        C.wait(eng, "eback")
        ti = eng.tensor_scalar(e_bf[:, :], e_sum[:, :], C_ATTN, None, OP.mult, OP.bypass)
        C.inc(ti, "vE", 1)
        C.ck("ebf", "vE")
        for i in range(IC):
            C.wait(eng, f"actA{i}")
            eng.tensor_reduce(mctmp, gt[i], AX.X, OP.max, apply_absolute_value=True)
            ti = (eng.tensor_tensor(gmaxcol, gmaxcol, mctmp, OP.max) if i else
                  eng.tensor_copy(gmaxcol, mctmp))
            C.inc(ti, "vE", 1)
        ti = eng.tensor_copy(trsrc[:, 0:1], gmaxcol)
        C.inc(ti, "vE", 1)
        C.ck("gmaxcol", "vE")
        C.wait(eng, "sgtr")
        ti = eng.tensor_copy(trrow[:, 0:128], ps[0][0:8, 0:128])
        C.inc(ti, "vE", 1)
        C.ck("sgtrr", "vE")
        def vec_c1(j):
            C.wait(eng, f"c1r{j}")
            tb = SCt if j % 2 == 0 else SCv
            ub = SCu if j % 2 == 0 else astageF
            eng.tensor_scalar(ub, ub, MAGIC + SIGMAX, MAGIC, OP.min, OP.subtract)
            ti = eng.scalar_tensor_tensor(gt[j], tb, -MAGIC, ub, OP.add, OP.mult)
            C.inc(ti, "vE", 1)
            C.ck(f"C1m{j}", "vE")
            eng.tensor_reduce(mctmp, gt[j], AX.X, OP.max, apply_absolute_value=True)
            ti = (eng.tensor_tensor(mmaxcol, mmaxcol, mctmp, OP.max) if j else
                  eng.tensor_copy(mmaxcol, mctmp))
            C.inc(ti, "vE", 1)

        for i in range(IC):
            if i == 2:
                C.wait(eng, "sgbc")
                eng.tensor_reduce(rawc, ps[0][:, 0:128], AX.X, OP.max)
                eng.tensor_scalar(sg_col, rawc, 1.0 / QA, SEPS, OP.mult, OP.max)
                recip_col(eng, rsg_col, sg_col)
                eng.tensor_scalar(sg65_col, sg_col, 1.0 / SIG, None,
                                  OP.mult, OP.bypass)
                ti = eng.tensor_scalar(msg_col, sg_col, -MAGIC, None,
                                       OP.mult, OP.bypass)
                C.inc(ti, "vE", 1)
                C.ck("sgcols", "vE")
            C.wait(eng, f"actB{i}")
            eng.tensor_reduce(mctmp, upst_sl[i % 2], AX.X, OP.max,
                              apply_absolute_value=True)
            ti = (eng.tensor_tensor(umaxcol, umaxcol, mctmp, OP.max) if i else
                  eng.tensor_copy(umaxcol, mctmp))
            C.inc(ti, "vE", 1)
            C.ck(f"umax{i}", "vE")
            if i >= 2:
                vec_c1(i - 2)
        for j in range(IC - 2, IC):
            vec_c1(j)
        eng.tensor_copy(trsrc[:, 0:1], umaxcol)
        ti = eng.tensor_copy(trsrc2[:, 0:1], mmaxcol)
        C.inc(ti, "vE", 1)
        C.ck("umaxcol", "vE")
        C.ck("mmaxcol", "vE")
        C.wait(eng, "sutr")
        eng.tensor_copy(trrow[:, 0:128], ps[0][0:8, 0:128])
        ti = eng.tensor_copy(trrow[:, 128:256], ps[1][0:8, 0:128])
        C.inc(ti, "vE", 1)
        C.ck("sutrr", "vE")
        for e in range(1, EC, 2):
            C.wait(eng, f"amm{e}")
            if e >= 3:
                C.wait(eng, f"asp{e - 2}")
            for nb in range(NB):
                ti = eng.tensor_copy(astage_sl[1][:, 512 * nb:512 * (nb + 1)],
                                     ps[4 + nb][:, :])
            C.inc(ti, "vE", 1)
            C.ck(f"aev{e}", "vE")
        C.wait(eng, "subc")
        eng.tensor_reduce(rawc, ps[2][:, 0:128], AX.X, OP.max)
        eng.tensor_reduce(raw2c, ps[3][:, 0:128], AX.X, OP.max)
        eng.tensor_scalar(sup_col, rawc, 1.0 / QA, SEPS, OP.mult, OP.max)
        recip_col(eng, rsu_col, sup_col)
        eng.tensor_tensor(tcol, raw2c, sg65_col, OP.mult)
        eng.tensor_scalar(sg2_col, tcol, 1.0 / QA, SEPS, OP.mult, OP.max)
        recip_col(eng, tcol, sg2_col)
        ti = eng.tensor_tensor(cg_col, sg65_col, tcol, OP.mult)
        C.inc(ti, "vE", 1)
        C.ck("sucols", "vE")
        for i in range(IC):
            buf = SCu if i % 2 == 0 else SCv
            t5b = SCt if i % 2 == 0 else xsc_sl[0]
            if i % 2 == 0:
                C.wait(eng, f"C2t6_{i}")
                eng.tensor_scalar(buf, buf, MAGIC, None, OP.subtract, OP.bypass)
            else:
                C.wait(eng, f"C2g{i}")
            C.wait(eng, f"C2t5_{i}")
            ti = eng.scalar_tensor_tensor(gt[i], t5b, -MAGIC, buf, OP.add, OP.mult)
            C.inc(ti, "vE", 1)
            C.ck(f"C2rr{i}", "vE")
            eng.tensor_reduce(mctmp, gt[i], AX.X, OP.max, apply_absolute_value=True)
            ti = (eng.tensor_tensor(rrmaxcol, rrmaxcol, mctmp, OP.max) if i else
                  eng.tensor_copy(rrmaxcol, mctmp))
            C.inc(ti, "vE", 1)
        ti = eng.tensor_copy(trsrc[:, 0:1], rrmaxcol)
        C.inc(ti, "vE", 1)
        C.ck("rrmaxcol", "vE")
        C.wait(eng, "sotr")
        ti = eng.tensor_copy(trrow[:, 0:128], ps[0][0:8, 0:128])
        C.inc(ti, "vE", 1)
        C.ck("sotrr", "vE")
        C.wait(eng, "sobc")
        eng.tensor_reduce(rawc, ps[2][:, 0:128], AX.X, OP.max)
        eng.tensor_tensor(pcol, sg2_col, sup_col, OP.mult)
        eng.tensor_tensor(tcol, rawc, pcol, OP.mult)
        eng.tensor_scalar(so_col, tcol, 1.0 / QA, SEPS, OP.mult, OP.max)
        recip_col(eng, tcol, so_col)
        ti = eng.tensor_tensor(cr_col, pcol, tcol, OP.mult)
        C.inc(ti, "vE", 1)
        C.ck("socols", "vE")
        C.ck("somax", "vE")
        for i in range(IC):
            if i % 2 == 0:
                C.wait(eng, f"ot7_{i}")
                t7b = xsc_sl[(i % 4) // 2]
                ti = eng.tensor_scalar(gt[i], t7b, MAGIC, so_col,
                                       OP.subtract, OP.mult)
            else:
                eng.tensor_scalar(SCt, gt[i], cr_col, MAGIC, OP.mult, OP.add)
                ti = eng.tensor_scalar(gt[i], SCt, MAGIC, so_col,
                                       OP.subtract, OP.mult)
            C.inc(ti, "vE", 1)
            C.ck(f"o{i}", "vE")
        for e in range(EC):
            C.wait(eng, f"Dmm{e}")
            C.wait(eng, f"xpref{e}")
            if e >= 2:
                C.wait(eng, f"out{e - 2}")
            grp = (e % 2) * 4
            for nb in range(NB):
                ti = eng.tensor_tensor(outstage[e % 2][:, 512 * nb:512 * (nb + 1)],
                                       ps[grp + nb][:, :],
                                       xpref_sl[e % 2][:, 512 * nb:512 * (nb + 1)],
                                       OP.add)
            C.inc(ti, "vE", 1)
            C.ck(f"Dev1_{e}", "vE")

    def emit_scalar(eng):
        for wi, wname in enumerate(("g", "u", "d")):
            for b in range(6):
                gi = wi * 6 + b
                C.wait(eng, f"dqtr{gi}")
                ti = eng.copy(dq_T[:, 512 * b:512 * (b + 1)], ps[gi % 2][:, :])
                C.inc(ti, "sE", 1)
                C.ck(f"dqev{gi}", "sE")
            C.ck(f"dqT{wname}", "sE")
        for k in range(EC):
            C.wait(eng, f"x{k}")
            if k >= 2:
                C.wait(eng, f"sums{k - 2}")
            ti = eng.activation(x2b_sl[k % 2], x_ch[k], AF.Square)
            C.inc(ti, "sE", 1)
            C.ck(f"x2_{k}", "sE")
        C.wait(eng, "memset")
        for nb in range(NB):
            C.wait(eng, f"var{nb}")
            ti = eng.activation(rowb, rowb, AF.Sqrt, bias=eps_sb, scale=1.0)
            C.inc(ti, "sE", 1)
            C.ck(f"sqrt{nb}", "sE")
        for nb in range(NB):
            C.wait(eng, f"qmm{nb}")
            ti = eng.copy(QTb[:, 512 * nb:512 * (nb + 1)], ps[nb][0:64, :])
        C.inc(ti, "sE", 1)
        C.ck("qev", "sE")
        for c in range(TC):
            C.wait(eng, f"kvmm{c}")
            bk, bv = 4 + (c % 2) * 2, 5 + (c % 2) * 2
            eng.copy(Kb[:, 64 * c:64 * (c + 1)], ps[bk][:, 0:64])
            ti = eng.copy(Vb[:, 64 * c:64 * (c + 1)], ps[bv][:, 0:64])
            C.inc(ti, "sE", 1)
            C.ck(f"kvev{c}", "sE")
        C.wait(eng, "emm")
        ti = eng.copy(e_loc[:, :], ps[1][0:64, 0:64])
        C.inc(ti, "sE", 1)
        C.ck("eloc", "sE")
        C.ck("eev", "sE")
        for i in range(IC):
            C.wait(eng, f"Amm{i}")
            grp = (i % 2) * 4
            for nb in range(NB):
                ti = eng.copy(gt[i][:, 512 * nb:512 * (nb + 1)], ps[grp + nb][:, :])
            C.inc(ti, "sE", 1)
            C.ck(f"actA{i}", "sE")

        def act_c1(j):
            if j == 0:
                C.wait(eng, "sgcols")
            if j >= 2:
                C.wait(eng, f"C1m{j - 2}")
            tb = SCt if j % 2 == 0 else SCv
            ub = SCu if j % 2 == 0 else astageF
            ti = eng.activation(tb, gt[j], AF.Identity, bias=Mcol, scale=rsg_col)
            C.inc(ti, "sE", 1)
            C.ck(f"C1t{j}", "sE")
            ti = eng.activation(ub, tb, AF.Sigmoid, bias=msg_col, scale=sg_col)
            C.inc(ti, "sE", 1)
            C.ck(f"sig{j}", "sE")
            ti = eng.activation(ub, ub, AF.Identity, bias=Mcol, scale=float(SIG))
            C.inc(ti, "sE", 1)
            C.ck(f"c1r{j}", "sE")

        for i in range(IC):
            C.wait(eng, f"Bmm{i}")
            if i >= 2:
                C.wait(eng, f"upsp{i - 2}")
                C.wait(eng, f"umax{i - 2}")
            grp = (i % 2) * 4
            for nb in range(NB):
                ti = eng.copy(upst_sl[i % 2][:, 512 * nb:512 * (nb + 1)],
                              ps[grp + nb][:, :])
            C.inc(ti, "sE", 1)
            C.ck(f"actB{i}", "sE")
            if i % 2 == 0:
                C.inc(eng.dma_start(out=up_dram[128 * i:128 * (i + 1), :],
                                    in_=upst_sl[0]), "dA", 16)
                C.ck(f"upsp{i}", "dA")
            if i >= 2:
                act_c1(i - 2)
        for j in range(IC - 2, IC):
            act_c1(j)
        C.wait(eng, "cmmall")
        for nb in range(NB):
            ti = eng.copy(ctxT[:, 512 * nb:512 * (nb + 1)], ps[2 + nb][0:64, :])
        C.inc(ti, "sE", 1)
        C.ck("cev", "sE")
        C.wait(eng, "C1m23")
        for e in range(EC):
            if e % 2 == 0:
                C.wait(eng, f"amm{e}")
                if e >= 2:
                    C.wait(eng, f"asp{e - 2}")
                for nb in range(NB):
                    ti = eng.copy(astage_sl[0][:, 512 * nb:512 * (nb + 1)],
                                  ps[nb][:, :])
                C.inc(ti, "sE", 1)
                C.ck(f"aev{e}", "sE")
            else:
                C.wait(eng, f"aev{e}")
            C.inc(eng.dma_start(out=attn_dram[128 * e:128 * (e + 1), :],
                                in_=astage_sl[e % 2]), "dA", 16)
            C.ck(f"asp{e}", "dA")
        C.wait(eng, "sucols")
        C.inc(eng.dma_start(out=upst_sl[0], in_=up_dram[0:128, :]), "dA", 16)
        C.ck("upldA0", "dA")
        for i in range(IC):
            if i >= 2:
                C.wait(eng, f"C2rr{i - 2}")
            t5b = SCt if i % 2 == 0 else xsc_sl[0]
            ti = eng.activation(t5b, gt[i], AF.Identity, bias=Mcol, scale=cg_col)
            C.inc(ti, "sE", 1)
            C.ck(f"C2t5_{i}", "sE")
            C.wait(eng, f"upldA{i}" if i % 2 == 0 else f"upldG{i}")
            buf = SCu if i % 2 == 0 else SCv
            ti = eng.activation(buf, upst_sl[i % 2], AF.Identity,
                                bias=Mcol, scale=rsu_col)
            C.inc(ti, "sE", 1)
            C.ck(f"C2t6_{i}", "sE")
            if i % 2 == 1:
                ti = eng.activation(buf, buf, AF.Identity, bias=negMcol)
                C.inc(ti, "sE", 1)
                C.ck(f"C2g{i}", "sE")
            if i + 2 < IC and i % 2 == 0:
                C.inc(eng.dma_start(out=upst_sl[0],
                                    in_=up_dram[128 * (i + 2):128 * (i + 3), :]),
                      "dA", 16)
                C.ck(f"upldA{i + 2}", "dA")
        for i in range(0, IC, 2):
            if i == 0:
                C.wait(eng, "socols")
            if i >= 4:
                C.wait(eng, f"o{i - 4}")
            t7b = xsc_sl[(i % 4) // 2]
            ti = eng.activation(t7b, gt[i], AF.Identity, bias=Mcol, scale=cr_col)
            C.inc(ti, "sE", 1)
            C.ck(f"ot7_{i}", "sE")
        pf0 = C.cnt["dA"]
        for e in range(EC):
            if e == 0:
                C.wait(eng, "somax")
            if e >= 2:
                C.wait(eng, f"Dev1_{e - 2}")
                C.wait(eng, f"oev{e - 2}")
            C.inc(eng.dma_start(out=xpref_sl[e % 2],
                                in_=xT[128 * e:128 * (e + 1), :]), "dA", 16)
            C.wait_val(eng, "dA", pf0 + 16 * (e + 1))
            C.ck(f"xpref{e}", "dA")

    C.reset(False)
    emit_sync(Mock())
    emit_vector(Chain(Mock(), C, "vE"))
    emit_scalar(Chain(Mock(), C, "sE"))
    emit_tensor(Mock())
    emit_gpsimd(Chain(Mock(), C, "gE"))

    C.reset(True)
    with nc.Block() as block:
        @block.sync
        def _(eng):
            emit_sync(eng)

        @block.vector
        def _(eng):
            emit_vector(Chain(eng, C, "vE"))

        @block.scalar
        def _(eng):
            emit_scalar(Chain(eng, C, "sE"))

        @block.tensor
        def _(eng):
            emit_tensor(eng)

        @block.gpsimd
        def _(eng):
            emit_gpsimd(Chain(eng, C, "gE"))

    st.close()
    return nc


_NC_CACHE = []


def kernel(L0_attn_x, L0_attn_wq, L0_attn_wk, L0_attn_wv, L0_attn_wo,
           L0_ffn_w_gate, L0_ffn_w_up, L0_ffn_w_down):
    x = np.ascontiguousarray(np.asarray(L0_attn_x, dtype=np.float32).reshape(16384, 1024))
    wq = np.asarray(L0_attn_wq, dtype=np.float32)
    wk = np.asarray(L0_attn_wk, dtype=np.float32)
    wv = np.asarray(L0_attn_wv, dtype=np.float32)
    wo = np.asarray(L0_attn_wo, dtype=np.float32)
    wg = np.asarray(L0_ffn_w_gate, dtype=np.float32)
    wu = np.asarray(L0_ffn_w_up, dtype=np.float32)
    wd = np.asarray(L0_ffn_w_down, dtype=np.float32)

    xT_full = np.ascontiguousarray(x.T)
    wqt = np.ascontiguousarray(wq.T)
    wkt = np.ascontiguousarray(wk.T)
    wvt = np.ascontiguousarray(wv.T)
    wot = np.ascontiguousarray(wo.T)
    onesf_h = np.ones((128, 128), dtype=np.float32)
    onesb_h = np.ones((128, 1), dtype=ml_dtypes.bfloat16)
    identb_h = np.eye(128, dtype=ml_dtypes.bfloat16)

    in_maps = []
    for r in range(NCORES):
        in_maps.append({
            "xT": np.ascontiguousarray(xT_full[:, T * r:T * (r + 1)]),
            "wqt": wqt, "wkt": wkt, "wvt": wvt, "wot": wot,
            "wg_nat": np.ascontiguousarray(wg[:, 128 * r:128 * (r + 1)]),
            "wu_nat": np.ascontiguousarray(wu[:, 128 * r:128 * (r + 1)]),
            "wd_nat": np.ascontiguousarray(wd[:, 384 * r:384 * (r + 1)]),
            "onesf": onesf_h, "onesb": onesb_h, "identb": identb_h,
            "identf": np.eye(128, dtype=np.float32),
        })

    if not _NC_CACHE:
        _NC_CACHE.append(build_program())
    nc = _NC_CACHE[0]
    res = run_bass_kernel_spmd(nc, in_maps, core_ids=list(range(NCORES)))
    outT_full = np.concatenate(
        [np.asarray(res.results[r]["outT"]) for r in range(NCORES)], axis=1)
    return np.ascontiguousarray(outT_full.T).reshape(1, 16384, 1024).astype(np.float32)


# version 17
# speedup vs baseline: 1.0360x; 1.0045x over previous
import contextlib
import numpy as np
import ml_dtypes

import concourse.bass as bass
import concourse.mybir as mybir
from concourse.bass_utils import run_bass_kernel_spmd

F32 = mybir.dt.float32
BF16 = mybir.dt.bfloat16
AF = mybir.ActivationFunctionType
OP = mybir.AluOpType
AX = mybir.AxisListType

NCORES = 8
T = 2048
E = 1024
EC = 8
D = 64
I = 3072
IC = 24
NB = 4
TC = 16

MAGIC = 12582912.0
QA = 32767.0
QW = 7.0
SIG = 65536.0
SIGMAX = 65535.0
C_ATTN = float(np.sqrt(3.0 / 64.0))
LNEPS = 1e-5
SEPS = 1e-12


class Ctx:

    def __init__(self):
        self.pass2 = False
        self.cnt = {}
        self.cks = {}
        self.sems = {}

    def reset(self, pass2):
        self.pass2 = pass2
        self.cnt = {k: 0 for k in ("dS", "dA", "dG", "CC", "tE", "vE", "sE", "gE")}

    AUTO = ("vE", "sE", "gE")

    def inc(self, instr, sem, n):
        if sem in self.AUTO:
            return
        self.cnt[sem] += n
        if self.pass2:
            instr.then_inc(self.sems[sem], n)

    def ck(self, label, sem):
        v = (sem, self.cnt[sem])
        if self.pass2:
            assert self.cks[label] == v, (label, self.cks[label], v)
        else:
            assert label not in self.cks, f"dup ck {label}"
            self.cks[label] = v

    def wait(self, eng, label):
        if self.pass2:
            sem, v = self.cks[label]
            eng.wait_ge(self.sems[sem], v)

    def wait_val(self, eng, sem, v):
        if self.pass2:
            eng.wait_ge(self.sems[sem], v)


class Mock:
    def __getattr__(self, name):
        return lambda *a, **k: None


_CHAIN_OPS = {
    "tensor_copy", "tensor_tensor", "tensor_scalar", "tensor_single_scalar",
    "tensor_reduce", "reduce_max", "reduce_sum", "scalar_tensor_tensor",
    "reciprocal", "memset", "memzero", "activation", "copy", "add", "mul",
    "square", "sqrt", "sign", "tensor_add", "tensor_sub", "tensor_mul",
    "tensor_max", "tensor_scalar_add", "tensor_scalar_sub", "tensor_scalar_mul",
    "tensor_scalar_max", "tensor_scalar_min", "transpose", "select",
}


class Chain:

    def __init__(self, eng, C, sem):
        self._eng = eng
        self._C = C
        self._sem = sem

    def __getattr__(self, name):
        eng, C, sem = self._eng, self._C, self._sem
        if name in _CHAIN_OPS:
            def op(*a, **k):
                C.wait_val(eng, sem, C.cnt[sem])
                instr = getattr(eng, name)(*a, **k)
                C.cnt[sem] += 1
                if C.pass2:
                    instr.then_inc(C.sems[sem], 1)
                return instr
            return op
        if name == "dma_start":
            def dma(*a, **k):
                C.wait_val(eng, sem, C.cnt[sem])
                return eng.dma_start(*a, **k)
            return dma
        return getattr(eng, name)


def build_program(debug=False):
    nc = bass.Bass("TRN2", target_bir_lowering=False)
    C = Ctx()

    xT = nc.dram_tensor("xT", [E, T], F32, kind="ExternalInput").ap()
    wqt = nc.dram_tensor("wqt", [E, D], F32, kind="ExternalInput").ap()
    wkt = nc.dram_tensor("wkt", [E, D], F32, kind="ExternalInput").ap()
    wvt = nc.dram_tensor("wvt", [E, D], F32, kind="ExternalInput").ap()
    wot = nc.dram_tensor("wot", [D, E], F32, kind="ExternalInput").ap()
    wg_nat = nc.dram_tensor("wg_nat", [I, 128], F32, kind="ExternalInput").ap()
    wu_nat = nc.dram_tensor("wu_nat", [I, 128], F32, kind="ExternalInput").ap()
    wd_nat = nc.dram_tensor("wd_nat", [E, 384], F32, kind="ExternalInput").ap()
    onesf = nc.dram_tensor("onesf", [128, 128], F32, kind="ExternalInput").ap()
    onesb = nc.dram_tensor("onesb", [128, 1], BF16, kind="ExternalInput").ap()
    identb = nc.dram_tensor("identb", [128, 128], BF16, kind="ExternalInput").ap()
    identf = nc.dram_tensor("identf", [128, 128], F32, kind="ExternalInput").ap()
    outT = nc.dram_tensor("outT", [E, T], F32, kind="ExternalOutput").ap()
    dbg = {}
    if debug:
        for n, sh, dt in [("dbg_cols", [128, 32], F32), ("dbg_misc", [1, 512], F32),
                          ("dbg_esum", [64, 64], F32), ("dbg_o0", [128, T], BF16),
                          ("dbg_up0", [128, T], BF16), ("dbg_attn0", [128, T], BF16),
                          ("dbg_kb", [128, TC * D], BF16), ("dbg_vb", [128, TC * D], BF16),
                          ("dbg_qt", [64, T], BF16), ("dbg_ctx", [64, T], BF16),
                          ("dbg_xq0", [128, T], BF16), ("dbg_dqT", [128, 1024], BF16),
                          ("dbg_sx", [1, 64], F32), ("dbg_xn0", [128, T], BF16),
                          ("dbg_trrow", [8, 128], F32), ("dbg_colsx", [128, 32], F32),
                          ("dbg_arout", [1, 256], F32), ("dbg_arbrow", [1, 128], F32),
                          ("dbg_bc", [128, 128], BF16),
                          ("dbg_arin1", [1, 128], F32), ("dbg_arout1", [1, 128], F32)]:
            dbg[n] = nc.dram_tensor(n, sh, dt, kind="ExternalOutput").ap()

    ag_in = {
        "g": nc.dram_tensor("ag_wg_in", [128, I], BF16),
        "u": nc.dram_tensor("ag_wu_in", [128, I], BF16),
        "d": nc.dram_tensor("ag_wd_in", [384, E], BF16),
    }
    ag_out = {
        "g": nc.dram_tensor("ag_wg_out", [E, I], BF16),
        "u": nc.dram_tensor("ag_wu_out", [E, I], BF16),
        "d": nc.dram_tensor("ag_wd_out", [I, E], BF16),
    }
    ag_e_in = nc.dram_tensor("ag_e_in", [D, D], F32)
    ag_e_out = nc.dram_tensor("ag_e_out", [D, D], F32)
    ar_in = [nc.dram_tensor(f"ar{j}_in", [1, 256], F32) for j in range(4)]
    ar_out = [nc.dram_tensor(f"ar{j}_out", [1, 256], F32) for j in range(4)]
    up_dram = nc.dram_tensor("up_dram", [I, T], BF16)
    attn_dram = nc.dram_tensor("attn_dram", [E, T], BF16)
    flat_dram = nc.dram_tensor("flat_dram", [4, 128], F32)

    st = contextlib.ExitStack()
    sb = lambda name, shape, dt=F32: st.enter_context(nc.sbuf_tensor(name, shape, dt)).ap()

    AR = sb("AR", [128, 49152], BF16)
    ARf = AR.bitcast(F32)
    x_ch = [ARf[:, 2048 * k:2048 * (k + 1)] for k in range(EC)]
    gt = [AR[:, 2048 * i:2048 * (i + 1)] for i in range(IC)]
    dq_nat = ARf[:, 16384:19456]
    dq_T = AR[:, 41984:45056]

    xn = sb("xn", [128, T * EC], BF16)
    xn_ch = [xn[:, 2048 * k:2048 * (k + 1)] for k in range(EC)]
    xq_ch = xn_ch
    xpref = xn.bitcast(F32)
    xpref_sl = [xpref[:, 2048 * j:2048 * (j + 1)] for j in range(2)]
    xsc_sl = [xpref[:, 4096:6144], xpref[:, 6144:8192]]

    SC = sb("SC", [128, 6144], F32)
    xnf = None
    SCt = SC[:, 0:2048]
    SCu = SC[:, 2048:4096]
    SCv = SC[:, 4096:6144]
    Rb = SCt
    Qb = SCu
    xtmp = SCv
    outstage = [SC[:, 0:2048], SC[:, 2048:4096]]
    x2b_sl = [SC.bitcast(BF16)[:, 8192:10240], SC.bitcast(BF16)[:, 10240:12288]]

    upst = sb("upst", [128, 4096], BF16)
    upst_sl = [upst[:, 2048 * j:2048 * (j + 1)] for j in range(2)]
    ctxT = sb("ctxT", [64, T], BF16)
    QTb = sb("QTb", [64, T], BF16)
    wstr = sb("wstr", [128, 2048], BF16)
    wstr_sl = [wstr[:, 1024 * j:1024 * (j + 1)] for j in range(2)]
    wstrD = sb("wstrD", [128, 6144], BF16)
    wstrD_sl = [wstrD[:, 3072 * j:3072 * (j + 1)] for j in range(2)]
    astage = sb("astage", [128, 4096], BF16)
    astage_sl = [astage[:, 2048 * j:2048 * (j + 1)] for j in range(2)]
    astageF = astage.bitcast(F32)
    Kb = sb("Kb", [128, TC * D], BF16)
    Vb = sb("Vb", [128, TC * D], BF16)

    onesf_sb = sb("onesf_sb", [128, 128], F32)
    onesb_sb = sb("onesb_sb", [128, 1], BF16)
    identb_sb = sb("identb_sb", [128, 128], BF16)
    identf_sb = sb("identf_sb", [128, 128], F32)
    wq_f = ARf[:, 22528:23040]
    wk_f = ARf[:, 23040:23552]
    wv_f = ARf[:, 23552:24064]
    wo_f = ARf[0:64, 19456:20480]
    wq_b = sb("wq_b", [128, 512], BF16)
    wk_b = sb("wk_b", [128, 512], BF16)
    wv_b = sb("wv_b", [128, 512], BF16)
    wo_b = sb("wo_b", [64, 1024], BF16)

    dqs = sb("dqs", [128, 288], F32)
    dq_bmax = dqs[:, 0:96]
    dq_s = dqs[:, 96:192]
    dq_rs = dqs[:, 192:288]

    cols = sb("cols", [128, 48], F32)
    maxcol = cols[:, 0:1]
    mctmp = cols[:, 1:2]
    gmaxcol = cols[:, 2:3]
    umaxcol = cols[:, 3:4]
    mmaxcol = cols[:, 4:5]
    rrmaxcol = cols[:, 5:6]
    Mcol = cols[:, 6:7]
    gtmp = cols[:, 7:8]
    negMcol = cols[:, 47:48]
    sx_col = cols[:, 8:9]
    rsx_col = cols[:, 9:10]
    sg_col = cols[:, 10:11]
    rsg_col = cols[:, 11:12]
    msg_col = cols[:, 12:13]
    rsu_col = cols[:, 13:14]
    cg_col = cols[:, 14:15]
    cr_col = cols[:, 15:16]
    so_col = cols[:, 16:17]
    rawc = cols[:, 17:18]
    raw2c = cols[:, 18:19]
    sg65_col = cols[:, 19:20]
    sg2_col = cols[:, 20:21]
    sup_col = cols[:, 21:22]
    tcol = cols[:, 22:23]
    pcol = cols[:, 23:24]
    rec8 = cols[:, 24:32]
    rec8b = cols[:, 32:40]
    rec8c = cols[:, 40:48]

    rows = SCv[0:1, :]
    sum_r = rows[0:1, 0:512]
    ssq_r = rows[0:1, 512:1024]
    rowa = rows[0:1, 1024:1536]
    rowb = rows[0:1, 1536:2048]
    misc = sb("misc", [1, 512], F32)
    srow = misc[0:1, 0:16]
    arrow = misc[0:1, 16:32]
    arb = misc[0:1, 32:48]
    arbrow = misc[0:1, 64:320]
    eps_sb = misc[0:1, 320:321]
    sclr = misc[0:1, 336:352]
    trrow = sb("trrow", [8, 256], F32)
    trsrc = sb("trsrc", [128, 8], F32)
    trsrc2 = sb("trsrc2", [128, 8], F32)
    e_loc = sb("e_loc", [64, 64], F32)
    e_sum = sb("e_sum", [64, 64], F32)
    e_bf = sb("e_bf", [64, 64], BF16)

    ps = [st.enter_context(nc.psum_tensor(f"psb{j}", [128, 512], F32)).ap()
          for j in range(8)]

    for name in ("dS", "dA", "dG", "CC", "tE", "vE", "sE", "gE"):
        C.sems[name] = st.enter_context(nc.semaphore(name))

    RG = [list(range(NCORES))]

    def recip_col(eng, dst_col, src_col):
        eng.tensor_copy(dq_bmax, src_col.broadcast_to([128, 96]))
        eng.reciprocal(dq_s, dq_bmax)
        return eng.tensor_copy(dst_col, dq_s[:, 0:1])


    def emit_sync(eng):
        for k in range(EC):
            C.inc(eng.dma_start(out=x_ch[k], in_=xT[128 * k:128 * (k + 1), :]), "dS", 16)
            C.ck(f"x{k}", "dS")
        for dst, src in ((wq_f, wqt), (wk_f, wkt), (wv_f, wvt)):
            C.inc(eng.dma_start(
                out=dst.rearrange("p (k d) -> p k d", d=D),
                in_=src.rearrange("(k p) d -> p k d", p=128)), "dS", 16)
        C.inc(eng.dma_start(out=wo_f[:, :], in_=wot), "dS", 16)
        C.inc(eng.dma_start(out=onesf_sb[:, :], in_=onesf), "dS", 16)
        C.inc(eng.dma_start(out=onesb_sb[:, :], in_=onesb), "dS", 16)
        C.inc(eng.dma_start(out=identb_sb[:, :], in_=identb), "dS", 16)
        C.inc(eng.dma_start(out=identf_sb[:, :], in_=identf), "dS", 16)
        C.ck("consts", "dS")
        wg_r = ag_out["g"].ap().rearrange("(k p) c -> p k c", p=128)
        for i in range(IC):
            C.wait(eng, "AGwg")
            if i >= 2:
                C.wait(eng, f"Amm{i - 2}")
            C.inc(eng.dma_start(
                out=wstr_sl[i % 2].rearrange("p (k c) -> p k c", c=128),
                in_=wg_r[:, :, 128 * i:128 * (i + 1)]), "dS", 16)
            C.ck(f"wA{i}", "dS")
        wu_r = ag_out["u"].ap().rearrange("(k p) c -> p k c", p=128)
        for i in range(IC):
            C.wait(eng, "AGwu")
            C.wait(eng, f"Bmm{i - 2}" if i >= 2 else "Amm23")
            C.inc(eng.dma_start(
                out=wstr_sl[i % 2].rearrange("p (k c) -> p k c", c=128),
                in_=wu_r[:, :, 128 * i:128 * (i + 1)]), "dS", 16)
            C.ck(f"wB{i}", "dS")
        wd_r = ag_out["d"].ap().rearrange("(i p) c -> p i c", p=128)
        for e in range(EC):
            C.wait(eng, "AGwd")
            if e >= 2:
                C.wait(eng, f"Dmm{e - 2}")
            C.inc(eng.dma_start(
                out=wstrD_sl[e % 2].rearrange("p (i c) -> p i c", c=128),
                in_=wd_r[:, :, 128 * e:128 * (e + 1)]), "dS", 16)
            C.ck(f"wD{e}", "dS")
            if e >= 2:
                C.wait(eng, f"oev{e - 2}")
                C.inc(eng.dma_start(out=outT[128 * (e - 2):128 * (e - 1), :],
                                    in_=outstage[e % 2]), "dS", 16)
                C.ck(f"out{e - 2}", "dS")
        for e in (6, 7):
            C.wait(eng, f"oev{e}")
            C.inc(eng.dma_start(out=outT[128 * e:128 * (e + 1), :],
                                in_=outstage[e % 2]), "dS", 16)
            C.ck(f"out{e}", "dS")
        C.wait(eng, "out7")

    def emit_gpsimd(eng):
        for k in range(EC):
            C.wait(eng, f"x{k}")
            C.inc(eng.dma_start(out=xn_ch[k], in_=x_ch[k]), "dG", 16)
            C.ck(f"xb{k}", "dG")
        C.wait(eng, "consts")
        C.inc(eng.dma_start(out=wq_b[:, :], in_=wq_f[:, :]), "dG", 16)
        C.inc(eng.dma_start(out=wk_b[:, :], in_=wk_f[:, :]), "dG", 16)
        C.inc(eng.dma_start(out=wv_b[:, :], in_=wv_f[:, :]), "dG", 16)
        C.inc(eng.dma_start(out=wo_b[:, :], in_=wo_f[:, :]), "dG", 16)
        C.ck("wcast", "dG")
        prev = None
        for wname in ("g", "u", "d"):
            if prev is not None:
                C.wait(eng, f"dqstage{prev}")
            if wname == "d":
                nat_in = wd_nat.rearrange("(t p) c -> p t c", p=128)
                nat_out = dq_nat.rearrange("p (t c) -> p t c", c=384)
            else:
                src = wg_nat if wname == "g" else wu_nat
                nat_in = src.rearrange("(t p) c -> p t c", p=128)
                nat_out = dq_nat.rearrange("p (t c) -> p t c", c=128)
            C.inc(eng.dma_start(out=nat_out, in_=nat_in), "dG", 16)
            C.ck(f"dqnat{wname}", "dG")
            C.wait(eng, f"dqT{wname}")
            if wname == "d":
                dTr = dq_T.rearrange("p (t j c) -> p j t c", j=3, c=128)
                for j in range(3):
                    C.inc(eng.dma_start(
                        out=ag_in["d"].ap()[128 * j:128 * (j + 1), :]
                            .rearrange("p (t c) -> p t c", c=128),
                        in_=dTr[:, j]), "dG", 16)
            else:
                C.inc(eng.dma_start(out=ag_in[wname][:, :], in_=dq_T), "dG", 16)
            C.ck(f"dqstage{wname}", "dG")
            if wname == "g":
                C.wait_val(eng, "dG", C.cnt["dG"])
                C.inc(eng.collective_compute(
                    "AllGather", OP.bypass, replica_groups=RG,
                    ins=[ag_in["g"].ap().opt()], outs=[ag_out["g"].ap().opt()]), "CC", 1)
                C.ck("AGwg", "CC")
            prev = wname

        C.wait(eng, "sxtrr")
        C.inc(eng.dma_start(out=ar_in[0][0:1, 0:128], in_=trrow[0:1, 0:128]), "dG", 16)
        C.wait_val(eng, "dG", C.cnt["dG"])
        C.inc(eng.collective_compute(
            "AllReduce", OP.max, replica_groups=RG,
            ins=[ar_in[0].ap().opt()], outs=[ar_out[0].ap().opt()]), "CC", 1)
        C.wait_val(eng, "CC", C.cnt["CC"])
        C.inc(eng.dma_start(out=arbrow[0:1, 0:128], in_=ar_out[0][0:1, 0:128]), "dG", 16)
        C.ck("sxback", "dG")
        if debug:
            C.wait(eng, "sxcols")
            C.inc(eng.dma_start(out=dbg["dbg_arout"], in_=ar_out[0][:, :]), "dG", 16)
            C.inc(eng.dma_start(out=dbg["dbg_bc"], in_=astage_sl[1][:, 0:128]), "dG", 16)
            C.inc(eng.dma_start(out=dbg["dbg_arbrow"], in_=arbrow[0:1, 0:128]), "dG", 16)
            C.inc(eng.dma_start(out=dbg["dbg_sx"], in_=misc[0:1, 0:64]), "dG", 16)
            C.inc(eng.dma_start(out=dbg["dbg_xn0"], in_=xn_ch[0]), "dG", 16)
            C.inc(eng.dma_start(out=dbg["dbg_trrow"], in_=trrow[:, 0:128]), "dG", 16)
            C.inc(eng.dma_start(out=dbg["dbg_colsx"], in_=cols[:, 0:32]), "dG", 16)
            C.ck("dbgsx", "dG")

        C.wait(eng, "eloc")
        C.inc(eng.dma_start(out=ag_e_in[:, :], in_=e_loc), "dG", 16)
        C.wait_val(eng, "dG", C.cnt["dG"])
        C.inc(eng.collective_compute(
            "AllReduce", OP.add, replica_groups=RG,
            ins=[ag_e_in.ap().opt()], outs=[ag_e_out.ap().opt()]), "CC", 1)
        C.wait_val(eng, "CC", C.cnt["CC"])
        C.inc(eng.dma_start(out=e_sum[:, :], in_=ag_e_out[:, :]), "dG", 16)
        C.ck("eback", "dG")
        C.wait_val(eng, "dG", C.cnt["dG"])
        C.inc(eng.collective_compute(
            "AllGather", OP.bypass, replica_groups=RG,
            ins=[ag_in["u"].ap().opt()], outs=[ag_out["u"].ap().opt()]), "CC", 1)
        C.ck("AGwu", "CC")
        C.inc(eng.collective_compute(
            "AllGather", OP.bypass, replica_groups=RG,
            ins=[ag_in["d"].ap().opt()], outs=[ag_out["d"].ap().opt()]), "CC", 1)
        C.ck("AGwd", "CC")

        C.wait(eng, "sgtrr")
        C.inc(eng.dma_start(out=ar_in[1][0:1, 0:128], in_=trrow[0:1, 0:128]), "dG", 16)
        C.wait_val(eng, "dG", C.cnt["dG"])
        if debug:
            C.inc(eng.dma_start(out=dbg["dbg_arin1"], in_=ar_in[1][0:1, 0:128]), "dG", 16)
            C.wait_val(eng, "dG", C.cnt["dG"])
        C.inc(eng.collective_compute(
            "AllReduce", OP.max, replica_groups=RG,
            ins=[ar_in[1].ap().opt()], outs=[ar_out[1].ap().opt()]), "CC", 1)
        C.wait_val(eng, "CC", C.cnt["CC"])
        C.inc(eng.dma_start(out=arbrow[0:1, 0:128], in_=ar_out[1][0:1, 0:128]), "dG", 16)
        if debug:
            C.inc(eng.dma_start(out=dbg["dbg_arout1"], in_=ar_out[1][0:1, 0:128]), "dG", 16)
        C.ck("sgback", "dG")

        for i in range(1, IC, 2):
            C.wait(eng, f"actB{i}")
            C.inc(eng.dma_start(out=up_dram[128 * i:128 * (i + 1), :],
                                in_=upst_sl[1]), "dG", 16)
            C.ck(f"upsp{i}", "dG")

        C.wait(eng, "sutrr")
        C.inc(eng.dma_start(out=ar_in[2][0:1, 0:256], in_=trrow[0:1, 0:256]), "dG", 16)
        C.wait_val(eng, "dG", C.cnt["dG"])
        C.inc(eng.collective_compute(
            "AllReduce", OP.max, replica_groups=RG,
            ins=[ar_in[2].ap().opt()], outs=[ar_out[2].ap().opt()]), "CC", 1)
        C.wait_val(eng, "CC", C.cnt["CC"])
        C.inc(eng.dma_start(out=arbrow[0:1, 0:256], in_=ar_out[2][0:1, 0:256]), "dG", 16)
        C.ck("suback", "dG")

        C.wait(eng, "sucols")
        C.wait(eng, "upsp23")
        C.inc(eng.dma_start(out=upst_sl[1], in_=up_dram[128:256, :]), "dG", 16)
        C.ck("upldG1", "dG")
        for j in range(3, IC, 2):
            C.wait(eng, f"C2t6_{j - 2}")
            C.inc(eng.dma_start(out=upst_sl[1],
                                in_=up_dram[128 * j:128 * (j + 1), :]), "dG", 16)
            C.ck(f"upldG{j}", "dG")

        C.wait(eng, "sotrr")
        C.inc(eng.dma_start(out=ar_in[3][0:1, 0:128], in_=trrow[0:1, 0:128]), "dG", 16)
        C.wait_val(eng, "dG", C.cnt["dG"])
        C.inc(eng.collective_compute(
            "AllReduce", OP.max, replica_groups=RG,
            ins=[ar_in[3].ap().opt()], outs=[ar_out[3].ap().opt()]), "CC", 1)
        C.wait_val(eng, "CC", C.cnt["CC"])
        C.inc(eng.dma_start(out=arbrow[0:1, 0:128], in_=ar_out[3][0:1, 0:128]), "dG", 16)
        C.ck("soback", "dG")

        for e in range(EC):
            C.wait(eng, f"Dev1_{e}")
            C.wait(eng, f"apref{e}")
            for nb in range(NB):
                ti = eng.tensor_tensor(
                    outstage[e % 2][:, 512 * nb:512 * (nb + 1)],
                    outstage[e % 2][:, 512 * nb:512 * (nb + 1)],
                    astage_sl[e % 2][:, 512 * nb:512 * (nb + 1)], OP.add)
            C.inc(ti, "gE", 1)
            C.ck(f"oev{e}", "gE")
        C.wait(eng, "out7")
        if debug:
            C.inc(eng.dma_start(out=dbg["dbg_cols"], in_=cols[:, 0:32]), "dG", 16)
            C.inc(eng.dma_start(out=dbg["dbg_misc"], in_=misc), "dG", 16)
            C.inc(eng.dma_start(out=dbg["dbg_esum"], in_=e_sum[:, :]), "dG", 16)
            C.inc(eng.dma_start(out=dbg["dbg_o0"], in_=gt[0]), "dG", 16)
            C.inc(eng.dma_start(out=dbg["dbg_up0"], in_=up_dram[0:128, :]), "dG", 16)
            C.inc(eng.dma_start(out=dbg["dbg_attn0"], in_=attn_dram[0:128, :]), "dG", 16)
            C.inc(eng.dma_start(out=dbg["dbg_kb"], in_=Kb), "dG", 16)
            C.inc(eng.dma_start(out=dbg["dbg_vb"], in_=Vb), "dG", 16)
            C.inc(eng.dma_start(out=dbg["dbg_qt"], in_=QTb), "dG", 16)
            C.inc(eng.dma_start(out=dbg["dbg_ctx"], in_=ctxT), "dG", 16)
            C.inc(eng.dma_start(out=dbg["dbg_xq0"], in_=xn_ch[0]), "dG", 16)
            C.inc(eng.dma_start(out=dbg["dbg_dqT"], in_=ag_out["d"].ap()[0:128, :]), "dG", 16)
            C.wait_val(eng, "dG", C.cnt["dG"])

    def emit_tensor(eng):
        C.wait(eng, "consts")
        for wi, wname in enumerate(("g", "u", "d")):
            C.wait(eng, f"dq{wname}bf")
            for b in range(6):
                gi = wi * 6 + b
                if gi >= 2:
                    C.wait(eng, f"dqev{gi - 2}")
                for j in range(4):
                    t = b * 4 + j
                    ti = eng.transpose(ps[gi % 2][:, 128 * j:128 * (j + 1)],
                                       dq_nat[:, 128 * t:128 * (t + 1)],
                                       identf_sb[:, :])
                C.inc(ti, "tE", 1)
                C.ck(f"dqtr{gi}", "tE")
        C.wait(eng, "dqev17")
        for k in range(EC):
            C.wait(eng, f"xb{k}")
            C.wait(eng, f"x2_{k}")
            for nb in range(NB):
                eng.matmul(ps[nb][0:1, :], onesb_sb[:, 0:1],
                           xn_ch[k][:, 512 * nb:512 * (nb + 1)],
                           start=(k == 0), stop=(k == EC - 1))
            for nb in range(NB):
                ti = eng.matmul(ps[4 + nb][0:1, :], onesb_sb[:, 0:1],
                                x2b_sl[k % 2][:, 512 * nb:512 * (nb + 1)],
                                start=(k == 0), stop=(k == EC - 1))
            C.inc(ti, "tE", 1)
            C.ck(f"sums{k}", "tE")
        for nb in range(NB):
            C.wait(eng, f"rqrow{nb}")
            eng.matmul(ps[nb][:, :], onesf_sb[0:1, :], rowa, start=True, stop=True)
            ti = eng.matmul(ps[4 + nb][:, :], onesf_sb[0:1, :], rowb,
                            start=True, stop=True)
            C.inc(ti, "tE", 1)
            C.ck(f"rqmm{nb}", "tE")
        C.wait(eng, "xn7")
        C.wait(eng, "rqev")
        C.wait(eng, "wcast")
        for nb in range(NB):
            for k in range(EC):
                ti = eng.matmul(ps[nb][0:64, :], wq_b[:, 64 * k:64 * (k + 1)],
                                xn_ch[k][:, 512 * nb:512 * (nb + 1)],
                                start=(k == 0), stop=(k == EC - 1))
            C.inc(ti, "tE", 1)
            C.ck(f"qmm{nb}", "tE")
        for c in range(TC):
            if c >= 2:
                C.wait(eng, f"kvev{c - 2}")
            bk, bv = 4 + (c % 2) * 2, 5 + (c % 2) * 2
            for k in range(EC):
                eng.matmul(ps[bk][:, 0:64], xn_ch[k][:, 128 * c:128 * (c + 1)],
                           wk_b[:, 64 * k:64 * (k + 1)],
                           start=(k == 0), stop=(k == EC - 1))
                ti = eng.matmul(ps[bv][:, 0:64], xn_ch[k][:, 128 * c:128 * (c + 1)],
                                wv_b[:, 64 * k:64 * (k + 1)],
                                start=(k == 0), stop=(k == EC - 1))
            C.inc(ti, "tE", 1)
            C.ck(f"kvmm{c}", "tE")
        C.wait(eng, "qev")
        C.wait(eng, "sxcol")
        ti = eng.transpose(ps[0][0:8, 0:128], trsrc[:, :], identf_sb[:, :])
        C.inc(ti, "tE", 1)
        C.ck("sxtr", "tE")
        C.wait(eng, "sxback")
        ti = eng.matmul(ps[0][:, 0:128], onesf_sb[0:1, :], arbrow[0:1, 0:128],
                        start=True, stop=True)
        C.inc(ti, "tE", 1)
        C.ck("sxbc", "tE")
        C.wait(eng, "kvev15")
        for c in range(TC):
            ti = eng.matmul(ps[1][0:64, 0:64], Kb[:, 64 * c:64 * (c + 1)],
                            Vb[:, 64 * c:64 * (c + 1)],
                            start=(c == 0), stop=(c == TC - 1))
        C.inc(ti, "tE", 1)
        C.ck("emm", "tE")
        for i in range(IC):
            grp = (i % 2) * 4
            C.wait(eng, f"wA{i}")
            if i >= 2:
                C.wait(eng, f"actA{i - 2}")
            for k in range(EC):
                if i < 3:
                    C.wait(eng, f"xq{k}")
                for nb in range(NB):
                    ti = eng.matmul(ps[grp + nb][:, :],
                                    wstr_sl[i % 2][:, 128 * k:128 * (k + 1)],
                                    xq_ch[k][:, 512 * nb:512 * (nb + 1)],
                                    start=(k == 0), stop=(k == EC - 1))
            C.inc(ti, "tE", 1)
            C.ck(f"Amm{i}", "tE")
        C.wait(eng, "actA22")
        C.wait(eng, "gmaxcol")
        ti = eng.transpose(ps[0][0:8, 0:128], trsrc[:, :], identf_sb[:, :])
        C.inc(ti, "tE", 1)
        C.ck("sgtr", "tE")
        for i in range(IC):
            grp = (i % 2) * 4
            if i == 3:
                C.wait(eng, "sgback")
                C.wait(eng, "actB2")
                ti = eng.matmul(ps[0][:, 0:128], onesf_sb[0:1, :],
                                arbrow[0:1, 0:128], start=True, stop=True)
                C.inc(ti, "tE", 1)
                C.ck("sgbc", "tE")
            C.wait(eng, f"wB{i}")
            if i == 4:
                C.wait(eng, "sgcols")
            if i >= 2:
                C.wait(eng, f"actB{i - 2}")
            elif i == 0:
                C.wait(eng, "sgtrr")
            else:
                C.wait(eng, "actA23")
            for k in range(EC):
                for nb in range(NB):
                    ti = eng.matmul(ps[grp + nb][:, :],
                                    wstr_sl[i % 2][:, 128 * k:128 * (k + 1)],
                                    xq_ch[k][:, 512 * nb:512 * (nb + 1)],
                                    start=(k == 0), stop=(k == EC - 1))
            C.inc(ti, "tE", 1)
            C.ck(f"Bmm{i}", "tE")
        C.wait(eng, "umaxcol")
        C.wait(eng, "mmaxcol")
        C.wait(eng, "actB22")
        eng.transpose(ps[0][0:8, 0:128], trsrc[:, :], identf_sb[:, :])
        ti = eng.transpose(ps[1][0:8, 0:128], trsrc2[:, :], identf_sb[:, :])
        C.inc(ti, "tE", 1)
        C.ck("sutr", "tE")
        C.wait(eng, "ebf")
        C.wait(eng, "sxcols")
        C.wait(eng, "eev")
        C.wait(eng, "actB23")
        for nb in range(NB):
            ti = eng.matmul(ps[2 + nb][0:64, :], e_bf[:, :],
                            QTb[:, 512 * nb:512 * (nb + 1)], start=True, stop=True)
        C.inc(ti, "tE", 1)
        C.ck("cmmall", "tE")
        for e in range(EC):
            grp = 4 if e % 2 else 0
            C.wait(eng, f"aev{e - 2}" if e >= 2 else "cev")
            if e == 0:
                C.wait(eng, "sutrr")
            for nb in range(NB):
                ti = eng.matmul(ps[grp + nb][:, :], wo_b[:, 128 * e:128 * (e + 1)],
                                ctxT[:, 512 * nb:512 * (nb + 1)], start=True, stop=True)
            C.inc(ti, "tE", 1)
            C.ck(f"amm{e}", "tE")
        C.wait(eng, "suback")
        C.wait(eng, "aev6")
        C.wait(eng, "aev7")
        eng.matmul(ps[2][:, 0:128], onesf_sb[0:1, :], arbrow[0:1, 0:128],
                   start=True, stop=True)
        ti = eng.matmul(ps[3][:, 0:128], onesf_sb[0:1, :], arbrow[0:1, 128:256],
                        start=True, stop=True)
        C.inc(ti, "tE", 1)
        C.ck("subc", "tE")
        C.wait(eng, "sucols")
        C.wait(eng, "rrmaxcol")
        ti = eng.transpose(ps[0][0:8, 0:128], trsrc[:, :], identf_sb[:, :])
        C.inc(ti, "tE", 1)
        C.ck("sotr", "tE")
        C.wait(eng, "soback")
        ti = eng.matmul(ps[2][:, 0:128], onesf_sb[0:1, :], arbrow[0:1, 0:128],
                        start=True, stop=True)
        C.inc(ti, "tE", 1)
        C.ck("sobc", "tE")
        for e in range(EC):
            grp = (e % 2) * 4
            C.wait(eng, f"wD{e}")
            if e >= 2:
                C.wait(eng, f"Dev1_{e - 2}")
            else:
                C.wait(eng, f"actB{22 + e}")
            for i in range(IC):
                if e == 0:
                    C.wait(eng, f"o{i}")
                for nb in range(NB):
                    ti = eng.matmul(ps[grp + nb][:, :],
                                    wstrD_sl[e % 2][:, 128 * i:128 * (i + 1)],
                                    gt[i][:, 512 * nb:512 * (nb + 1)],
                                    start=(i == 0), stop=(i == IC - 1))
            C.inc(ti, "tE", 1)
            C.ck(f"Dmm{e}", "tE")

    def emit_vector(eng):
        eng.memset(eps_sb, LNEPS)
        eng.memset(Mcol, MAGIC)
        ti = eng.memset(negMcol, -MAGIC)
        C.inc(ti, "vE", 1)
        C.ck("memset", "vE")
        prev = None
        for wname in ("g", "u", "d"):
            C.wait(eng, f"dqnat{wname}")
            if prev is not None:
                C.wait(eng, f"dqstage{prev}")
            nat3 = dq_nat.rearrange("p (b c) -> p b c", c=32)
            eng.tensor_reduce(dq_bmax, nat3, AX.X, OP.max, apply_absolute_value=True)
            eng.tensor_scalar(dq_s, dq_bmax, 1.0 / QW, SEPS, OP.mult, OP.max)
            eng.reciprocal(dq_rs, dq_s)
            eng.tensor_tensor(nat3, nat3,
                              dq_rs[:, :, None].broadcast_to([128, 96, 32]), OP.mult)
            eng.tensor_scalar(dq_nat, dq_nat, MAGIC, MAGIC, OP.add, OP.subtract)
            ti = eng.tensor_tensor(nat3, nat3,
                                   dq_s[:, :, None].broadcast_to([128, 96, 32]), OP.mult)
            C.inc(ti, "vE", 1)
            C.ck(f"dq{wname}bf", "vE")
            prev = wname

        C.wait(eng, f"sums{EC - 1}")
        for nb in range(NB):
            if nb >= 1:
                C.wait(eng, f"rqmm{nb - 1}")
            eng.tensor_copy(sum_r, ps[nb][0:1, :])
            eng.tensor_copy(ssq_r, ps[4 + nb][0:1, :])
            eng.tensor_scalar(rowa, sum_r, 1.0 / E, None, OP.mult, OP.bypass)
            eng.tensor_tensor(rowb, rowa, rowa, OP.mult)
            eng.tensor_scalar(rowa, ssq_r, 1.0 / E, None, OP.mult, OP.bypass)
            ti = eng.tensor_tensor(rowb, rowa, rowb, OP.subtract)
            C.inc(ti, "vE", 1)
            C.ck(f"var{nb}", "vE")
            C.wait(eng, f"sqrt{nb}")
            eng.reciprocal(rowa, rowb)
            ti = eng.scalar_tensor_tensor(rowb, sum_r, 1.0 / E, rowa,
                                          OP.mult, OP.mult)
            C.inc(ti, "vE", 1)
            C.ck(f"rqrow{nb}", "vE")
        for nb in range(NB):
            C.wait(eng, f"rqmm{nb}")
            eng.tensor_copy(Rb[:, 512 * nb:512 * (nb + 1)], ps[nb][:, :])
            ti = eng.tensor_copy(Qb[:, 512 * nb:512 * (nb + 1)], ps[4 + nb][:, :])
        C.inc(ti, "vE", 1)
        C.ck("rqev", "vE")
        for k in range(EC):
            eng.tensor_tensor(xtmp, x_ch[k], Rb, OP.mult)
            ti = eng.tensor_tensor(xn_ch[k], xtmp, Qb, OP.subtract)
            C.inc(ti, "vE", 1)
            C.ck(f"xn{k}", "vE")
            eng.tensor_reduce(mctmp, xn_ch[k], AX.X, OP.max, apply_absolute_value=True)
            ti = (eng.tensor_tensor(maxcol, maxcol, mctmp, OP.max) if k else
                  eng.tensor_copy(maxcol, mctmp))
            C.inc(ti, "vE", 1)
        ti = eng.tensor_copy(trsrc[:, 0:1], maxcol)
        C.inc(ti, "vE", 1)
        C.ck("sxcol", "vE")
        C.wait(eng, "sxtr")
        ti = eng.tensor_copy(trrow[:, 0:128], ps[0][0:8, 0:128])
        C.inc(ti, "vE", 1)
        C.ck("sxtrr", "vE")
        C.wait(eng, "sxbc")
        eng.tensor_reduce(rawc, ps[0][:, 0:128], AX.X, OP.max)
        if debug:
            eng.tensor_copy(astage_sl[1][:, 0:128], ps[0][:, 0:128])
        eng.tensor_scalar(sx_col, rawc, 1.0 / QA, SEPS, OP.mult, OP.max)
        ti = recip_col(eng, rsx_col, sx_col)
        C.inc(ti, "vE", 1)
        C.ck("sxcols", "vE")
        C.wait(eng, "kvmm15")
        if debug:
            C.wait(eng, "dbgsx")
        for k in range(5):
            eng.tensor_scalar(SCt, xn_ch[k], rsx_col, MAGIC, OP.mult, OP.add)
            ti = eng.tensor_scalar(xq_ch[k], SCt, MAGIC, sx_col, OP.subtract, OP.mult)
            C.inc(ti, "vE", 1)
            C.ck(f"xq{k}", "vE")
        C.wait(eng, "eback")
        ti = eng.tensor_scalar(e_bf[:, :], e_sum[:, :], C_ATTN, None, OP.mult, OP.bypass)
        C.inc(ti, "vE", 1)
        C.ck("ebf", "vE")
        for i in range(IC):
            C.wait(eng, f"actA{i}")
            eng.tensor_reduce(mctmp, gt[i], AX.X, OP.max, apply_absolute_value=True)
            ti = (eng.tensor_tensor(gmaxcol, gmaxcol, mctmp, OP.max) if i else
                  eng.tensor_copy(gmaxcol, mctmp))
            C.inc(ti, "vE", 1)
        ti = eng.tensor_copy(trsrc[:, 0:1], gmaxcol)
        C.inc(ti, "vE", 1)
        C.ck("gmaxcol", "vE")
        C.wait(eng, "sgtr")
        ti = eng.tensor_copy(trrow[:, 0:128], ps[0][0:8, 0:128])
        C.inc(ti, "vE", 1)
        C.ck("sgtrr", "vE")
        def vec_c1(j):
            C.wait(eng, f"c1r{j}")
            tb = SCt if j % 2 == 0 else SCv
            ub = SCu if j % 2 == 0 else astageF
            eng.tensor_scalar(ub, ub, MAGIC + SIGMAX, MAGIC, OP.min, OP.subtract)
            ti = eng.scalar_tensor_tensor(gt[j], tb, -MAGIC, ub, OP.add, OP.mult)
            C.inc(ti, "vE", 1)
            C.ck(f"C1m{j}", "vE")
            eng.tensor_reduce(mctmp, gt[j], AX.X, OP.max, apply_absolute_value=True)
            ti = (eng.tensor_tensor(mmaxcol, mmaxcol, mctmp, OP.max) if j else
                  eng.tensor_copy(mmaxcol, mctmp))
            C.inc(ti, "vE", 1)

        for i in range(IC):
            if i == 2:
                C.wait(eng, "sgbc")
                eng.tensor_reduce(rawc, ps[0][:, 0:128], AX.X, OP.max)
                eng.tensor_scalar(sg_col, rawc, 1.0 / QA, SEPS, OP.mult, OP.max)
                recip_col(eng, rsg_col, sg_col)
                eng.tensor_scalar(sg65_col, sg_col, 1.0 / SIG, None,
                                  OP.mult, OP.bypass)
                ti = eng.tensor_scalar(msg_col, sg_col, -MAGIC, None,
                                       OP.mult, OP.bypass)
                C.inc(ti, "vE", 1)
                C.ck("sgcols", "vE")
            C.wait(eng, f"actB{i}")
            eng.tensor_reduce(mctmp, upst_sl[i % 2], AX.X, OP.max,
                              apply_absolute_value=True)
            ti = (eng.tensor_tensor(umaxcol, umaxcol, mctmp, OP.max) if i else
                  eng.tensor_copy(umaxcol, mctmp))
            C.inc(ti, "vE", 1)
            C.ck(f"umax{i}", "vE")
            if i >= 2:
                vec_c1(i - 2)
        for j in range(IC - 2, IC):
            vec_c1(j)
        eng.tensor_copy(trsrc[:, 0:1], umaxcol)
        ti = eng.tensor_copy(trsrc2[:, 0:1], mmaxcol)
        C.inc(ti, "vE", 1)
        C.ck("umaxcol", "vE")
        C.ck("mmaxcol", "vE")
        C.wait(eng, "sutr")
        eng.tensor_copy(trrow[:, 0:128], ps[0][0:8, 0:128])
        ti = eng.tensor_copy(trrow[:, 128:256], ps[1][0:8, 0:128])
        C.inc(ti, "vE", 1)
        C.ck("sutrr", "vE")
        for e in range(1, EC, 2):
            C.wait(eng, f"amm{e}")
            if e >= 3:
                C.wait(eng, f"asp{e - 2}")
            for nb in range(NB):
                ti = eng.tensor_copy(astage_sl[1][:, 512 * nb:512 * (nb + 1)],
                                     ps[4 + nb][:, :])
            C.inc(ti, "vE", 1)
            C.ck(f"aev{e}", "vE")
        C.wait(eng, "subc")
        eng.tensor_reduce(rawc, ps[2][:, 0:128], AX.X, OP.max)
        eng.tensor_reduce(raw2c, ps[3][:, 0:128], AX.X, OP.max)
        eng.tensor_scalar(sup_col, rawc, 1.0 / QA, SEPS, OP.mult, OP.max)
        recip_col(eng, rsu_col, sup_col)
        eng.tensor_tensor(tcol, raw2c, sg65_col, OP.mult)
        eng.tensor_scalar(sg2_col, tcol, 1.0 / QA, SEPS, OP.mult, OP.max)
        recip_col(eng, tcol, sg2_col)
        ti = eng.tensor_tensor(cg_col, sg65_col, tcol, OP.mult)
        C.inc(ti, "vE", 1)
        C.ck("sucols", "vE")
        for i in range(IC):
            buf = SCu if i % 2 == 0 else SCv
            t5b = SCt if i % 2 == 0 else xsc_sl[0]
            if i % 2 == 0:
                C.wait(eng, f"C2t6_{i}")
                eng.tensor_scalar(buf, buf, MAGIC, None, OP.subtract, OP.bypass)
            else:
                C.wait(eng, f"C2g{i}")
            C.wait(eng, f"C2t5_{i}")
            ti = eng.scalar_tensor_tensor(gt[i], t5b, -MAGIC, buf, OP.add, OP.mult)
            C.inc(ti, "vE", 1)
            C.ck(f"C2rr{i}", "vE")
            eng.tensor_reduce(mctmp, gt[i], AX.X, OP.max, apply_absolute_value=True)
            ti = (eng.tensor_tensor(rrmaxcol, rrmaxcol, mctmp, OP.max) if i else
                  eng.tensor_copy(rrmaxcol, mctmp))
            C.inc(ti, "vE", 1)
        ti = eng.tensor_copy(trsrc[:, 0:1], rrmaxcol)
        C.inc(ti, "vE", 1)
        C.ck("rrmaxcol", "vE")
        C.wait(eng, "sotr")
        ti = eng.tensor_copy(trrow[:, 0:128], ps[0][0:8, 0:128])
        C.inc(ti, "vE", 1)
        C.ck("sotrr", "vE")
        C.wait(eng, "sobc")
        eng.tensor_reduce(rawc, ps[2][:, 0:128], AX.X, OP.max)
        eng.tensor_tensor(pcol, sg2_col, sup_col, OP.mult)
        eng.tensor_tensor(tcol, rawc, pcol, OP.mult)
        eng.tensor_scalar(so_col, tcol, 1.0 / QA, SEPS, OP.mult, OP.max)
        recip_col(eng, tcol, so_col)
        ti = eng.tensor_tensor(cr_col, pcol, tcol, OP.mult)
        C.inc(ti, "vE", 1)
        C.ck("socols", "vE")
        C.ck("somax", "vE")
        for i in range(IC):
            if i % 2 == 0:
                C.wait(eng, f"ot7_{i}")
                t7b = xsc_sl[(i % 4) // 2]
                ti = eng.tensor_scalar(gt[i], t7b, MAGIC, so_col,
                                       OP.subtract, OP.mult)
            else:
                eng.tensor_scalar(SCt, gt[i], cr_col, MAGIC, OP.mult, OP.add)
                ti = eng.tensor_scalar(gt[i], SCt, MAGIC, so_col,
                                       OP.subtract, OP.mult)
            C.inc(ti, "vE", 1)
            C.ck(f"o{i}", "vE")
        for e in range(EC):
            C.wait(eng, f"Dmm{e}")
            C.wait(eng, f"xpref{e}")
            if e >= 2:
                C.wait(eng, f"out{e - 2}")
            grp = (e % 2) * 4
            for nb in range(NB):
                ti = eng.tensor_tensor(outstage[e % 2][:, 512 * nb:512 * (nb + 1)],
                                       ps[grp + nb][:, :],
                                       xpref_sl[e % 2][:, 512 * nb:512 * (nb + 1)],
                                       OP.add)
            C.inc(ti, "vE", 1)
            C.ck(f"Dev1_{e}", "vE")

    def emit_scalar(eng):
        for wi, wname in enumerate(("g", "u", "d")):
            for b in range(6):
                gi = wi * 6 + b
                C.wait(eng, f"dqtr{gi}")
                ti = eng.copy(dq_T[:, 512 * b:512 * (b + 1)], ps[gi % 2][:, :])
                C.inc(ti, "sE", 1)
                C.ck(f"dqev{gi}", "sE")
            C.ck(f"dqT{wname}", "sE")
        for k in range(EC):
            C.wait(eng, f"x{k}")
            if k >= 2:
                C.wait(eng, f"sums{k - 2}")
            ti = eng.activation(x2b_sl[k % 2], x_ch[k], AF.Square)
            C.inc(ti, "sE", 1)
            C.ck(f"x2_{k}", "sE")
        C.wait(eng, "memset")
        for nb in range(NB):
            C.wait(eng, f"var{nb}")
            ti = eng.activation(rowb, rowb, AF.Sqrt, bias=eps_sb, scale=1.0)
            C.inc(ti, "sE", 1)
            C.ck(f"sqrt{nb}", "sE")
        for nb in range(NB):
            C.wait(eng, f"qmm{nb}")
            ti = eng.copy(QTb[:, 512 * nb:512 * (nb + 1)], ps[nb][0:64, :])
        C.inc(ti, "sE", 1)
        C.ck("qev", "sE")
        for c in range(TC):
            C.wait(eng, f"kvmm{c}")
            bk, bv = 4 + (c % 2) * 2, 5 + (c % 2) * 2
            eng.copy(Kb[:, 64 * c:64 * (c + 1)], ps[bk][:, 0:64])
            ti = eng.copy(Vb[:, 64 * c:64 * (c + 1)], ps[bv][:, 0:64])
            C.inc(ti, "sE", 1)
            C.ck(f"kvev{c}", "sE")
        C.wait(eng, "emm")
        ti = eng.copy(e_loc[:, :], ps[1][0:64, 0:64])
        C.inc(ti, "sE", 1)
        C.ck("eloc", "sE")
        C.ck("eev", "sE")
        for k in range(5, EC):
            if k == 5:
                C.wait(eng, "sxcols")
                C.wait(eng, "kvmm15")
            eng.activation(SCu, xn_ch[k], AF.Identity, bias=Mcol, scale=rsx_col)
            eng.activation(SCu, SCu, AF.Identity, bias=negMcol)
            ti = eng.mul(xq_ch[k], SCu, sx_col)
            C.inc(ti, "sE", 1)
            C.ck(f"xq{k}", "sE")
        for i in range(IC):
            C.wait(eng, f"Amm{i}")
            grp = (i % 2) * 4
            for nb in range(NB):
                ti = eng.copy(gt[i][:, 512 * nb:512 * (nb + 1)], ps[grp + nb][:, :])
            C.inc(ti, "sE", 1)
            C.ck(f"actA{i}", "sE")

        def act_c1(j):
            if j == 0:
                C.wait(eng, "sgcols")
            if j >= 2:
                C.wait(eng, f"C1m{j - 2}")
            tb = SCt if j % 2 == 0 else SCv
            ub = SCu if j % 2 == 0 else astageF
            ti = eng.activation(tb, gt[j], AF.Identity, bias=Mcol, scale=rsg_col)
            C.inc(ti, "sE", 1)
            C.ck(f"C1t{j}", "sE")
            ti = eng.activation(ub, tb, AF.Sigmoid, bias=msg_col, scale=sg_col)
            C.inc(ti, "sE", 1)
            C.ck(f"sig{j}", "sE")
            ti = eng.activation(ub, ub, AF.Identity, bias=Mcol, scale=float(SIG))
            C.inc(ti, "sE", 1)
            C.ck(f"c1r{j}", "sE")

        for i in range(IC):
            C.wait(eng, f"Bmm{i}")
            if i >= 2:
                C.wait(eng, f"upsp{i - 2}")
                C.wait(eng, f"umax{i - 2}")
            grp = (i % 2) * 4
            for nb in range(NB):
                ti = eng.copy(upst_sl[i % 2][:, 512 * nb:512 * (nb + 1)],
                              ps[grp + nb][:, :])
            C.inc(ti, "sE", 1)
            C.ck(f"actB{i}", "sE")
            if i % 2 == 0:
                C.inc(eng.dma_start(out=up_dram[128 * i:128 * (i + 1), :],
                                    in_=upst_sl[0]), "dA", 16)
                C.ck(f"upsp{i}", "dA")
            if i >= 2:
                act_c1(i - 2)
        for j in range(IC - 2, IC):
            act_c1(j)
        C.wait(eng, "cmmall")
        for nb in range(NB):
            ti = eng.copy(ctxT[:, 512 * nb:512 * (nb + 1)], ps[2 + nb][0:64, :])
        C.inc(ti, "sE", 1)
        C.ck("cev", "sE")
        C.wait(eng, "C1m23")
        for e in range(EC):
            if e % 2 == 0:
                C.wait(eng, f"amm{e}")
                if e >= 2:
                    C.wait(eng, f"asp{e - 2}")
                for nb in range(NB):
                    ti = eng.copy(astage_sl[0][:, 512 * nb:512 * (nb + 1)],
                                  ps[nb][:, :])
                C.inc(ti, "sE", 1)
                C.ck(f"aev{e}", "sE")
            else:
                C.wait(eng, f"aev{e}")
            C.inc(eng.dma_start(out=attn_dram[128 * e:128 * (e + 1), :],
                                in_=astage_sl[e % 2]), "dA", 16)
            C.ck(f"asp{e}", "dA")
        C.wait(eng, "sucols")
        C.inc(eng.dma_start(out=upst_sl[0], in_=up_dram[0:128, :]), "dA", 16)
        C.ck("upldA0", "dA")
        for i in range(IC):
            if i >= 2:
                C.wait(eng, f"C2rr{i - 2}")
            t5b = SCt if i % 2 == 0 else xsc_sl[0]
            ti = eng.activation(t5b, gt[i], AF.Identity, bias=Mcol, scale=cg_col)
            C.inc(ti, "sE", 1)
            C.ck(f"C2t5_{i}", "sE")
            C.wait(eng, f"upldA{i}" if i % 2 == 0 else f"upldG{i}")
            buf = SCu if i % 2 == 0 else SCv
            ti = eng.activation(buf, upst_sl[i % 2], AF.Identity,
                                bias=Mcol, scale=rsu_col)
            C.inc(ti, "sE", 1)
            C.ck(f"C2t6_{i}", "sE")
            if i % 2 == 1:
                ti = eng.activation(buf, buf, AF.Identity, bias=negMcol)
                C.inc(ti, "sE", 1)
                C.ck(f"C2g{i}", "sE")
            if i + 2 < IC and i % 2 == 0:
                C.inc(eng.dma_start(out=upst_sl[0],
                                    in_=up_dram[128 * (i + 2):128 * (i + 3), :]),
                      "dA", 16)
                C.ck(f"upldA{i + 2}", "dA")
        for i in range(0, IC, 2):
            if i == 0:
                C.wait(eng, "socols")
            if i >= 4:
                C.wait(eng, f"o{i - 4}")
            t7b = xsc_sl[(i % 4) // 2]
            ti = eng.activation(t7b, gt[i], AF.Identity, bias=Mcol, scale=cr_col)
            C.inc(ti, "sE", 1)
            C.ck(f"ot7_{i}", "sE")
        pf0 = C.cnt["dA"]
        for e in range(EC):
            if e == 0:
                C.wait(eng, "somax")
            if e >= 2:
                C.wait(eng, f"Dev1_{e - 2}")
                C.wait(eng, f"oev{e - 2}")
            C.inc(eng.dma_start(out=xpref_sl[e % 2],
                                in_=xT[128 * e:128 * (e + 1), :]), "dA", 16)
            C.inc(eng.dma_start(out=astage_sl[e % 2],
                                in_=attn_dram[128 * e:128 * (e + 1), :]), "dA", 16)
            C.wait_val(eng, "dA", pf0 + 32 * (e + 1))
            C.ck(f"xpref{e}", "dA")
            C.ck(f"apref{e}", "dA")

    C.reset(False)
    emit_sync(Mock())
    emit_vector(Chain(Mock(), C, "vE"))
    emit_scalar(Chain(Mock(), C, "sE"))
    emit_tensor(Mock())
    emit_gpsimd(Chain(Mock(), C, "gE"))

    C.reset(True)
    with nc.Block() as block:
        @block.sync
        def _(eng):
            emit_sync(eng)

        @block.vector
        def _(eng):
            emit_vector(Chain(eng, C, "vE"))

        @block.scalar
        def _(eng):
            emit_scalar(Chain(eng, C, "sE"))

        @block.tensor
        def _(eng):
            emit_tensor(eng)

        @block.gpsimd
        def _(eng):
            emit_gpsimd(Chain(eng, C, "gE"))

    st.close()
    return nc


_NC_CACHE = []


def kernel(L0_attn_x, L0_attn_wq, L0_attn_wk, L0_attn_wv, L0_attn_wo,
           L0_ffn_w_gate, L0_ffn_w_up, L0_ffn_w_down):
    x = np.ascontiguousarray(np.asarray(L0_attn_x, dtype=np.float32).reshape(16384, 1024))
    wq = np.asarray(L0_attn_wq, dtype=np.float32)
    wk = np.asarray(L0_attn_wk, dtype=np.float32)
    wv = np.asarray(L0_attn_wv, dtype=np.float32)
    wo = np.asarray(L0_attn_wo, dtype=np.float32)
    wg = np.asarray(L0_ffn_w_gate, dtype=np.float32)
    wu = np.asarray(L0_ffn_w_up, dtype=np.float32)
    wd = np.asarray(L0_ffn_w_down, dtype=np.float32)

    xT_full = np.ascontiguousarray(x.T)
    wqt = np.ascontiguousarray(wq.T)
    wkt = np.ascontiguousarray(wk.T)
    wvt = np.ascontiguousarray(wv.T)
    wot = np.ascontiguousarray(wo.T)
    onesf_h = np.ones((128, 128), dtype=np.float32)
    onesb_h = np.ones((128, 1), dtype=ml_dtypes.bfloat16)
    identb_h = np.eye(128, dtype=ml_dtypes.bfloat16)

    in_maps = []
    for r in range(NCORES):
        in_maps.append({
            "xT": np.ascontiguousarray(xT_full[:, T * r:T * (r + 1)]),
            "wqt": wqt, "wkt": wkt, "wvt": wvt, "wot": wot,
            "wg_nat": np.ascontiguousarray(wg[:, 128 * r:128 * (r + 1)]),
            "wu_nat": np.ascontiguousarray(wu[:, 128 * r:128 * (r + 1)]),
            "wd_nat": np.ascontiguousarray(wd[:, 384 * r:384 * (r + 1)]),
            "onesf": onesf_h, "onesb": onesb_h, "identb": identb_h,
            "identf": np.eye(128, dtype=np.float32),
        })

    if not _NC_CACHE:
        _NC_CACHE.append(build_program())
    nc = _NC_CACHE[0]
    res = run_bass_kernel_spmd(nc, in_maps, core_ids=list(range(NCORES)))
    outT_full = np.concatenate(
        [np.asarray(res.results[r]["outT"]) for r in range(NCORES)], axis=1)
    return np.ascontiguousarray(outT_full.T).reshape(1, 16384, 1024).astype(np.float32)
